# version 02 (final)
# baseline (speedup 1.0000x reference)
import math
from contextlib import ExitStack

import numpy as np
import ml_dtypes

import concourse.bass as bass
import concourse.mybir as mybir
from concourse.bass_utils import run_bass_kernel_spmd

F32 = mybir.dt.float32
BF16 = mybir.dt.bfloat16
AF = mybir.ActivationFunctionType
ALU = mybir.AluOpType
AX = mybir.AxisListType
BF = ml_dtypes.bfloat16

HID = 4096
H, KVH, D, W = 32, 8, 128, 64
THETA = 10000.0
C = 128
FFN_H = 11008
RANK = 1024
INTER = 11008
NSPEC = 2048
EPS = 1e-5
NEG = -1.0e30
RSD = 1.0 / math.sqrt(D)
NHT = FFN_H // 128


def make_tables(n_tok):
    t = {}
    t["tb_ident"] = np.eye(128, dtype=BF)
    iu = np.arange(C)
    t["tb_utri"] = (iu[:, None] <= iu[None, :]).astype(np.float32)
    a = np.arange(C)[:, None]
    wp = np.arange(C + W)[None, :]
    base = (wp >= a) & (wp <= a + W)
    m = np.where(base, 0.0, NEG).astype(np.float32)
    t["tb_swam"] = m
    m0 = m.copy()
    m0[:, :W] = NEG
    t["tb_swam0"] = m0
    half = np.arange(D // 2, dtype=np.float64)
    inv = 1.0 / THETA ** (2.0 * half / D)
    pos = np.arange(n_tok, dtype=np.float64)
    fr = pos[:, None] * inv[None, :]
    emb = np.concatenate([fr, fr], axis=-1)
    cos, sin = np.cos(emb), np.sin(emb)
    t["tb_rcq"] = (cos * RSD).astype(BF)
    t["tb_rsq"] = (sin * RSD).astype(BF)
    t["tb_rck"] = cos.astype(BF)
    t["tb_rsk"] = sin.astype(BF)
    L = HID
    kk = np.arange(NSPEC, dtype=np.float64)
    tt = np.arange(L, dtype=np.float64)
    ang = 2.0 * np.pi * np.outer(tt, kk) / L
    t["tb_dcos"] = np.cos(ang).astype(BF)
    t["tb_dsin"] = np.sin(ang).astype(BF)
    t["tb_dnyq"] = np.ascontiguousarray(np.cos(np.pi * tt).astype(BF).reshape(L // 128, 128).T)
    ck = np.full(NSPEC, 2.0 / L)
    ck[0] = 1.0 / L
    angi = 2.0 * np.pi * np.outer(kk, tt) / L
    t["tb_icos"] = (ck[:, None] * np.cos(angi)).astype(BF)
    t["tb_isin"] = (-ck[:, None] * np.sin(angi)).astype(BF)
    t["tb_alt"] = np.ascontiguousarray(
        np.broadcast_to(((-1.0) ** tt / L).astype(BF), (128, L)))
    return t


class Sem:
    def __init__(self, handle):
        self.h = handle
        self.n = 0

    def inc(self, instr, k=1):
        instr.then_inc(self.h, k)
        self.n += k
        return instr

    def wait(self, eng, val=None):
        eng.wait_ge(self.h, self.n if val is None else val)


class P:
    pass


def _mk_prog(nc, ctx):
    pg = P()
    pg.nc = nc
    pg.ctx = ctx
    E = ctx.enter_context
    for nm in ("sLD", "sLG", "sPE", "sDV", "sAC", "sST", "sSG", "sCC"):
        setattr(pg, nm, Sem(E(nc.semaphore(nm))))
    pg.par = nc.partition_id() % 2
    return pg


def sbuf(pg, name, shape, dtype):
    return pg.ctx.enter_context(pg.nc.sbuf_tensor(name, shape, dtype))


def br_load(pg, q, sem, dst, src_lo, src_hi):
    with q.If(pg.par):
        q.dma_start(dst, src_hi).then_inc(sem.h, 16)
    with q.Else():
        q.dma_start(dst, src_lo).then_inc(sem.h, 16)
    sem.n += 16


def phase_barrier(pg):
    nc = pg.nc
    for eng in (nc.sync, nc.gpsimd, nc.vector, nc.scalar, nc.tensor):
        for s in (pg.sPE, pg.sDV, pg.sAC, pg.sST, pg.sSG, pg.sLD, pg.sLG):
            eng.wait_ge(s.h, s.n)



def build_program(n_tok=4096, debug_taps=(), single_core=False, stop_after=None):
    nc = bass.Bass("TRN2", target_bir_lowering=False)
    NCH = n_tok // C
    MYT = n_tok // 2
    MCH = MYT // C
    QCOLS = (H // 2) * D
    KCOLS = (KVH // 2) * D
    TSW = min(512, MYT)
    NTS = MYT // TSW

    dt = nc.dram_tensor
    T = {}
    for nm, shp in [
        ("x_b", [n_tok, HID]), ("wq", [HID, QCOLS]), ("wk", [HID, KCOLS]),
        ("wv", [HID, KCOLS]), ("wo", [HID, HID]),
        ("w1qT", [1024, FFN_H]), ("w3qT", [1024, FFN_H]), ("w2qT", [2752, HID]),
        ("a1T", [HID, 2]), ("a3T", [HID, 2]), ("a2T", [FFN_H, 2]),
        ("b1T", [2, FFN_H]), ("b3T", [2, FFN_H]), ("b2T", [2, HID]),
        ("u1w", [HID, RANK]), ("v1w", [RANK, 2 * INTER]), ("u2w", [INTER, RANK]),
        ("v2w", [RANK, HID]), ("b1r", [1, 2 * INTER]), ("b2r", [1, HID]),
        ("x_my", [MYT, HID]),
        ("tb_utri", [C, C]), ("tb_swam", [C, C + W]), ("tb_swam0", [C, C + W]),
    ]:
        T[nm] = dt(nm, shp, F32, kind="ExternalInput")
    for nm, shp in [
        ("tb_ident", [128, 128]), ("tb_alt", [128, HID]),
        ("tb_rcq", [n_tok, D]), ("tb_rsq", [n_tok, D]),
        ("tb_rck", [n_tok, D]), ("tb_rsk", [n_tok, D]),
        ("tb_dcos", [HID, NSPEC]), ("tb_dsin", [HID, NSPEC]), ("tb_dnyq", [128, HID // 128]),
        ("tb_icos", [NSPEC, HID]), ("tb_isin", [NSPEC, HID]),
    ]:
        T[nm] = dt(nm, shp, BF16, kind="ExternalInput")

    T["out_my"] = dt("out_my", [MYT, HID], F32, kind="ExternalOutput")

    T["xnT_sc"] = dt("xnT_sc", [NCH, 32, 128, C], BF16)
    T["qkv_sc"] = dt("qkv_sc", [NCH, C, QCOLS + 2 * KCOLS], F32)
    T["o_mine"] = dt("o_mine", [n_tok, QCOLS], BF16)
    T["o_all"] = dt("o_all", [2 * n_tok, QCOLS], BF16)
    T["x1_sc"] = dt("x1_sc", [MYT, HID], F32)
    T["xn1T_sc"] = dt("xn1T_sc", [MCH, 32, 128, C], BF16)
    T["fr_sc"] = dt("fr_sc", [MYT, NSPEC], F32)
    T["fi_sc"] = dt("fi_sc", [MYT, NSPEC], F32)
    T["fn_sc"] = dt("fn_sc", [MYT, 8], F32)
    T["p_sc"] = dt("p_sc", [MYT, NSPEC], BF16)
    T["hT_sc"] = dt("hT_sc", [MCH, 32, 128, C], BF16)
    T["r_sc"] = dt("r_sc", [MYT, NSPEC], BF16)
    T["i_sc"] = dt("i_sc", [MYT, NSPEC], BF16)
    T["rn_sc"] = dt("rn_sc", [MYT, 8], F32)
    T["act_sc"] = dt("act_sc", [MCH, 128, FFN_H], BF16)
    T["act1T_sc"] = dt("act1T_sc", [NHT, NTS, 128, TSW], BF16)

    taps = {}

    def tap(name, shape, dtype=F32):
        if name in debug_taps:
            taps[name] = nc.dram_tensor("tap_" + name, shape, dtype,
                                        kind="ExternalOutput")
            return taps[name]
        return None

    with ExitStack() as ctx:
        pg = _mk_prog(nc, ctx)
        _consts(pg, T)
        _phase_a(pg, T, n_tok, tap)
        _phase_b(pg, T, n_tok, tap)
        if single_core:
            nc.gpsimd.wait_ge(pg.sST.h, pg.sST.n)
            pg.sSG.inc(nc.gpsimd.dma_start(T["o_all"][0:n_tok, :],
                                           T["o_mine"][:, :]), 16)
            pg.sSG.inc(nc.gpsimd.dma_start(T["o_all"][n_tok:2 * n_tok, :],
                                           T["o_mine"][:, :]), 16)
            pg.cc_wait = (pg.sSG, pg.sSG.n)
        else:
            nc.gpsimd.wait_ge(pg.sST.h, pg.sST.n)
            cc = nc.gpsimd.collective_compute(
                "AllGather", ALU.bypass,
                replica_groups=[[0, 1], [2, 3], [4, 5], [6, 7]],
                ins=[T["o_mine"].ap().opt()],
                outs=[T["o_all"].ap().opt()],
            )
            pg.sCC.inc(cc, 1)
            pg.cc_wait = (pg.sCC, pg.sCC.n)
        for nm, fn in [("c_ffn0", _phase_c_ffn0), ("c_wo", _phase_c_wo),
                       ("d0", _phase_d0_ln), ("d1", _phase_d1_fwd),
                       ("d2", _phase_d2_corr), ("d3", _phase_d3_gprod),
                       ("d4", _phase_d4_inv), ("d5", _phase_d5_ffn1)]:
            if stop_after == "ab":
                break
            fn(pg, T, n_tok, tap)
            if stop_after == nm:
                break
        nc.sync.wait_ge(pg.sST.h, pg.sST.n)
        nc.sync.wait_ge(pg.sSG.h, pg.sSG.n)
        nc.gpsimd.wait_ge(pg.sSG.h, pg.sSG.n)

    return nc, T, taps


def _consts(pg, T):
    nc = pg.nc
    pg.ident = sbuf(pg, "ident", [128, 128], BF16)
    pg.utri = sbuf(pg, "utri", [C, C], F32)
    pg.swam = sbuf(pg, "swam", [C, C + W], F32)
    pg.swam0 = sbuf(pg, "swam0", [C, C + W], F32)
    pg.ones_bf = sbuf(pg, "ones_bf", [1, 512], BF16)
    pg.dnyq_b = sbuf(pg, "dnyq_b", [128, 32, 1], BF16)
    pg.cvals = sbuf(pg, "cvals", [128, 8], F32)

    pg.sLD.inc(nc.sync.dma_start(pg.ident[:, :], T["tb_ident"][:, :]), 16)
    pg.sLD.inc(nc.sync.dma_start(pg.utri[:, :], T["tb_utri"][:, :]), 16)
    pg.sLD.inc(nc.sync.dma_start(pg.swam[:, :], T["tb_swam"][:, :]), 16)
    pg.sLD.inc(nc.sync.dma_start(pg.swam0[:, :], T["tb_swam0"][:, :]), 16)
    pg.sLD.inc(nc.sync.dma_start(pg.dnyq_b[:, :, 0], T["tb_dnyq"][:, :]), 16)
    pg.sLD.wait(nc.vector)
    clist = [EPS, -1.0, 0.5, -0.5, 1.0, 0.0]
    for i, v in enumerate(clist):
        nc.vector.memset(pg.cvals[:, i:i + 1], v)
        nc.const_aps.register(F32, v, pg.cvals[:, i:i + 1])
    pg.sDV.inc(nc.vector.memset(pg.ones_bf[:, :], 1.0))
    pg.sDV.wait(nc.scalar)
    pg.sDV.wait(nc.tensor)


def _layernorm(pg, x_t, xn_bf, wrk, stat):
    nc = pg.nc
    n = x_t.shape[1]
    pg.sDV.wait(nc.scalar)
    nc.scalar.activation(wrk[:, :n], x_t[:, :], AF.Square, bias=0.0, scale=1.0,
                         accum_out=stat[:, 1:2])
    a2 = nc.scalar.activation(wrk[:, :n], x_t[:, :], AF.Copy, bias=0.0, scale=1.0,
                              accum_out=stat[:, 0:1])
    pg.sAC.inc(a2)
    pg.sAC.wait(nc.vector)
    nc.vector.tensor_scalar(stat[:, 2:3], stat[:, 0:1], 1.0 / n, None, ALU.mult)
    t0 = nc.vector.tensor_scalar(stat[:, 3:4], stat[:, 1:2], 1.0 / n, None, ALU.mult)
    pg.sDV.inc(t0)
    pg.sDV.wait(nc.vector)
    t1 = nc.vector.tensor_scalar(stat[:, 6:7], stat[:, 2:3], stat[:, 2:3], None,
                                 ALU.mult)
    pg.sDV.inc(t1)
    pg.sDV.wait(nc.vector)
    v = nc.vector.tensor_scalar(stat[:, 4:5], stat[:, 3:4], stat[:, 6:7], None,
                                ALU.subtract)
    pg.sDV.inc(v)
    pg.sDV.wait(nc.scalar)
    nc.scalar.activation(stat[:, 5:6], stat[:, 4:5], AF.Ln, bias=EPS, scale=1.0)
    a3 = nc.scalar.activation(stat[:, 5:6], stat[:, 5:6], AF.Exp, bias=0.0, scale=-0.5)
    pg.sAC.inc(a3)
    pg.sAC.wait(nc.vector)
    xn = nc.vector.tensor_scalar(xn_bf[:, :], x_t[:, :], stat[:, 2:3], stat[:, 5:6],
                                 ALU.subtract, ALU.mult)
    pg.sDV.inc(xn)


def _transpose_batch(pg, pairs, pbank, nper=8):
    nc = pg.nc
    i = 0
    while i < len(pairs):
        batch = pairs[i:i + nper]
        for j, (s, d) in enumerate(batch):
            mm = nc.tensor.transpose(pbank[:, j * 128:(j + 1) * 128], s,
                                     pg.ident[:, :])
        pg.sPE.inc(mm)
        pg.sPE.wait(nc.vector)
        for j, (s, d) in enumerate(batch):
            cp = nc.vector.tensor_copy(d, pbank[:, j * 128:(j + 1) * 128])
        pg.sDV.inc(cp)
        pg.sDV.wait(nc.tensor)
        i += nper



def _phase_a(pg, T, n_tok, tap):
    nc = pg.nc
    phase_barrier(pg)
    NCH = n_tok // C
    QCOLS = (H // 2) * D
    KCOLS = (KVH // 2) * D
    GRP = min(8, NCH)
    NGRP = NCH // GRP
    NSTR = (QCOLS + 2 * KCOLS) // 512

    tap_xn = tap("xn", [n_tok, HID], BF16)

    with ExitStack() as ctx:
        E = ctx.enter_context
        x_t = [E(nc.sbuf_tensor(f"pa_x{i}", [128, HID], F32)) for i in range(2)]
        wrk = E(nc.sbuf_tensor("pa_wrk", [128, HID], F32))
        stat = E(nc.sbuf_tensor("pa_stat", [128, 8], F32))
        xn_bf = E(nc.sbuf_tensor("pa_xnbf", [128, HID], BF16))
        xnT = E(nc.sbuf_tensor("pa_xnT", [128, GRP, 32, 128], BF16))
        wstr = [E(nc.sbuf_tensor(f"pa_w{i}", [128, 32, 512], BF16)) for i in range(2)]
        qkv_o = [E(nc.sbuf_tensor(f"pa_qo{i}", [128, 512], F32)) for i in range(2)]
        pb_t = E(nc.psum_tensor("pa_pt", [128, 1024], BF16))
        pb_mm = [E(nc.psum_tensor(f"pa_pm{i}", [128, 512], F32)) for i in range(2)]

        wsrc = [(T["wq"], s * 512) for s in range(QCOLS // 512)]
        wsrc.append((T["wk"], 0))
        wsrc.append((T["wv"], 0))

        lnfence = [0, 0]
        wfence = [0, 0]
        qfence = [0, 0]
        qstfence = [0, 0]
        for g in range(NGRP):
            grp_pe = pg.sPE.n
            nc.vector.wait_ge(pg.sPE.h, grp_pe)
            for cc in range(GRP):
                ci = g * GRP + cc
                par = ci % 2
                nc.sync.wait_ge(pg.sDV.h, lnfence[par])
                ld = nc.sync.dma_start(x_t[par][:, :],
                                       T["x_b"][ci * C:(ci + 1) * C, :])
                pg.sLD.inc(ld, 16)
                pg.sLD.wait(nc.scalar)
                pg.sLD.wait(nc.vector)
                _layernorm(pg, x_t[par], xn_bf, wrk, stat)
                lnfence[par] = pg.sDV.n
                if tap_xn is not None:
                    pg.sDV.wait(nc.sync)
                    pg.sST.inc(nc.sync.dma_start(
                        tap_xn[ci * C:(ci + 1) * C, :], xn_bf[:, :]), 16)
                pg.sDV.wait(nc.tensor)
                _transpose_batch(
                    pg,
                    [(xn_bf[:, ft * 128:(ft + 1) * 128], xnT[:, cc, ft, :])
                     for ft in range(32)], pb_t)
                pg.sDV.wait(nc.sync)
                st = nc.sync.dma_start(
                    T["xnT_sc"][ci].rearrange("ft p t -> p ft t"),
                    xnT[:, cc, :, :])
                pg.sST.inc(st, 16)
            for s in range(NSTR):
                spar = s % 2
                wdram, col0 = wsrc[s]
                nc.gpsimd.wait_ge(pg.sPE.h, wfence[spar])
                ldw = nc.gpsimd.dma_start(
                    wstr[spar][:, :, :],
                    wdram[:, col0:col0 + 512].rearrange("(kt p) n -> p kt n", p=128))
                pg.sLG.inc(ldw, 16)
                pg.sLG.wait(nc.tensor)
                for cc in range(GRP):
                    ci = g * GRP + cc
                    mpar = (s * GRP + cc) % 2
                    nc.tensor.wait_ge(pg.sDV.h, qfence[mpar])
                    for kt in range(32):
                        mm = nc.tensor.matmul(pb_mm[mpar][:, :],
                                              xnT[:, cc, kt, :],
                                              wstr[spar][:, kt, :],
                                              start=(kt == 0), stop=(kt == 31))
                    pg.sPE.inc(mm)
                    wfence[spar] = pg.sPE.n
                    pg.sPE.wait(nc.vector)
                    nc.vector.wait_ge(pg.sST.h, qstfence[mpar])
                    ev = nc.vector.tensor_copy(qkv_o[mpar][:, :], pb_mm[mpar][:, :])
                    pg.sDV.inc(ev)
                    qfence[mpar] = pg.sDV.n
                    pg.sDV.wait(nc.sync)
                    st = nc.sync.dma_start(
                        T["qkv_sc"][ci, :, s * 512:(s + 1) * 512], qkv_o[mpar][:, :])
                    pg.sST.inc(st, 16)
                    qstfence[mpar] = pg.sST.n



def _phase_b(pg, T, n_tok, tap):
    nc = pg.nc
    phase_barrier(pg)
    NCH = n_tok // C
    QCOLS = (H // 2) * D
    KCOLS = (KVH // 2) * D
    NKV = KVH // 2
    NQ = H // 2

    tap_o = tap("o_", [n_tok, QCOLS], BF16)
    tap_qs = tap("qs", [n_tok, QCOLS])
    tap_gs = tap("gs", [n_tok, KCOLS])
    tap_ks = tap("ks", [n_tok, KCOLS])

    with ExitStack() as ctx:
        E = ctx.enter_context
        q4 = [E(nc.sbuf_tensor(f"pb_q{i}", [128, QCOLS], F32)) for i in range(2)]
        k4 = [E(nc.sbuf_tensor(f"pb_k{i}", [128, KCOLS], F32)) for i in range(2)]
        v4 = [E(nc.sbuf_tensor(f"pb_v{i}", [128, KCOLS], BF16)) for i in range(3)]
        vlast = [E(nc.sbuf_tensor(f"pb_vl{i}", [64, KCOLS], BF16)) for i in range(2)]
        rop = [E(nc.sbuf_tensor(f"pb_r{i}", [128, 4, D], BF16)) for i in range(2)]
        gs = E(nc.sbuf_tensor("pb_gs", [128, KCOLS], F32))
        w1 = E(nc.sbuf_tensor("pb_w1", [128, QCOLS], F32))
        w2 = E(nc.sbuf_tensor("pb_w2", [128, QCOLS], F32))
        qs = E(nc.sbuf_tensor("pb_qs", [128, QCOLS], F32))
        ks = E(nc.sbuf_tensor("pb_ks", [128, KCOLS], F32))
        qrope = E(nc.sbuf_tensor("pb_qrope", [128, QCOLS], BF16))
        krope = E(nc.sbuf_tensor("pb_krope", [128, KCOLS], BF16))
        stat = E(nc.sbuf_tensor("pb_stat", [128, 64], F32))
        qs_bf = E(nc.sbuf_tensor("pb_qsbf", [128, QCOLS], BF16))
        ks_bf = E(nc.sbuf_tensor("pb_ksbf", [128, KCOLS], BF16))
        qsT = E(nc.sbuf_tensor("pb_qsT", [128, NQ, 128], BF16))
        ksT = E(nc.sbuf_tensor("pb_ksT", [128, NKV, 128], BF16))
        qrT = E(nc.sbuf_tensor("pb_qrT", [128, NQ, 128], BF16))
        krT = [E(nc.sbuf_tensor(f"pb_krT{i}", [128, NKV, 128], BF16))
               for i in range(3)]
        eaT = E(nc.sbuf_tensor("pb_eaT", [128, KCOLS], F32))
        enaT = E(nc.sbuf_tensor("pb_enaT", [128, KCOLS], F32))
        ekH = E(nc.sbuf_tensor("pb_ekH", [128, KCOLS], F32))
        ktil = E(nc.sbuf_tensor("pb_ktil", [128, 128], BF16))
        kHbf = E(nc.sbuf_tensor("pb_kHbf", [128, 128], BF16))
        kHT = E(nc.sbuf_tensor("pb_kHT", [128, 128], BF16))
        qtilT = E(nc.sbuf_tensor("pb_qtilT", [128, 4, 128], BF16))
        ptm = E(nc.sbuf_tensor("pb_ptm", [128, 4, 128], BF16))
        S = E(nc.sbuf_tensor("pb_S", [128, KCOLS], F32))
        S_bf = E(nc.sbuf_tensor("pb_Sbf", [128, KCOLS], BF16))
        sco = E(nc.sbuf_tensor("pb_sco", [128, 4, C + W], F32))
        pt_sw = E(nc.sbuf_tensor("pb_ptsw", [128, 4, C + W], BF16))
        ptA = E(nc.sbuf_tensor("pb_ptA", [64, 4, 128], BF16))
        ptB = E(nc.sbuf_tensor("pb_ptB", [128, 4, 128], BF16))
        o_ch = E(nc.sbuf_tensor("pb_och", [128, QCOLS], BF16))

        pb_a = E(nc.psum_tensor("pb_pa", [128, 512], F32))
        pb_t = E(nc.psum_tensor("pb_pt", [128, 1024], BF16))
        pb_pt = E(nc.psum_tensor("pb_ppt", [128, 512], F32))
        pb_o = E(nc.psum_tensor("pb_po", [128, 512], F32))
        pb_s1 = E(nc.psum_tensor("pb_ps1", [128, 2, C + W], F32))
        pb_s2 = E(nc.psum_tensor("pb_ps2", [128, 2, C + W], F32))
        pb_ds = E(nc.psum_tensor("pb_pds", [128, 512], F32))

        nc.vector.memset(S[:, :], 0.0)
        nc.vector.memset(krT[2][:, :, :], 0.0)
        nc.vector.memset(v4[2][:, :], 0.0)
        mz = nc.vector.memset(vlast[0][:, :], 0.0)
        pg.sDV.inc(mz)

        pe_end = {-1: pg.sPE.n, -2: pg.sPE.n}
        dv_end = {-1: pg.sDV.n, -2: pg.sDV.n}
        ds_fence = 0
        o_fence = 0

        def dma_in(ci):
            par = ci % 2
            nc.sync.wait_ge(pg.sPE.h, pe_end[ci - 2])
            nc.sync.wait_ge(pg.sDV.h, dv_end[ci - 2])
            pg.sLD.inc(nc.sync.dma_start(q4[par][:, :],
                                         T["qkv_sc"][ci, :, 0:QCOLS]), 16)
            pg.sLD.inc(nc.sync.dma_start(k4[par][:, :],
                                         T["qkv_sc"][ci, :, QCOLS:QCOLS + KCOLS]), 16)
            for j, tb in enumerate(("tb_rcq", "tb_rsq", "tb_rck", "tb_rsk")):
                pg.sLD.inc(nc.sync.dma_start(rop[par][:, j, :],
                                             T[tb][ci * C:(ci + 1) * C, :]), 16)
            nc.gpsimd.wait_ge(pg.sPE.h, pe_end[ci - 2])
            pg.sLG.inc(nc.gpsimd.dma_start(v4[ci % 3][:, :],
                                           T["qkv_sc"][ci, :, QCOLS + KCOLS:]), 16)
            if ci >= 1:
                pg.sLG.inc(nc.gpsimd.dma_start(
                    vlast[ci % 2][:, :],
                    T["qkv_sc"][ci - 1, 64:128, QCOLS + KCOLS:]), 16)

        nc.sync.wait_ge(pg.sST.h, pg.sST.n)
        nc.gpsimd.wait_ge(pg.sST.h, pg.sST.n)
        dma_in(0)
        for ci in range(NCH):
            par = ci % 2
            v_cur = v4[ci % 3]
            krT_cur = krT[ci % 3]
            krT_prev = krT[(ci - 1) % 3]
            ld_mark = pg.sLD.n
            lg_mark = pg.sLG.n
            if ci + 1 < NCH:
                dma_in(ci + 1)
            nc.vector.wait_ge(pg.sPE.h, pe_end[ci - 1])
            nc.scalar.wait_ge(pg.sPE.h, pe_end[ci - 1])
            nc.vector.wait_ge(pg.sLD.h, ld_mark)
            nc.vector.wait_ge(pg.sLG.h, lg_mark)
            nc.scalar.wait_ge(pg.sLD.h, ld_mark)
            nc.tensor.wait_ge(pg.sLG.h, lg_mark)
            nc.scalar.wait_ge(pg.sST.h, o_fence)

            kt_ = k4[par]
            qt_ = q4[par]

            nc.scalar.activation(w2[:, 0:KCOLS], kt_[:, :], AF.Abs,
                                 bias=0.0, scale=1.0)
            nc.scalar.activation(w2[:, 0:KCOLS], w2[:, 0:KCOLS], AF.Exp,
                                 bias=0.0, scale=-1.0)
            ac = nc.scalar.activation(w2[:, 0:KCOLS], w2[:, 0:KCOLS], AF.Ln,
                                      bias=1.0, scale=1.0)
            pg.sAC.inc(ac)
            nc.vector.tensor_scalar(gs[:, :], kt_[:, :], -1.0, 0.0, ALU.mult, ALU.max)
            pg.sAC.wait(nc.vector)
            nc.vector.tensor_tensor(gs[:, :], gs[:, :], w2[:, 0:KCOLS], ALU.add)
            gsi = nc.vector.tensor_scalar(gs[:, :], gs[:, :], -1.0 / 16.0, None,
                                          ALU.mult)
            pg.sDV.inc(gsi)
            if tap_gs is not None:
                pg.sDV.wait(nc.sync)
                pg.sST.inc(nc.sync.dma_start(tap_gs[ci * C:(ci + 1) * C, :],
                                             gs[:, :]), 16)

            kg = kt_[:, :].rearrange("p (h d) -> p h d", d=D)
            nc.vector.tensor_reduce(
                stat[:, 0:NKV].rearrange("p (h o) -> p h o", o=1), kg, AX.X, ALU.max)
            qg = qt_[:, :].rearrange("p (h d) -> p h d", d=D)
            rr = nc.vector.tensor_reduce(
                stat[:, 16:16 + NQ].rearrange("p (h o) -> p h o", o=1),
                qg, AX.X, ALU.max)
            pg.sDV.inc(rr)
            pg.sDV.wait(nc.vector)
            nc.vector.tensor_scalar(stat[:, 8:8 + NKV], stat[:, 0:NKV], -1.0, None,
                                    ALU.mult)
            nv = nc.vector.tensor_scalar(stat[:, 32:32 + NQ], stat[:, 16:16 + NQ],
                                         -1.0, None, ALU.mult)
            pg.sDV.inc(nv)
            pg.sDV.wait(nc.scalar)
            for h_ in range(NKV):
                nc.scalar.activation(ks[:, h_ * D:(h_ + 1) * D],
                                     kt_[:, h_ * D:(h_ + 1) * D], AF.Exp,
                                     bias=stat[:, 8 + h_:9 + h_], scale=1.0,
                                     accum_out=stat[:, 12 + h_:13 + h_])
            for h_ in range(NQ):
                ac = nc.scalar.activation(qs[:, h_ * D:(h_ + 1) * D],
                                          qt_[:, h_ * D:(h_ + 1) * D], AF.Exp,
                                          bias=stat[:, 32 + h_:33 + h_], scale=1.0,
                                          accum_out=stat[:, 48 + h_:49 + h_])
            pg.sAC.inc(ac)
            pg.sAC.wait(nc.vector)
            nc.vector.reciprocal(stat[:, 12:12 + NKV], stat[:, 12:12 + NKV])
            rc = nc.vector.reciprocal(stat[:, 48:48 + NQ], stat[:, 48:48 + NQ])
            pg.sDV.inc(rc)
            pg.sDV.wait(nc.vector)
            for h_ in range(NKV):
                nc.vector.tensor_scalar(ks[:, h_ * D:(h_ + 1) * D],
                                        ks[:, h_ * D:(h_ + 1) * D],
                                        stat[:, 12 + h_:13 + h_], None, ALU.mult)
            nc.vector.tensor_copy(ks_bf[:, :], ks[:, :])
            for h_ in range(NQ):
                nc.vector.tensor_scalar(qs[:, h_ * D:(h_ + 1) * D],
                                        qs[:, h_ * D:(h_ + 1) * D],
                                        stat[:, 48 + h_:49 + h_], None, ALU.mult)
            qcast = nc.vector.tensor_copy(qs_bf[:, :], qs[:, :])
            pg.sDV.inc(qcast)
            if tap_qs is not None:
                pg.sDV.wait(nc.sync)
                pg.sST.inc(nc.sync.dma_start(tap_qs[ci * C:(ci + 1) * C, :],
                                             qs[:, :]), 16)
            if tap_ks is not None:
                pg.sDV.wait(nc.sync)
                pg.sST.inc(nc.sync.dma_start(tap_ks[ci * C:(ci + 1) * C, :],
                                             ks[:, :]), 16)

            def rope(dst, src, ncols, cosi, sini):
                nh = ncols // D
                g3 = src.rearrange("p (h d) -> p h d", d=D)
                gd = dst.rearrange("p (h d) -> p h d", d=D)
                wv = w2[:, 0:ncols].rearrange("p (h d) -> p h d", d=D)
                nc.vector.tensor_scalar(wv[:, :, 0:64], g3[:, :, 64:128], -1.0,
                                        None, ALU.mult)
                nc.vector.tensor_copy(wv[:, :, 64:128], g3[:, :, 0:64])
                cosb = rop[par][:, cosi:cosi + 1, :].to_broadcast([128, nh, D])
                sinb = rop[par][:, sini:sini + 1, :].to_broadcast([128, nh, D])
                nc.vector.tensor_tensor(wv, wv, sinb, ALU.mult)
                t2v = w1[:, 0:ncols].rearrange("p (h d) -> p h d", d=D)
                nc.vector.tensor_tensor(t2v, g3, cosb, ALU.mult)
                return nc.vector.tensor_tensor(gd, t2v, wv, ALU.add)

            rope(qrope[:, :], qt_[:, :], QCOLS, 0, 1)
            rp = rope(krope[:, :], kt_[:, :], KCOLS, 2, 3)
            pg.sDV.inc(rp)

            pg.sDV.wait(nc.tensor)
            pg.sAC.wait(nc.tensor)
            for h_ in range(NKV):
                mm = nc.tensor.matmul(pb_a[:, h_ * 128:(h_ + 1) * 128],
                                      gs[:, h_ * D:(h_ + 1) * D], pg.utri[:, :],
                                      start=True, stop=True)
            pg.sPE.inc(mm)
            pg.sPE.wait(nc.scalar)
            nc.scalar.activation(eaT[:, :], pb_a[:, :], AF.Exp, bias=0.0, scale=1.0)
            ac = nc.scalar.activation(enaT[:, :], pb_a[:, :], AF.Exp, bias=0.0,
                                      scale=-1.0)
            pg.sAC.inc(ac)
            pg.sAC.wait(nc.vector)
            for h_ in range(NKV):
                nc.vector.tensor_scalar(ekH[:, h_ * D:(h_ + 1) * D],
                                        enaT[:, h_ * D:(h_ + 1) * D],
                                        eaT[:, h_ * D + 127:h_ * D + 128],
                                        None, ALU.mult)
            scst = nc.vector.tensor_copy(S_bf[:, :], S[:, :])
            pg.sDV.inc(scst)

            pg.sDV.wait(nc.tensor)
            pairs = ([(qs_bf[:, i * 128:(i + 1) * 128], qsT[:, i, :])
                      for i in range(NQ)]
                     + [(ks_bf[:, i * 128:(i + 1) * 128], ksT[:, i, :])
                        for i in range(NKV)]
                     + [(qrope[:, i * 128:(i + 1) * 128], qrT[:, i, :])
                        for i in range(NQ)]
                     + [(krope[:, i * 128:(i + 1) * 128], krT_cur[:, i, :])
                        for i in range(NKV)])
            _transpose_batch(pg, pairs, pb_t)

            for h_ in range(NKV):
                hsl = slice(h_ * D, (h_ + 1) * D)
                nc.vector.tensor_tensor(ktil[:, :], ksT[:, h_, :], enaT[:, hsl],
                                        ALU.mult)
                nc.vector.tensor_tensor(kHbf[:, :], ksT[:, h_, :], ekH[:, hsl],
                                        ALU.mult)
                for j in range(4):
                    tt = nc.vector.tensor_tensor(qtilT[:, j, :],
                                                 qsT[:, h_ * 4 + j, :],
                                                 eaT[:, hsl], ALU.mult)
                pg.sDV.inc(tt)
                pg.sDV.wait(nc.tensor)
                for j in range(4):
                    nc.tensor.matmul(pb_pt[:, j * 128:(j + 1) * 128], ktil[:, :],
                                     qtilT[:, j, :], start=True, stop=True)
                nc.tensor.transpose(pb_t[:, 0:128], kHbf[:, :], pg.ident[:, :])
                for j in range(4):
                    qh = h_ * 4 + j
                    pb_s = pb_s1 if j < 2 else pb_s2
                    jj = j % 2
                    nc.tensor.matmul(pb_s[:, jj, 0:W], qrT[:, qh, :],
                                     krT_prev[:, h_, 64:128], start=True, stop=True)
                    mm = nc.tensor.matmul(pb_s[:, jj, W:W + C], qrT[:, qh, :],
                                          krT_cur[:, h_, :], start=True, stop=True)
                pg.sPE.inc(mm)
                pg.sPE.wait(nc.vector)
                for j in range(4):
                    nc.vector.tensor_tensor(ptm[:, j, :],
                                            pb_pt[:, j * 128:(j + 1) * 128],
                                            pg.utri[:, :], ALU.mult)
                nc.vector.tensor_copy(kHT[:, :], pb_t[:, 0:128])
                msk = pg.swam0 if ci == 0 else pg.swam
                for j in range(4):
                    pb_s = pb_s1 if j < 2 else pb_s2
                    nc.vector.tensor_tensor(sco[:, j, :], pb_s[:, j % 2, :],
                                            msk[:, :], ALU.add)
                rr = nc.vector.tensor_reduce(
                    stat[:, 0:4].rearrange("p (h o) -> p h o", o=1),
                    sco[:, :, :], AX.X, ALU.max)
                pg.sDV.inc(rr)
                pg.sDV.wait(nc.vector)
                rd = nc.vector.tensor_scalar(stat[:, 4:8], stat[:, 0:4], -1.0,
                                             None, ALU.mult)
                pg.sDV.inc(rd)
                pg.sDV.wait(nc.scalar)
                pg.sDV.wait(nc.tensor)
                for j in range(4):
                    ac = nc.scalar.activation(sco[:, j, :], sco[:, j, :], AF.Exp,
                                              bias=stat[:, 4 + j:5 + j], scale=1.0,
                                              accum_out=stat[:, 8 + j:9 + j])
                pg.sAC.inc(ac)
                pg.sAC.wait(nc.vector)
                rc = nc.vector.reciprocal(stat[:, 8:12], stat[:, 8:12])
                pg.sDV.inc(rc)
                pg.sDV.wait(nc.vector)
                for j in range(4):
                    sw = nc.vector.tensor_scalar(pt_sw[:, j, :], sco[:, j, :],
                                                 stat[:, 8 + j:9 + j], None, ALU.mult)
                pg.sDV.inc(sw)
                pg.sDV.wait(nc.tensor)
                nc.tensor.wait_ge(pg.sDV.h, ds_fence)
                nc.tensor.matmul(pb_ds[:, h_ * 128:(h_ + 1) * 128], kHT[:, :],
                                 v_cur[:, hsl], start=True, stop=True)
                for j in range(4):
                    nc.tensor.transpose(pb_t[0:64, j * 128:(j + 1) * 128],
                                        pt_sw[:, j, 0:64], pg.ident[:, :])
                    tr = nc.tensor.transpose(
                        pb_t[:, 512 + j * 128:512 + (j + 1) * 128],
                        pt_sw[:, j, 64:192], pg.ident[:, :])
                pg.sPE.inc(tr)
                pg.sPE.wait(nc.vector)
                nc.vector.tensor_copy(
                    ptA[:, :, :],
                    pb_t[0:64, 0:512].rearrange("p (j t) -> p j t", t=128))
                pa = nc.vector.tensor_copy(
                    ptB[:, :, :],
                    pb_t[:, 512:1024].rearrange("p (j t) -> p j t", t=128))
                pg.sDV.inc(pa)
                pg.sDV.wait(nc.tensor)
                for j in range(4):
                    po = pb_o[:, j * 128:(j + 1) * 128]
                    nc.tensor.matmul(po, ptm[:, j, :], v_cur[:, hsl],
                                     start=True, stop=False)
                    nc.tensor.matmul(po, qtilT[:, j, :], S_bf[:, hsl],
                                     start=False, stop=False)
                    nc.tensor.matmul(po, ptA[:, j, :], vlast[ci % 2][:, hsl],
                                     start=False, stop=False)
                    mm = nc.tensor.matmul(po, ptB[:, j, :], v_cur[:, hsl],
                                          start=False, stop=True)
                pg.sPE.inc(mm)
                pg.sPE.wait(nc.scalar)
                oc = nc.scalar.activation(o_ch[:, h_ * 512:(h_ + 1) * 512],
                                          pb_o[:, :], AF.Copy, bias=0.0, scale=0.5)
                pg.sAC.inc(oc)
                pg.sAC.wait(nc.tensor)
                pg.sPE.wait(nc.vector)
                nc.vector.tensor_scalar(S[:, hsl], S[:, hsl],
                                        eaT[:, h_ * D + 127:h_ * D + 128], None,
                                        ALU.mult)
                su = nc.vector.tensor_tensor(S[:, hsl], S[:, hsl],
                                             pb_ds[:, h_ * 128:(h_ + 1) * 128],
                                             ALU.add)
                pg.sDV.inc(su)
                ds_fence = pg.sDV.n

            pg.sAC.wait(nc.sync)
            st = nc.sync.dma_start(T["o_mine"][ci * C:(ci + 1) * C, :], o_ch[:, :])
            pg.sST.inc(st, 16)
            o_fence = pg.sST.n
            if tap_o is not None:
                pg.sST.inc(nc.sync.dma_start(tap_o[ci * C:(ci + 1) * C, :],
                                             o_ch[:, :]), 16)
            pe_end[ci] = pg.sPE.n
            dv_end[ci] = pg.sDV.n



def _phase_c_ffn0(pg, T, n_tok, tap):
    nc = pg.nc
    phase_barrier(pg)
    MYT = n_tok // 2
    MCH = MYT // C
    NCH = n_tok // C

    tap_f = tap("ffn0", [MYT, HID])
    tap_act = tap("act0", [MYT, FFN_H], BF16)

    with ExitStack() as ctx:
        E = ctx.enter_context
        xnT_ch = E(nc.sbuf_tensor("c1_xnT", [128, 32, 128], BF16))
        xfT = E(nc.sbuf_tensor("c1_xfT", [128, 8, MYT], BF16))
        u1T = E(nc.sbuf_tensor("c1_u1T", [2, MYT], BF16))
        u3T = E(nc.sbuf_tensor("c1_u3T", [2, MYT], BF16))
        wrkf = E(nc.sbuf_tensor("c1_wrkf", [128, 128], F32))
        pb_u = E(nc.psum_tensor("c1_pu", [2, 512], F32))
        w1s = [E(nc.sbuf_tensor(f"c1_w1s{i}", [128, 8, 512], BF16)) for i in range(2)]
        w3s = [E(nc.sbuf_tensor(f"c1_w3s{i}", [128, 8, 512], BF16)) for i in range(2)]
        silu_t = E(nc.sbuf_tensor("c1_silu", [128, 512], F32))
        act_t = E(nc.sbuf_tensor("c1_act", [128, 512], BF16))
        pb_m = [E(nc.psum_tensor(f"c1_pm{i}", [128, 512], F32)) for i in range(4)]
        pg.a1T_b = E(nc.sbuf_tensor("a1T_b", [128, 32, 2], BF16))
        pg.a3T_b = E(nc.sbuf_tensor("a3T_b", [128, 32, 2], BF16))
        pg.b1T_b = E(nc.sbuf_tensor("b1T_b", [2, FFN_H], BF16))
        pg.b3T_b = E(nc.sbuf_tensor("b3T_b", [2, FFN_H], BF16))
        pg.sLG.inc(nc.gpsimd.dma_start(
            pg.a1T_b[:, :, :],
            T["a1T"][:, :].rearrange("(ft p) r -> p ft r", p=128)), 16)
        pg.sLG.inc(nc.gpsimd.dma_start(
            pg.a3T_b[:, :, :],
            T["a3T"][:, :].rearrange("(ft p) r -> p ft r", p=128)), 16)
        pg.sLG.inc(nc.gpsimd.dma_start(pg.b1T_b[:, :], T["b1T"][:, :]), 16)
        pg.sLG.inc(nc.gpsimd.dma_start(pg.b3T_b[:, :], T["b3T"][:, :]), 16)
        pg.sLG.wait(nc.tensor)
        pg.sLG.wait(nc.vector)

        ld_fence = 0
        dvf = 0
        for ci in range(MCH):
            nc.sync.wait_ge(pg.sPE.h, ld_fence)
            nc.sync.wait_ge(pg.sDV.h, dvf)
            br_load(pg, nc.sync, pg.sLD, xnT_ch[:, :, :],
                    T["xnT_sc"][ci].rearrange("ft p t -> p ft t"),
                    T["xnT_sc"][NCH // 2 + ci].rearrange("ft p t -> p ft t"))
            pg.sLD.wait(nc.vector)
            pg.sLD.wait(nc.tensor)
            tsl = slice(ci * C, (ci + 1) * C)
            for fq in range(8):
                nc.vector.tensor_tensor(wrkf[:, :], xnT_ch[:, fq, :],
                                        xnT_ch[:, fq + 8, :], ALU.add)
                nc.vector.tensor_tensor(wrkf[:, :], wrkf[:, :],
                                        xnT_ch[:, fq + 16, :], ALU.add)
                fv = nc.vector.tensor_tensor(xfT[:, fq, tsl], wrkf[:, :],
                                             xnT_ch[:, fq + 24, :], ALU.add)
            pg.sDV.inc(fv)
            dvf = pg.sDV.n
            pg.sDV.wait(nc.tensor)
            nc.tensor.wait_ge(pg.sAC.h, pg.c1uf if hasattr(pg, "c1uf") else 0)
            for kt in range(32):
                nc.tensor.matmul(pb_u[:, 0:128], pg.a1T_b[:, kt, :],
                                 xnT_ch[:, kt, :], start=(kt == 0), stop=(kt == 31))
            for kt in range(32):
                mm = nc.tensor.matmul(pb_u[:, 128:256], pg.a3T_b[:, kt, :],
                                      xnT_ch[:, kt, :], start=(kt == 0),
                                      stop=(kt == 31))
            pg.sPE.inc(mm)
            ld_fence = pg.sPE.n
            pg.sPE.wait(nc.scalar)
            nc.scalar.activation(u1T[:, tsl], pb_u[:, 0:128], AF.Copy, bias=0.0,
                                 scale=0.5)
            ua = nc.scalar.activation(u3T[:, tsl], pb_u[:, 128:256], AF.Copy,
                                      bias=0.0, scale=0.5)
            pg.sAC.inc(ua)
            pg.c1uf = pg.sAC.n

        nstrips = (FFN_H + 511) // 512
        wf = [0, 0]
        acf = [0, 0]
        dvf2 = [0, 0]
        stf = 0
        pg.sDV.wait(nc.tensor)
        pg.sAC.wait(nc.tensor)
        for si in range(nstrips):
            h0 = si * 512
            hw = min(512, FFN_H - h0)
            spar = si % 2
            nc.gpsimd.wait_ge(pg.sPE.h, wf[spar])
            pg.sLG.inc(nc.gpsimd.dma_start(
                w1s[spar][:, :, 0:hw],
                T["w1qT"][:, h0:h0 + hw].rearrange("(kt p) n -> p kt n", p=128)), 16)
            pg.sLG.inc(nc.gpsimd.dma_start(
                w3s[spar][:, :, 0:hw],
                T["w3qT"][:, h0:h0 + hw].rearrange("(kt p) n -> p kt n", p=128)), 16)
            lg_mark = pg.sLG.n
            nc.tensor.wait_ge(pg.sLG.h, lg_mark)
            for ci in range(MCH):
                tsl = slice(ci * C, (ci + 1) * C)
                bank = ci % 2
                bankU = 2 + bank
                nc.tensor.wait_ge(pg.sAC.h, acf[bank])
                nc.tensor.wait_ge(pg.sDV.h, dvf2[bank])
                for kt in range(8):
                    nc.tensor.matmul(pb_m[bank][:, 0:hw], xfT[:, kt, tsl],
                                     w1s[spar][:, kt, 0:hw], start=(kt == 0),
                                     stop=False)
                nc.tensor.matmul(pb_m[bank][:, 0:hw], u1T[:, tsl],
                                 pg.b1T_b[:, h0:h0 + hw], start=False, stop=True)
                for kt in range(8):
                    nc.tensor.matmul(pb_m[bankU][:, 0:hw], xfT[:, kt, tsl],
                                     w3s[spar][:, kt, 0:hw], start=(kt == 0),
                                     stop=False)
                mm = nc.tensor.matmul(pb_m[bankU][:, 0:hw], u3T[:, tsl],
                                      pg.b3T_b[:, h0:h0 + hw], start=False,
                                      stop=True)
                pg.sPE.inc(mm)
                wf[spar] = pg.sPE.n
                pg.sPE.wait(nc.scalar)
                nc.scalar.wait_ge(pg.sDV.h, dvf2[1 - bank])
                sl = nc.scalar.activation(silu_t[:, 0:hw], pb_m[bank][:, 0:hw],
                                          AF.Silu, bias=0.0, scale=1.0)
                pg.sAC.inc(sl)
                acf[bank] = pg.sAC.n
                pg.sAC.wait(nc.vector)
                nc.vector.wait_ge(pg.sST.h, stf)
                av = nc.vector.tensor_tensor(act_t[:, 0:hw], silu_t[:, 0:hw],
                                             pb_m[bankU][:, 0:hw], ALU.mult)
                pg.sDV.inc(av)
                dvf2[bank] = pg.sDV.n
                pg.sDV.wait(nc.sync)
                st = nc.sync.dma_start(T["act_sc"][ci, :, h0:h0 + hw],
                                       act_t[:, 0:hw])
                pg.sST.inc(st, 16)
                stf = pg.sST.n
                if tap_act is not None:
                    pg.sST.inc(nc.sync.dma_start(
                        tap_act[tsl, h0:h0 + hw], act_t[:, 0:hw]), 16)

    phase_barrier(pg)
    with ExitStack() as ctx:
        E = ctx.enter_context
        act_ch = E(nc.sbuf_tensor("c2_act", [128, FFN_H], BF16))
        hf_bf = E(nc.sbuf_tensor("c2_hf", [128, 2752], BF16))
        hfT = E(nc.sbuf_tensor("c2_hfT", [128, 22, MYT], BF16))
        u2T = E(nc.sbuf_tensor("c2_u2T", [2, MYT], BF16))
        aT_st = E(nc.sbuf_tensor("c2_aTst", [128, 8, 128], BF16))
        wrkf = E(nc.sbuf_tensor("c2_wrkf", [128, 2752], F32))
        pb_u = E(nc.psum_tensor("c2_pu", [2, 512], F32))
        pb_t = E(nc.psum_tensor("c2_pt", [128, 1024], BF16))
        w2s = [E(nc.sbuf_tensor(f"c2_w2s{i}", [128, 22, 512], BF16)) for i in range(2)]
        pb_m = [E(nc.psum_tensor(f"c2_pm{i}", [128, 512], F32)) for i in range(2)]
        x_t = E(nc.sbuf_tensor("c2_xt", [128, 512], F32))
        o_t = E(nc.sbuf_tensor("c2_ot", [128, 512], F32))
        pg.a2T_b = E(nc.sbuf_tensor("a2T_b", [128, NHT, 2], BF16))
        pg.b2T_b = E(nc.sbuf_tensor("b2T_b", [2, HID], BF16))
        pg.sLG.inc(nc.gpsimd.dma_start(
            pg.a2T_b[:, :, :],
            T["a2T"][:, :].rearrange("(ft p) r -> p ft r", p=128)), 16)
        pg.sLG.inc(nc.gpsimd.dma_start(pg.b2T_b[:, :], T["b2T"][:, :]), 16)
        pg.sLG.wait(nc.tensor)

        nc.sync.wait_ge(pg.sST.h, pg.sST.n)
        ld_fence = 0
        dvf = 0
        for ci in range(MCH):
            tsl = slice(ci * C, (ci + 1) * C)
            nc.sync.wait_ge(pg.sPE.h, ld_fence)
            nc.sync.wait_ge(pg.sDV.h, dvf)
            ld = nc.sync.dma_start(act_ch[:, :], T["act_sc"][ci, :, :])
            pg.sLD.inc(ld, 16)
            pg.sLD.wait(nc.vector)
            pg.sLD.wait(nc.tensor)
            nc.vector.tensor_tensor(wrkf[:, :], act_ch[:, 0:2752],
                                    act_ch[:, 2752:5504], ALU.add)
            nc.vector.tensor_tensor(wrkf[:, :], wrkf[:, :],
                                    act_ch[:, 5504:8256], ALU.add)
            fv = nc.vector.tensor_tensor(hf_bf[:, :], wrkf[:, :],
                                         act_ch[:, 8256:11008], ALU.add)
            pg.sDV.inc(fv)
            pg.sDV.wait(nc.tensor)
            _transpose_batch(
                pg,
                [(hf_bf[:, kt * 128:(kt + 1) * 128], hfT[:, kt, tsl])
                 for kt in range(21)], pb_t)
            tr = nc.tensor.transpose(pb_t[0:64, 0:128], hf_bf[:, 2688:2752],
                                     pg.ident[:, :])
            pg.sPE.inc(tr)
            pg.sPE.wait(nc.vector)
            hv = nc.vector.tensor_copy(hfT[0:64, 21, tsl], pb_t[0:64, 0:128])
            pg.sDV.inc(hv)
            pg.sDV.wait(nc.tensor)
            for bi in range(NHT // 8 + (1 if NHT % 8 else 0)):
                kts = list(range(bi * 8, min(bi * 8 + 8, NHT)))
                for j, kt in enumerate(kts):
                    tr = nc.tensor.transpose(pb_t[:, j * 128:(j + 1) * 128],
                                             act_ch[:, kt * 128:(kt + 1) * 128],
                                             pg.ident[:, :])
                pg.sPE.inc(tr)
                pg.sPE.wait(nc.vector)
                for j, kt in enumerate(kts):
                    cp = nc.vector.tensor_copy(aT_st[:, j, :],
                                               pb_t[:, j * 128:(j + 1) * 128])
                pg.sDV.inc(cp)
                pg.sDV.wait(nc.tensor)
                for j, kt in enumerate(kts):
                    mm = nc.tensor.matmul(pb_u[:, 0:128], pg.a2T_b[:, kt, :],
                                          aT_st[:, j, :], start=(kt == 0),
                                          stop=(kt == NHT - 1))
                pg.sPE.inc(mm)
            ld_fence = pg.sPE.n
            dvf = pg.sDV.n
            pg.sPE.wait(nc.scalar)
            ua = nc.scalar.activation(u2T[:, tsl], pb_u[:, 0:128], AF.Copy,
                                      bias=0.0, scale=0.5)
            pg.sAC.inc(ua)
            pg.sAC.wait(nc.tensor)

        pg.sDV.wait(nc.tensor)
        pg.sAC.wait(nc.tensor)
        wf = [0, 0]
        evf = [0, 0]
        stf = 0
        for s in range(8):
            spar = s % 2
            nc.gpsimd.wait_ge(pg.sPE.h, wf[spar])
            pg.sLG.inc(nc.gpsimd.dma_start(
                w2s[spar][:, 0:21, :],
                T["w2qT"][0:2688, s * 512:(s + 1) * 512].rearrange(
                    "(kt p) n -> p kt n", p=128)), 16)
            pg.sLG.inc(nc.gpsimd.dma_start(
                w2s[spar][0:64, 21, :],
                T["w2qT"][2688:2752, s * 512:(s + 1) * 512]), 16)
            lg_mark = pg.sLG.n
            nc.tensor.wait_ge(pg.sLG.h, lg_mark)
            for ci in range(MCH):
                bank = ci % 2
                tsl = slice(ci * C, (ci + 1) * C)
                nc.tensor.wait_ge(pg.sDV.h, evf[bank])
                for kt in range(21):
                    nc.tensor.matmul(pb_m[bank][:, :], hfT[:, kt, tsl],
                                     w2s[spar][:, kt, :], start=(kt == 0),
                                     stop=False)
                nc.tensor.matmul(pb_m[bank][:, :], hfT[0:64, 21, tsl],
                                 w2s[spar][0:64, 21, :], start=False, stop=False)
                mm = nc.tensor.matmul(pb_m[bank][:, :], u2T[:, tsl],
                                      pg.b2T_b[:, s * 512:(s + 1) * 512],
                                      start=False, stop=True)
                pg.sPE.inc(mm)
                wf[spar] = pg.sPE.n
                nc.sync.wait_ge(pg.sDV.h, evf[bank])
                ld = nc.sync.dma_start(x_t[:, :],
                                       T["x_my"][tsl, s * 512:(s + 1) * 512])
                pg.sLD.inc(ld, 16)
                pg.sLD.wait(nc.vector)
                pg.sPE.wait(nc.vector)
                nc.vector.wait_ge(pg.sST.h, stf)
                ev = nc.vector.tensor_tensor(o_t[:, :], pb_m[bank][:, :],
                                             x_t[:, :], ALU.add)
                pg.sDV.inc(ev)
                evf[bank] = pg.sDV.n
                pg.sDV.wait(nc.sync)
                st = nc.sync.dma_start(T["x1_sc"][tsl, s * 512:(s + 1) * 512],
                                       o_t[:, :])
                pg.sST.inc(st, 16)
                stf = pg.sST.n
                if tap_f is not None:
                    pg.sST.inc(nc.sync.dma_start(
                        tap_f[tsl, s * 512:(s + 1) * 512], o_t[:, :]), 16)



def _phase_c_wo(pg, T, n_tok, tap):
    nc = pg.nc
    phase_barrier(pg)
    MYT = n_tok // 2
    MCH = MYT // C
    QCOLS = (H // 2) * D
    GRP = min(8, MCH)
    NGRP = MCH // GRP

    tap_wo = tap("wo_out", [MYT, HID])

    with ExitStack() as ctx:
        E = ctx.enter_context
        o_in = E(nc.sbuf_tensor("cw_oin", [128, HID], BF16))
        oT = E(nc.sbuf_tensor("cw_oT", [128, GRP, 32, 128], BF16))
        wos = [E(nc.sbuf_tensor(f"cw_wos{i}", [128, 32, 512], BF16)) for i in range(2)]
        o_ev = E(nc.sbuf_tensor("cw_oev", [128, 512], F32))
        pb_t = E(nc.psum_tensor("cw_pt", [128, 1024], BF16))
        pb_m = [E(nc.psum_tensor(f"cw_pm{i}", [128, 512], F32)) for i in range(2)]

        sem, val = pg.cc_wait
        nc.sync.wait_ge(sem.h, val)
        nc.gpsimd.wait_ge(sem.h, val)
        nc.gpsimd.wait_ge(pg.sST.h, pg.sST.n)

        wf = [0, 0]
        evf = [0, 0]
        sgf = 0
        for g in range(NGRP):
            grp_pe = pg.sPE.n
            nc.vector.wait_ge(pg.sPE.h, grp_pe)
            for cc in range(GRP):
                ci = g * GRP + cc
                pg.sDV.wait(nc.sync)
                for slot in range(2):
                    br_load(pg, nc.sync, pg.sLD,
                            o_in[:, slot * QCOLS:(slot + 1) * QCOLS],
                            T["o_all"][slot * n_tok + ci * C:
                                       slot * n_tok + (ci + 1) * C, :],
                            T["o_all"][slot * n_tok + MYT + ci * C:
                                       slot * n_tok + MYT + (ci + 1) * C, :])
                pg.sLD.wait(nc.tensor)
                _transpose_batch(pg, [(o_in[:, ft * 128:(ft + 1) * 128],
                                       oT[:, cc, ft, :]) for ft in range(32)], pb_t)
            for s in range(8):
                spar = s % 2
                nc.gpsimd.wait_ge(pg.sPE.h, wf[spar])
                pg.sLG.inc(nc.gpsimd.dma_start(
                    wos[spar][:, :, :],
                    T["wo"][:, s * 512:(s + 1) * 512].rearrange(
                        "(kt p) n -> p kt n", p=128)), 16)
                lg_mark = pg.sLG.n
                nc.tensor.wait_ge(pg.sLG.h, lg_mark)
                for cc in range(GRP):
                    ci = g * GRP + cc
                    bank = cc % 2
                    nc.tensor.wait_ge(pg.sDV.h, evf[bank])
                    for kt in range(32):
                        mm = nc.tensor.matmul(pb_m[bank][:, :], oT[:, cc, kt, :],
                                              wos[spar][:, kt, :],
                                              start=(kt == 0), stop=(kt == 31))
                    pg.sPE.inc(mm)
                    wf[spar] = pg.sPE.n
                    pg.sPE.wait(nc.vector)
                    nc.vector.wait_ge(pg.sSG.h, sgf)
                    ev = nc.vector.tensor_copy(o_ev[:, :], pb_m[bank][:, :])
                    pg.sDV.inc(ev)
                    evf[bank] = pg.sDV.n
                    pg.sDV.wait(nc.gpsimd)
                    st = nc.gpsimd.dma_start(
                        T["x1_sc"][ci * C:(ci + 1) * C, s * 512:(s + 1) * 512],
                        o_ev[:, :], accum_op=ALU.add)
                    pg.sSG.inc(st, 16)
                    sgf = pg.sSG.n
                    if tap_wo is not None:
                        pg.sDV.wait(nc.sync)
                        pg.sST.inc(nc.sync.dma_start(
                            tap_wo[ci * C:(ci + 1) * C, s * 512:(s + 1) * 512],
                            o_ev[:, :]), 16)



def _phase_d0_ln(pg, T, n_tok, tap):
    nc = pg.nc
    phase_barrier(pg)
    MYT = n_tok // 2
    MCH = MYT // C
    tap_xn1 = tap("xn1", [MYT, HID], BF16)
    tap_xd0 = tap("xd0", [MCH, 32, 128, C], BF16)

    with ExitStack() as ctx:
        E = ctx.enter_context
        x_t = [E(nc.sbuf_tensor(f"d0_x{i}", [128, HID], F32)) for i in range(2)]
        wrk = E(nc.sbuf_tensor("d0_wrk", [128, HID], F32))
        stat = E(nc.sbuf_tensor("d0_stat", [128, 8], F32))
        xn_bf = E(nc.sbuf_tensor("d0_xnbf", [128, HID], BF16))
        xnT = E(nc.sbuf_tensor("d0_xnT", [128, 32, 128], BF16))
        pb_t = E(nc.psum_tensor("d0_pt", [128, 1024], BF16))

        nc.sync.wait_ge(pg.sSG.h, pg.sSG.n)
        nc.sync.wait_ge(pg.sST.h, pg.sST.n)
        lnf = [0, 0]
        stf = 0
        for ci in range(MCH):
            par = ci % 2
            nc.sync.wait_ge(pg.sDV.h, lnf[par])
            ld = nc.sync.dma_start(x_t[par][:, :], T["x1_sc"][ci * C:(ci + 1) * C, :])
            pg.sLD.inc(ld, 16)
            pg.sLD.wait(nc.scalar)
            pg.sLD.wait(nc.vector)
            _layernorm(pg, x_t[par], xn_bf, wrk, stat)
            lnf[par] = pg.sDV.n
            if tap_xn1 is not None:
                pg.sDV.wait(nc.sync)
                pg.sST.inc(nc.sync.dma_start(tap_xn1[ci * C:(ci + 1) * C, :],
                                             xn_bf[:, :]), 16)
            pg.sDV.wait(nc.tensor)
            nc.vector.wait_ge(pg.sST.h, stf)
            _transpose_batch(pg, [(xn_bf[:, ft * 128:(ft + 1) * 128], xnT[:, ft, :])
                                  for ft in range(32)], pb_t)
            pg.sDV.wait(nc.sync)
            st = nc.sync.dma_start(T["xn1T_sc"][ci].rearrange("ft p t -> p ft t"),
                                   xnT[:, :, :])
            pg.sST.inc(st, 16)
            if tap_xd0 is not None:
                pg.sST.inc(nc.sync.dma_start(
                    tap_xd0[ci].rearrange("ft p t -> p ft t"), xnT[:, :, :]), 16)
            stf = pg.sST.n



def _phase_d1_fwd(pg, T, n_tok, tap):
    nc = pg.nc
    phase_barrier(pg)
    MYT = n_tok // 2
    MCH = MYT // C
    GRP = min(8, MCH)
    NGRP = MCH // GRP

    with ExitStack() as ctx:
        E = ctx.enter_context
        xnT = E(nc.sbuf_tensor("d1_xnT", [128, GRP, 32, 128], BF16))
        dcs = [E(nc.sbuf_tensor(f"d1_dc{i}", [128, 32, 512], BF16)) for i in range(2)]
        dss = [E(nc.sbuf_tensor(f"d1_ds{i}", [128, 32, 512], BF16)) for i in range(2)]
        fr_t = E(nc.sbuf_tensor("d1_fr", [128, 512], F32))
        fi_t = E(nc.sbuf_tensor("d1_fi", [128, 512], F32))
        sq_t = E(nc.sbuf_tensor("d1_sq", [128, 512], F32))
        sq2_t = E(nc.sbuf_tensor("d1_sq2", [128, 512], F32))
        p_t = E(nc.sbuf_tensor("d1_p", [128, 512], BF16))
        fn_t = E(nc.sbuf_tensor("d1_fn", [128, 8], F32))
        pb_r = E(nc.psum_tensor("d1_pr", [128, 512], F32))
        pb_i = E(nc.psum_tensor("d1_pi", [128, 512], F32))
        pb_n = E(nc.psum_tensor("d1_pn", [128, 8], F32))

        tap_fr = tap("frd", [MYT, NSPEC])
        tap_xg = tap("xng", [MCH, 32, 128, C], BF16)
        tap_pp = tap("ppd", [MYT, NSPEC], BF16)
        tap_fn = tap("fnd", [MYT, 8])
        nc.sync.wait_ge(pg.sST.h, pg.sST.n)
        wfd = [0, 0]
        for g in range(NGRP):
            nc.sync.wait_ge(pg.sPE.h, pg.sPE.n)
            for cc in range(GRP):
                ld = nc.sync.dma_start(
                    xnT[:, cc, :, :],
                    T["xn1T_sc"][g * GRP + cc].rearrange("ft p t -> p ft t"))
                pg.sLD.inc(ld, 16)
            pg.sLD.wait(nc.tensor)
            if tap_xg is not None:
                pg.sLD.wait(nc.sync)
                for cc in range(GRP):
                    pg.sST.inc(nc.sync.dma_start(
                        tap_xg[g * GRP + cc].rearrange("ft p t -> p ft t"),
                        xnT[:, cc, :, :]), 16)
            for cc in range(GRP):
                ci = g * GRP + cc
                nc.tensor.wait_ge(pg.sDV.h, pg.d1nf if hasattr(pg, "d1nf") else 0)
                for kt in range(32):
                    mm = nc.tensor.matmul(pb_n[:, 0:1], xnT[:, cc, kt, :],
                                          pg.dnyq_b[:, kt, :],
                                          start=(kt == 0), stop=(kt == 31))
                pg.sPE.inc(mm)
                pg.sPE.wait(nc.vector)
                nc.vector.wait_ge(pg.sST.h, pg.d1nsf if hasattr(pg, "d1nsf") else 0)
                fn = nc.vector.tensor_copy(fn_t[:, 0:1], pb_n[:, 0:1])
                nc.vector.memset(fn_t[:, 1:8], 0.0)
                pg.sDV.inc(fn)
                pg.d1nf = pg.sDV.n
                pg.sDV.wait(nc.sync)
                pg.sST.inc(nc.sync.dma_start(T["fn_sc"][ci * C:(ci + 1) * C, :],
                                             fn_t[:, :]), 16)
                if tap_fn is not None:
                    pg.sST.inc(nc.sync.dma_start(
                        tap_fn[ci * C:(ci + 1) * C, :], fn_t[:, :]), 16)
                pg.d1nsf = pg.sST.n
            for s in range(4):
                spar = s % 2
                nc.gpsimd.wait_ge(pg.sPE.h, wfd[spar])
                pg.sLG.inc(nc.gpsimd.dma_start(
                    dcs[spar][:, :, :],
                    T["tb_dcos"][:, s * 512:(s + 1) * 512].rearrange(
                        "(kt p) n -> p kt n", p=128)), 16)
                pg.sLG.inc(nc.gpsimd.dma_start(
                    dss[spar][:, :, :],
                    T["tb_dsin"][:, s * 512:(s + 1) * 512].rearrange(
                        "(kt p) n -> p kt n", p=128)), 16)
                lg_mark = pg.sLG.n
                nc.tensor.wait_ge(pg.sLG.h, lg_mark)
                for cc in range(GRP):
                    ci = g * GRP + cc
                    nc.tensor.wait_ge(pg.sDV.h, pg.d1ef if hasattr(pg, "d1ef") else 0)
                    for kt in range(32):
                        nc.tensor.matmul(pb_r[:, :], xnT[:, cc, kt, :],
                                         dcs[spar][:, kt, :],
                                         start=(kt == 0), stop=(kt == 31))
                    for kt in range(32):
                        mm = nc.tensor.matmul(pb_i[:, :], xnT[:, cc, kt, :],
                                              dss[spar][:, kt, :],
                                              start=(kt == 0), stop=(kt == 31))
                    pg.sPE.inc(mm)
                    wfd[spar] = pg.sPE.n
                    pg.sPE.wait(nc.vector)
                    nc.vector.wait_ge(pg.sST.h, pg.d1sf if hasattr(pg, "d1sf") else 0)
                    nc.vector.tensor_copy(fr_t[:, :], pb_r[:, :])
                    nc.vector.tensor_scalar(fi_t[:, :], pb_i[:, :], -1.0, None,
                                            ALU.mult)
                    nc.vector.tensor_tensor(sq_t[:, :], fr_t[:, :], fr_t[:, :],
                                            ALU.mult)
                    nc.vector.tensor_tensor(sq2_t[:, :], fi_t[:, :], fi_t[:, :],
                                            ALU.mult)
                    ev = nc.vector.tensor_tensor(p_t[:, :], sq_t[:, :], sq2_t[:, :],
                                                 ALU.add)
                    pg.sDV.inc(ev)
                    pg.d1ef = pg.sDV.n
                    pg.sDV.wait(nc.sync)
                    ssl = slice(s * 512, (s + 1) * 512)
                    tsl = slice(ci * C, (ci + 1) * C)
                    pg.sST.inc(nc.sync.dma_start(T["fr_sc"][tsl, ssl],
                                                 fr_t[:, :]), 16)
                    pg.sST.inc(nc.sync.dma_start(T["fi_sc"][tsl, ssl],
                                                 fi_t[:, :]), 16)
                    pg.sST.inc(nc.sync.dma_start(T["p_sc"][tsl, ssl], p_t[:, :]), 16)
                    if tap_fr is not None:
                        pg.sST.inc(nc.sync.dma_start(tap_fr[tsl, ssl],
                                                     fr_t[:, :]), 16)
                    if tap_pp is not None:
                        pg.sST.inc(nc.sync.dma_start(tap_pp[tsl, ssl],
                                                     p_t[:, :]), 16)
                    pg.d1sf = pg.sST.n



def _phase_d2_corr(pg, T, n_tok, tap):
    nc = pg.nc
    phase_barrier(pg)
    MYT = n_tok // 2
    MCH = MYT // C
    tap_corr = tap("corr", [MYT, HID])
    tap_h = tap("hvec", [MYT, HID], BF16)

    with ExitStack() as ctx:
        E = ctx.enter_context
        icos = E(nc.sbuf_tensor("d2_icos", [128, 16, HID], BF16))
        alt = E(nc.sbuf_tensor("d2_alt", [128, HID], BF16))
        p_ch = E(nc.sbuf_tensor("d2_p", [128, NSPEC], BF16))
        pT = E(nc.sbuf_tensor("d2_pT", [128, 16, 128], BF16))
        fn_t = E(nc.sbuf_tensor("d2_fn", [128, 8], F32))
        corr = E(nc.sbuf_tensor("d2_corr", [128, HID], F32))
        mask = E(nc.sbuf_tensor("d2_mask", [128, HID], F32))
        h_bf = E(nc.sbuf_tensor("d2_h", [128, HID], BF16))
        m8 = E(nc.sbuf_tensor("d2_m8", [128, 8], F32))
        st8 = E(nc.sbuf_tensor("d2_st8", [128, 8], F32))
        e8 = E(nc.sbuf_tensor("d2_e8", [128, 8], F32))
        hT = E(nc.sbuf_tensor("d2_hT", [128, 32, 128], BF16))
        pb_t = E(nc.psum_tensor("d2_pt", [128, 1024], BF16))
        pb_c = [E(nc.psum_tensor(f"d2_pc{i}", [128, 512], F32)) for i in range(2)]

        nc.sync.wait_ge(pg.sST.h, pg.sST.n)
        ld = nc.sync.dma_start(icos[:, :, :],
                               T["tb_icos"][:, :].rearrange("(kt p) t -> p kt t",
                                                            p=128))
        pg.sLD.inc(ld, 16)
        pg.sLD.inc(nc.sync.dma_start(alt[:, :], T["tb_alt"][:, :]), 16)
        ldm = pg.sLD.n

        dvf = 0
        stf = 0
        for ci in range(MCH):
            tsl = slice(ci * C, (ci + 1) * C)
            nc.sync.wait_ge(pg.sPE.h, pg.sPE.n)
            nc.sync.wait_ge(pg.sDV.h, dvf)
            pg.sLD.inc(nc.sync.dma_start(p_ch[:, :], T["p_sc"][tsl, :]), 16)
            pg.sLD.inc(nc.sync.dma_start(fn_t[:, :], T["fn_sc"][tsl, :]), 16)
            pg.sLD.wait(nc.tensor)
            pg.sLD.wait(nc.vector)
            nc.vector.wait_ge(pg.sLD.h, ldm)
            pn = nc.vector.tensor_tensor(fn_t[:, 1:2], fn_t[:, 0:1], fn_t[:, 0:1],
                                         ALU.mult)
            pg.sDV.inc(pn)
            pg.sDV.wait(nc.vector)
            _transpose_batch(pg, [(p_ch[:, i * 128:(i + 1) * 128], pT[:, i, :])
                                  for i in range(16)], pb_t)
            pg.sDV.wait(nc.tensor)
            for s in range(8):
                bank = s % 2
                ssl = slice(s * 512, (s + 1) * 512)
                nc.tensor.wait_ge(pg.sDV.h, pg.d2cf[bank]
                                  if hasattr(pg, "d2cf") else 0)
                for kt in range(16):
                    mm = nc.tensor.matmul(pb_c[bank][:, :], pT[:, kt, :],
                                          icos[:, kt, ssl], start=(kt == 0),
                                          stop=(kt == 15))
                pg.sPE.inc(mm)
                pg.sPE.wait(nc.vector)
                cv = nc.vector.scalar_tensor_tensor(corr[:, ssl], alt[:, ssl],
                                                    fn_t[:, 1:2], pb_c[bank][:, :],
                                                    ALU.mult, ALU.add)
                pg.sDV.inc(cv)
                if not hasattr(pg, "d2cf"):
                    pg.d2cf = [0, 0]
                pg.d2cf[bank] = pg.sDV.n
            if tap_corr is not None:
                pg.sDV.wait(nc.sync)
                pg.sST.inc(nc.sync.dma_start(tap_corr[tsl, :], corr[:, :]), 16)
            m8i = nc.vector.max(out=m8[:, :], in_=corr[:, :])
            pg.sDV.inc(m8i)
            pg.sDV.wait(nc.vector)
            nc.vector.tensor_scalar(st8[:, 0:1], m8[:, 0:1], -1.0, None, ALU.mult)
            nv = nc.vector.tensor_scalar(mask[:, :], corr[:, :], m8[:, 7:8], None,
                                         ALU.is_ge)
            pg.sDV.inc(nv)
            pg.sDV.wait(nc.scalar)
            ac = nc.scalar.activation(e8[:, :], m8[:, :], AF.Exp,
                                      bias=st8[:, 0:1], scale=1.0,
                                      accum_out=st8[:, 2:3])
            pg.sAC.inc(ac)
            ac2 = nc.scalar.activation(corr[:, :], corr[:, :], AF.Exp,
                                       bias=st8[:, 0:1], scale=1.0)
            pg.sAC.inc(ac2)
            pg.sAC.wait(nc.vector)
            rc = nc.vector.reciprocal(st8[:, 3:4], st8[:, 2:3])
            pg.sDV.inc(rc)
            pg.sDV.wait(nc.vector)
            hv = nc.vector.scalar_tensor_tensor(h_bf[:, :], corr[:, :],
                                                st8[:, 3:4], mask[:, :],
                                                ALU.mult, ALU.mult)
            pg.sDV.inc(hv)
            if tap_h is not None:
                pg.sDV.wait(nc.sync)
                pg.sST.inc(nc.sync.dma_start(tap_h[tsl, :], h_bf[:, :]), 16)
            pg.sDV.wait(nc.tensor)
            nc.vector.wait_ge(pg.sST.h, stf)
            _transpose_batch(pg, [(h_bf[:, i * 128:(i + 1) * 128], hT[:, i, :])
                                  for i in range(32)], pb_t)
            dvf = pg.sDV.n
            pg.sDV.wait(nc.sync)
            st = nc.sync.dma_start(T["hT_sc"][ci].rearrange("ft p t -> p ft t"),
                                   hT[:, :, :])
            pg.sST.inc(st, 16)
            stf = pg.sST.n



D3_MODE = "full"


def _phase_d3_gprod(pg, T, n_tok, tap):
    nc = pg.nc
    phase_barrier(pg)
    MYT = n_tok // 2
    MCH = MYT // C
    GRP = min(8, MCH)
    NGRP = MCH // GRP

    with ExitStack() as ctx:
        E = ctx.enter_context
        hT = E(nc.sbuf_tensor("d3_hT", [128, GRP, 32, 128], BF16))
        dcs = [E(nc.sbuf_tensor(f"d3_dc{i}", [128, 32, 512], BF16)) for i in range(2)]
        dss = [E(nc.sbuf_tensor(f"d3_ds{i}", [128, 32, 512], BF16)) for i in range(2)]
        fr_t = E(nc.sbuf_tensor("d3_fr", [128, 512], F32))
        fi_t = E(nc.sbuf_tensor("d3_fi", [128, 512], F32))
        t1 = E(nc.sbuf_tensor("d3_t1", [128, 512], F32))
        t2 = E(nc.sbuf_tensor("d3_t2", [128, 512], F32))
        r_t = E(nc.sbuf_tensor("d3_r", [128, 512], BF16))
        i_t = E(nc.sbuf_tensor("d3_i", [128, 512], BF16))
        fn_t = E(nc.sbuf_tensor("d3_fn", [128, 8], F32))
        hn_t = E(nc.sbuf_tensor("d3_hn", [128, 8], F32))
        pb_r = E(nc.psum_tensor("d3_pr", [128, 512], F32))
        pb_i = E(nc.psum_tensor("d3_pi", [128, 512], F32))
        pb_n = E(nc.psum_tensor("d3_pn", [128, 8], F32))

        nc.sync.wait_ge(pg.sST.h, pg.sST.n)
        wfd = [0, 0]
        for g in range(NGRP):
            nc.sync.wait_ge(pg.sPE.h, pg.sPE.n)
            for cc in range(GRP):
                ld = nc.sync.dma_start(
                    hT[:, cc, :, :],
                    T["hT_sc"][g * GRP + cc].rearrange("ft p t -> p ft t"))
                pg.sLD.inc(ld, 16)
            pg.sLD.wait(nc.tensor)
            for cc in range(GRP):
                if D3_MODE == "strips":
                    break
                ci = g * GRP + cc
                tsl = slice(ci * C, (ci + 1) * C)
                nc.sync.wait_ge(pg.sDV.h, pg.d3nf if hasattr(pg, "d3nf") else 0)
                pg.sLD.inc(nc.sync.dma_start(fn_t[:, :], T["fn_sc"][tsl, :]), 16)
                ldm = pg.sLD.n
                nc.tensor.wait_ge(pg.sDV.h, pg.d3nf if hasattr(pg, "d3nf") else 0)
                for kt in range(32):
                    mm = nc.tensor.matmul(pb_n[:, 0:1], hT[:, cc, kt, :],
                                          pg.dnyq_b[:, kt, :],
                                          start=(kt == 0), stop=(kt == 31))
                pg.sPE.inc(mm)
                pg.sPE.wait(nc.vector)
                nc.vector.wait_ge(pg.sLD.h, ldm)
                nc.vector.wait_ge(pg.sST.h, pg.d3nsf if hasattr(pg, "d3nsf") else 0)
                nc.vector.tensor_copy(hn_t[:, 0:8], fn_t[:, 0:8])
                hn = nc.vector.tensor_tensor(hn_t[:, 0:1], fn_t[:, 0:1],
                                             pb_n[:, 0:1], ALU.mult)
                pg.sDV.inc(hn)
                pg.d3nf = pg.sDV.n
                pg.sDV.wait(nc.sync)
                pg.sST.inc(nc.sync.dma_start(T["rn_sc"][tsl, :], hn_t[:, :]), 16)
                pg.d3nsf = pg.sST.n
            for s in range(4):
                if D3_MODE == "nyq":
                    break
                spar = s % 2
                ssl = slice(s * 512, (s + 1) * 512)
                nc.gpsimd.wait_ge(pg.sPE.h, wfd[spar])
                pg.sLG.inc(nc.gpsimd.dma_start(
                    dcs[spar][:, :, :],
                    T["tb_dcos"][:, ssl].rearrange("(kt p) n -> p kt n", p=128)), 16)
                pg.sLG.inc(nc.gpsimd.dma_start(
                    dss[spar][:, :, :],
                    T["tb_dsin"][:, ssl].rearrange("(kt p) n -> p kt n", p=128)), 16)
                lg_mark = pg.sLG.n
                nc.tensor.wait_ge(pg.sLG.h, lg_mark)
                for cc in range(GRP):
                    ci = g * GRP + cc
                    tsl = slice(ci * C, (ci + 1) * C)
                    nc.sync.wait_ge(pg.sDV.h, pg.d3ef if hasattr(pg, "d3ef") else 0)
                    pg.sLD.inc(nc.sync.dma_start(fr_t[:, :], T["fr_sc"][tsl, ssl]),
                               16)
                    pg.sLD.inc(nc.sync.dma_start(fi_t[:, :], T["fi_sc"][tsl, ssl]),
                               16)
                    ldm = pg.sLD.n
                    nc.tensor.wait_ge(pg.sDV.h, pg.d3ef if hasattr(pg, "d3ef") else 0)
                    for kt in range(32):
                        nc.tensor.matmul(pb_r[:, :], hT[:, cc, kt, :],
                                         dcs[spar][:, kt, :],
                                         start=(kt == 0), stop=(kt == 31))
                    for kt in range(32):
                        mm = nc.tensor.matmul(pb_i[:, :], hT[:, cc, kt, :],
                                              dss[spar][:, kt, :],
                                              start=(kt == 0), stop=(kt == 31))
                    pg.sPE.inc(mm)
                    pg.sPE.wait(nc.vector)
                    nc.vector.wait_ge(pg.sLD.h, ldm)
                    nc.vector.wait_ge(pg.sST.h, pg.d3sf if hasattr(pg, "d3sf") else 0)
                    nc.vector.tensor_tensor(t1[:, :], fr_t[:, :], pb_r[:, :],
                                            ALU.mult)
                    nc.vector.tensor_tensor(t2[:, :], fi_t[:, :], pb_i[:, :],
                                            ALU.mult)
                    nc.vector.tensor_tensor(r_t[:, :], t1[:, :], t2[:, :],
                                            ALU.subtract)
                    nc.vector.tensor_tensor(t1[:, :], fr_t[:, :], pb_i[:, :],
                                            ALU.mult)
                    nc.vector.tensor_tensor(t2[:, :], fi_t[:, :], pb_r[:, :],
                                            ALU.mult)
                    ev = nc.vector.tensor_tensor(i_t[:, :], t1[:, :], t2[:, :],
                                                 ALU.add)
                    pg.sDV.inc(ev)
                    pg.d3ef = pg.sDV.n
                    pg.sDV.wait(nc.sync)
                    pg.sST.inc(nc.sync.dma_start(T["r_sc"][tsl, ssl], r_t[:, :]), 16)
                    pg.sST.inc(nc.sync.dma_start(T["i_sc"][tsl, ssl], i_t[:, :]), 16)
                    pg.d3sf = pg.sST.n



def _phase_d4_inv(pg, T, n_tok, tap):
    nc = pg.nc
    phase_barrier(pg)
    MYT = n_tok // 2
    MCH = MYT // C
    tap_a1 = tap("attn1", [MYT, HID])

    with ExitStack() as ctx:
        E = ctx.enter_context
        icos = E(nc.sbuf_tensor("d4_icos", [128, 16, NSPEC], BF16))
        isin = E(nc.sbuf_tensor("d4_isin", [128, 16, NSPEC], BF16))
        alt = E(nc.sbuf_tensor("d4_alt", [128, HID], BF16))
        r_ch = E(nc.sbuf_tensor("d4_r", [128, NSPEC], BF16))
        i_ch = E(nc.sbuf_tensor("d4_i", [128, NSPEC], BF16))
        rT = E(nc.sbuf_tensor("d4_rT", [128, 16, 128], BF16))
        iT = E(nc.sbuf_tensor("d4_iT", [128, 16, 128], BF16))
        rn_t = E(nc.sbuf_tensor("d4_rn", [128, 8], F32))
        x1_t = E(nc.sbuf_tensor("d4_x1", [128, NSPEC], F32))
        o_t = E(nc.sbuf_tensor("d4_o", [128, 512], F32))
        pb_t = E(nc.psum_tensor("d4_pt", [128, 1024], BF16))
        pb_m = [E(nc.psum_tensor(f"d4_pm{i}", [128, 512], F32)) for i in range(2)]

        nc.sync.wait_ge(pg.sST.h, pg.sST.n)
        for half in range(2):
            hssl = slice(half * NSPEC, (half + 1) * NSPEC)
            nc.sync.wait_ge(pg.sPE.h, pg.sPE.n)
            nc.sync.wait_ge(pg.sDV.h, pg.sDV.n)
            ld = nc.sync.dma_start(
                icos[:, :, :],
                T["tb_icos"][:, hssl].rearrange("(kt p) t -> p kt t", p=128))
            pg.sLD.inc(ld, 16)
            ld = nc.sync.dma_start(
                isin[:, :, :],
                T["tb_isin"][:, hssl].rearrange("(kt p) t -> p kt t", p=128))
            pg.sLD.inc(ld, 16)
            pg.sLD.inc(nc.sync.dma_start(alt[:, :], T["tb_alt"][:, :]), 16)
            ldtab = pg.sLD.n
            dvf = 0
            for ci in range(MCH):
                tsl = slice(ci * C, (ci + 1) * C)
                nc.sync.wait_ge(pg.sPE.h, pg.sPE.n)
                nc.sync.wait_ge(pg.sDV.h, dvf)
                pg.sLD.inc(nc.sync.dma_start(r_ch[:, :], T["r_sc"][tsl, :]), 16)
                pg.sLD.inc(nc.sync.dma_start(i_ch[:, :], T["i_sc"][tsl, :]), 16)
                pg.sLD.inc(nc.sync.dma_start(rn_t[:, :], T["rn_sc"][tsl, :]), 16)
                pg.sLD.inc(nc.sync.dma_start(x1_t[:, :], T["x1_sc"][tsl, hssl]), 16)
                pg.sLD.wait(nc.tensor)
                pg.sLD.wait(nc.vector)
                nc.vector.wait_ge(pg.sLD.h, ldtab)
                _transpose_batch(pg, [(r_ch[:, i * 128:(i + 1) * 128], rT[:, i, :])
                                      for i in range(16)]
                                 + [(i_ch[:, i * 128:(i + 1) * 128], iT[:, i, :])
                                    for i in range(16)], pb_t)
                dvf = pg.sDV.n
                pg.sDV.wait(nc.tensor)
                for s in range(4):
                    bank = s % 2
                    ssl = slice(s * 512, (s + 1) * 512)
                    osl = slice(half * NSPEC + s * 512, half * NSPEC + (s + 1) * 512)
                    nc.tensor.wait_ge(pg.sDV.h, pg.d4f[bank]
                                      if hasattr(pg, "d4f") else 0)
                    for kt in range(16):
                        nc.tensor.matmul(pb_m[bank][:, :], rT[:, kt, :],
                                         icos[:, kt, ssl], start=(kt == 0),
                                         stop=False)
                    for kt in range(16):
                        mm = nc.tensor.matmul(pb_m[bank][:, :], iT[:, kt, :],
                                              isin[:, kt, ssl],
                                              start=False, stop=(kt == 15))
                    pg.sPE.inc(mm)
                    pg.sPE.wait(nc.vector)
                    nc.vector.wait_ge(pg.sST.h, pg.d4sf
                                      if hasattr(pg, "d4sf") else 0)
                    nc.vector.scalar_tensor_tensor(o_t[:, :], alt[:, osl],
                                                   rn_t[:, 0:1], pb_m[bank][:, :],
                                                   ALU.mult, ALU.add)
                    ev = nc.vector.tensor_tensor(o_t[:, :], o_t[:, :],
                                                 x1_t[:, ssl], ALU.add)
                    pg.sDV.inc(ev)
                    if not hasattr(pg, "d4f"):
                        pg.d4f = [0, 0]
                    pg.d4f[bank] = pg.sDV.n
                    pg.sDV.wait(nc.sync)
                    st = nc.sync.dma_start(T["out_my"][tsl, osl], o_t[:, :])
                    pg.sST.inc(st, 16)
                    pg.d4sf = pg.sST.n
                    if tap_a1 is not None:
                        pg.sST.inc(nc.sync.dma_start(tap_a1[tsl, osl],
                                                     o_t[:, :]), 16)



def _phase_d5_ffn1(pg, T, n_tok, tap):
    nc = pg.nc
    phase_barrier(pg)
    MYT = n_tok // 2
    MCH = MYT // C
    TSW = min(512, MYT)
    NTS = MYT // TSW
    CPT = TSW // C
    tap_f1 = tap("ffn1", [MYT, HID])

    with ExitStack() as ctx:
        E = ctx.enter_context
        u1b = E(nc.sbuf_tensor("d5_u1", [128, 32, RANK], BF16))
        xns = E(nc.sbuf_tensor("d5_xns", [128, 32, TSW], BF16))
        pT = E(nc.sbuf_tensor("d5_pT", [128, 8, MYT], BF16))
        pb_m = [E(nc.psum_tensor(f"d5a_pm{i}", [128, TSW], F32)) for i in range(2)]

        nc.gpsimd.wait_ge(pg.sST.h, pg.sST.n)
        ldw = nc.gpsimd.dma_start(
            u1b[:, :, :], T["u1w"][:, :].rearrange("(kt p) r -> p kt r", p=128))
        pg.sLG.inc(ldw, 16)
        dvf = [0, 0]
        for ts in range(NTS):
            nc.sync.wait_ge(pg.sPE.h, pg.sPE.n)
            for cc in range(CPT):
                ld = nc.sync.dma_start(
                    xns[:, :, cc * C:(cc + 1) * C],
                    T["xn1T_sc"][ts * CPT + cc].rearrange("ft p t -> p ft t"))
                pg.sLD.inc(ld, 16)
            pg.sLD.wait(nc.tensor)
            pg.sLG.wait(nc.tensor)
            for rt in range(8):
                bank = rt % 2
                nc.tensor.wait_ge(pg.sDV.h, dvf[bank])
                for kt in range(32):
                    mm = nc.tensor.matmul(pb_m[bank][:, :],
                                          u1b[:, kt, rt * 128:(rt + 1) * 128],
                                          xns[:, kt, :], start=(kt == 0),
                                          stop=(kt == 31))
                pg.sPE.inc(mm)
                pg.sPE.wait(nc.vector)
                ev = nc.vector.tensor_copy(pT[:, rt, ts * TSW:(ts + 1) * TSW],
                                           pb_m[bank][:, :])
                pg.sDV.inc(ev)
                dvf[bank] = pg.sDV.n

        phase_barrier(pg)
        with ExitStack() as ctx2:
            E2 = ctx2.enter_context
            v1g = [E2(nc.sbuf_tensor(f"d5_v1g{i}", [128, 8, 128], BF16))
                   for i in range(2)]
            v1u = [E2(nc.sbuf_tensor(f"d5_v1u{i}", [128, 8, 128], BF16))
                   for i in range(2)]
            silu_t = E2(nc.sbuf_tensor("d5_silu", [128, TSW], F32))
            act_t = E2(nc.sbuf_tensor("d5_act", [128, TSW], BF16))
            pb_g = E2(nc.psum_tensor("d5_pg", [128, TSW], F32))
            pb_up = E2(nc.psum_tensor("d5_pu", [128, TSW], F32))
            pg.b1r_b = E2(nc.sbuf_tensor("b1r_b", [1, 2 * INTER], BF16))
            pg.sLG.inc(nc.gpsimd.dma_start(pg.b1r_b[:, :], T["b1r"][:, :]), 16)
            pg.sLG.wait(nc.tensor)

            wf = [0, 0]
            acf = 0
            dv2 = 0
            stf = 0
            for gi in range(NHT):
                spar = gi % 2
                nc.gpsimd.wait_ge(pg.sPE.h, wf[spar])
                pg.sLG.inc(nc.gpsimd.dma_start(
                    v1g[spar][:, :, :],
                    T["v1w"][:, gi * 128:(gi + 1) * 128].rearrange(
                        "(kt p) n -> p kt n", p=128)), 16)
                pg.sLG.inc(nc.gpsimd.dma_start(
                    v1u[spar][:, :, :],
                    T["v1w"][:, INTER + gi * 128:INTER + (gi + 1) * 128].rearrange(
                        "(kt p) n -> p kt n", p=128)), 16)
                lg_mark = pg.sLG.n
                nc.tensor.wait_ge(pg.sLG.h, lg_mark)
                for ts in range(NTS):
                    tsl = slice(ts * TSW, (ts + 1) * TSW)
                    nc.tensor.wait_ge(pg.sAC.h, acf)
                    nc.tensor.wait_ge(pg.sDV.h, dv2)
                    for kt in range(8):
                        nc.tensor.matmul(pb_g[:, :], v1g[spar][:, kt, :],
                                         pT[:, kt, tsl], start=(kt == 0),
                                         stop=False)
                    nc.tensor.matmul(pb_g[:, :],
                                     pg.b1r_b[0:1, gi * 128:(gi + 1) * 128],
                                     pg.ones_bf[0:1, 0:TSW], start=False, stop=True)
                    for kt in range(8):
                        nc.tensor.matmul(pb_up[:, :], v1u[spar][:, kt, :],
                                         pT[:, kt, tsl], start=(kt == 0),
                                         stop=False)
                    mm = nc.tensor.matmul(
                        pb_up[:, :],
                        pg.b1r_b[0:1, INTER + gi * 128:INTER + (gi + 1) * 128],
                        pg.ones_bf[0:1, 0:TSW], start=False, stop=True)
                    pg.sPE.inc(mm)
                    wf[spar] = pg.sPE.n
                    pg.sPE.wait(nc.scalar)
                    nc.scalar.wait_ge(pg.sDV.h, dv2)
                    sl = nc.scalar.activation(silu_t[:, :], pb_g[:, :], AF.Silu,
                                              bias=0.0, scale=1.0)
                    pg.sAC.inc(sl)
                    acf = pg.sAC.n
                    pg.sAC.wait(nc.vector)
                    nc.vector.wait_ge(pg.sST.h, stf)
                    av = nc.vector.tensor_tensor(act_t[:, :], silu_t[:, :],
                                                 pb_up[:, :], ALU.mult)
                    pg.sDV.inc(av)
                    dv2 = pg.sDV.n
                    pg.sDV.wait(nc.sync)
                    st = nc.sync.dma_start(T["act1T_sc"][gi, ts], act_t[:, :])
                    pg.sST.inc(st, 16)
                    stf = pg.sST.n

    phase_barrier(pg)
    with ExitStack() as zctx:
      ZT = zctx.enter_context(nc.sbuf_tensor("d5_ZT", [128, 8, MYT], BF16))
      with ExitStack() as ctx:
        E = ctx.enter_context
        a1s = E(nc.sbuf_tensor("d5_a1s", [128, NHT, TSW], BF16))
        u2b = [E(nc.sbuf_tensor(f"d5_u2{i}", [128, NHT, 128], BF16))
               for i in range(2)]
        pb_m = [E(nc.psum_tensor(f"d5c_pm{i}", [128, TSW], F32)) for i in range(2)]

        nc.sync.wait_ge(pg.sST.h, pg.sST.n)
        wf = [0, 0]
        dvf = [0, 0]
        for ts in range(NTS):
            nc.sync.wait_ge(pg.sPE.h, pg.sPE.n)
            ld = nc.sync.dma_start(a1s[:, :, :],
                                   T["act1T_sc"][:, ts].rearrange("h p t -> p h t"))
            pg.sLD.inc(ld, 16)
            for rt in range(8):
                spar = rt % 2
                bank = rt % 2
                nc.gpsimd.wait_ge(pg.sPE.h, wf[spar])
                ldw = nc.gpsimd.dma_start(
                    u2b[spar][:, :, :],
                    T["u2w"][:, rt * 128:(rt + 1) * 128].rearrange(
                        "(kt p) n -> p kt n", p=128))
                pg.sLG.inc(ldw, 16)
                pg.sLD.wait(nc.tensor)
                pg.sLG.wait(nc.tensor)
                nc.tensor.wait_ge(pg.sDV.h, dvf[bank])
                for kt in range(NHT):
                    mm = nc.tensor.matmul(pb_m[bank][:, :], u2b[spar][:, kt, :],
                                          a1s[:, kt, :], start=(kt == 0),
                                          stop=(kt == NHT - 1))
                pg.sPE.inc(mm)
                wf[spar] = pg.sPE.n
                pg.sPE.wait(nc.vector)
                ev = nc.vector.tensor_copy(ZT[:, rt, ts * TSW:(ts + 1) * TSW],
                                           pb_m[bank][:, :])
                pg.sDV.inc(ev)
                dvf[bank] = pg.sDV.n

      phase_barrier(pg)
      if True:
        with ExitStack() as ctx2:
            E2 = ctx2.enter_context
            v2b = E2(nc.sbuf_tensor("d5_v2", [128, 8, HID], BF16))
            o_t = E2(nc.sbuf_tensor("d5_o", [128, 512], F32))
            pb_o = [E2(nc.psum_tensor(f"d5d_pm{i}", [128, 512], F32))
                    for i in range(2)]
            pg.b2r_b = E2(nc.sbuf_tensor("b2r_b", [1, HID], BF16))
            pg.sLG.inc(nc.gpsimd.dma_start(pg.b2r_b[:, :], T["b2r"][:, :]), 16)

            ldw = nc.gpsimd.dma_start(
                v2b[:, :, :], T["v2w"][:, :].rearrange("(kt p) n -> p kt n", p=128))
            pg.sLG.inc(ldw, 16)
            pg.sLG.wait(nc.tensor)
            nc.gpsimd.wait_ge(pg.sST.h, pg.sST.n)
            dvf2 = [0, 0]
            sgf = 0
            for ci in range(MCH):
                tsl = slice(ci * C, (ci + 1) * C)
                for s in range(8):
                    bank = s % 2
                    ssl = slice(s * 512, (s + 1) * 512)
                    nc.tensor.wait_ge(pg.sDV.h, dvf2[bank])
                    for kt in range(8):
                        nc.tensor.matmul(pb_o[bank][:, :], ZT[:, kt, tsl],
                                         v2b[:, kt, ssl], start=(kt == 0),
                                         stop=False)
                    mm = nc.tensor.matmul(pb_o[bank][:, :], pg.ones_bf[0:1, 0:128],
                                          pg.b2r_b[0:1, ssl], start=False, stop=True)
                    pg.sPE.inc(mm)
                    pg.sPE.wait(nc.vector)
                    nc.vector.wait_ge(pg.sSG.h, sgf)
                    ev = nc.vector.tensor_copy(o_t[:, :], pb_o[bank][:, :])
                    pg.sDV.inc(ev)
                    dvf2[bank] = pg.sDV.n
                    pg.sDV.wait(nc.gpsimd)
                    st = nc.gpsimd.dma_start(T["out_my"][tsl, ssl], o_t[:, :],
                                             accum_op=ALU.add)
                    pg.sSG.inc(st, 16)
                    sgf = pg.sSG.n
                    if tap_f1 is not None:
                        pg.sDV.wait(nc.sync)
                        pg.sST.inc(nc.sync.dma_start(tap_f1[tsl, ssl],
                                                     o_t[:, :]), 16)



_CACHE = {}
_LAST_RES = None


def kernel(**inputs):
    n_tok = int(inputs["L0_attn_hidden_states"].shape[1])
    key = ("full", n_tok)
    if key not in _CACHE:
        _CACHE[key] = build_program(n_tok=n_tok)
    nc, T, taps = _CACHE[key]

    x = np.ascontiguousarray(inputs["L0_attn_hidden_states"], dtype=np.float32)
    tb = make_tables(n_tok)
    MYT = n_tok // 2
    ct = np.ascontiguousarray

    shared = {
        "wo": ct(inputs["L0_attn_Wo"]),
        "w1qT": ct(inputs["L0_ffn_w1q"].T),
        "w3qT": ct(inputs["L0_ffn_w3q"].T),
        "w2qT": ct(inputs["L0_ffn_w2q"].T),
        "a1T": ct(inputs["L0_ffn_a1"].T),
        "a3T": ct(inputs["L0_ffn_a3"].T),
        "a2T": ct(inputs["L0_ffn_a2"].T),
        "b1T": ct(inputs["L0_ffn_b1"].T),
        "b3T": ct(inputs["L0_ffn_b3"].T),
        "b2T": ct(inputs["L0_ffn_b2"].T),
        "u1w": ct(inputs["L1_ffn_U1"]),
        "v1w": ct(inputs["L1_ffn_V1"]),
        "u2w": ct(inputs["L1_ffn_U2"]),
        "v2w": ct(inputs["L1_ffn_V2"]),
        "b1r": ct(inputs["L1_ffn_b1"].reshape(1, -1)),
        "b2r": ct(inputs["L1_ffn_b2"].reshape(1, -1)),
    }
    shared.update(tb)

    in_maps = []
    for c in range(8):
        b, g = c // 2, c % 2
        m = dict(shared)
        m["x_b"] = ct(x[b])
        m["x_my"] = ct(x[b, g * MYT:(g + 1) * MYT])
        m["wq"] = ct(inputs["L0_attn_Wq"][:, g * 2048:(g + 1) * 2048])
        m["wk"] = ct(inputs["L0_attn_Wk"][:, g * 512:(g + 1) * 512])
        m["wv"] = ct(inputs["L0_attn_Wv"][:, g * 512:(g + 1) * 512])
        in_maps.append(m)

    res = run_bass_kernel_spmd(nc, in_maps, core_ids=list(range(8)))
    global _LAST_RES
    _LAST_RES = res
    out = np.empty((4, n_tok, HID), dtype=np.float32)
    for c in range(8):
        b, g = c // 2, c % 2
        out[b, g * MYT:(g + 1) * MYT] = res.results[c]["out_my"]
    return out


# version 03
# speedup vs baseline: 1.0548x; 1.0548x over previous
import math
from contextlib import ExitStack

import numpy as np
import ml_dtypes

import concourse.bass as bass
import concourse.mybir as mybir
from concourse.bass_utils import run_bass_kernel_spmd

F32 = mybir.dt.float32
BF16 = mybir.dt.bfloat16
AF = mybir.ActivationFunctionType
ALU = mybir.AluOpType
AX = mybir.AxisListType
BF = ml_dtypes.bfloat16

HID = 4096
H, KVH, D, W = 32, 8, 128, 64
THETA = 10000.0
C = 128
FFN_H = 11008
RANK = 1024
INTER = 11008
NSPEC = 2048
EPS = 1e-5
NEG = -1.0e30
RSD = 1.0 / math.sqrt(D)
NHT = FFN_H // 128


def make_tables(n_tok):
    t = {}
    t["tb_ident"] = np.eye(128, dtype=BF)
    iu = np.arange(C)
    t["tb_utri"] = (iu[:, None] <= iu[None, :]).astype(np.float32)
    a = np.arange(C)[:, None]
    wp = np.arange(C + W)[None, :]
    base = (wp >= a) & (wp <= a + W)
    m = np.where(base, 0.0, NEG).astype(np.float32)
    t["tb_swam"] = m
    m0 = m.copy()
    m0[:, :W] = NEG
    t["tb_swam0"] = m0
    half = np.arange(D // 2, dtype=np.float64)
    inv = 1.0 / THETA ** (2.0 * half / D)
    pos = np.arange(n_tok, dtype=np.float64)
    fr = pos[:, None] * inv[None, :]
    emb = np.concatenate([fr, fr], axis=-1)
    cos, sin = np.cos(emb), np.sin(emb)
    t["tb_rcq"] = (cos * RSD).astype(BF)
    t["tb_rsq"] = (sin * RSD).astype(BF)
    t["tb_rck"] = cos.astype(BF)
    t["tb_rsk"] = sin.astype(BF)
    L = HID
    kk = np.arange(NSPEC, dtype=np.float64)
    tt = np.arange(L, dtype=np.float64)
    ang = 2.0 * np.pi * np.outer(tt, kk) / L
    t["tb_dcos"] = np.cos(ang).astype(BF)
    t["tb_dsin"] = np.sin(ang).astype(BF)
    t["tb_dnyq"] = np.ascontiguousarray(np.cos(np.pi * tt).astype(BF).reshape(L // 128, 128).T)
    ck = np.full(NSPEC, 2.0 / L)
    ck[0] = 1.0 / L
    angi = 2.0 * np.pi * np.outer(kk, tt) / L
    t["tb_icos"] = (ck[:, None] * np.cos(angi)).astype(BF)
    t["tb_isin"] = (-ck[:, None] * np.sin(angi)).astype(BF)
    t["tb_alt"] = np.ascontiguousarray(
        np.broadcast_to(((-1.0) ** tt / L).astype(BF), (128, L)))
    return t


class Sem:
    def __init__(self, handle):
        self.h = handle
        self.n = 0

    def inc(self, instr, k=1):
        instr.then_inc(self.h, k)
        self.n += k
        return instr

    def wait(self, eng, val=None):
        eng.wait_ge(self.h, self.n if val is None else val)


class P:
    pass


def _mk_prog(nc, ctx):
    pg = P()
    pg.nc = nc
    pg.ctx = ctx
    E = ctx.enter_context
    for nm in ("sLD", "sLG", "sPE", "sDV", "sAC", "sST", "sSG", "sCC"):
        setattr(pg, nm, Sem(E(nc.semaphore(nm))))
    pg.par = nc.partition_id() % 2
    return pg


def sbuf(pg, name, shape, dtype):
    return pg.ctx.enter_context(pg.nc.sbuf_tensor(name, shape, dtype))


def br_load(pg, q, sem, dst, src_lo, src_hi):
    with q.If(pg.par):
        q.dma_start(dst, src_hi).then_inc(sem.h, 16)
    with q.Else():
        q.dma_start(dst, src_lo).then_inc(sem.h, 16)
    sem.n += 16


def phase_barrier(pg):
    nc = pg.nc
    for eng in (nc.sync, nc.gpsimd, nc.vector, nc.scalar, nc.tensor):
        for s in (pg.sPE, pg.sDV, pg.sAC, pg.sST, pg.sSG, pg.sLD, pg.sLG):
            eng.wait_ge(s.h, s.n)



def build_program(n_tok=4096, debug_taps=(), single_core=False, stop_after=None):
    nc = bass.Bass("TRN2", target_bir_lowering=False)
    NCH = n_tok // C
    MYT = n_tok // 2
    MCH = MYT // C
    QCOLS = (H // 2) * D
    KCOLS = (KVH // 2) * D
    TSW = min(512, MYT)
    NTS = MYT // TSW

    dt = nc.dram_tensor
    T = {}
    for nm, shp in [
        ("x_b", [n_tok, HID]), ("wq", [HID, QCOLS]), ("wk", [HID, KCOLS]),
        ("wv", [HID, KCOLS]), ("wo", [HID, HID]),
        ("w1qT", [1024, FFN_H]), ("w3qT", [1024, FFN_H]), ("w2qT", [2752, HID]),
        ("a1T", [HID, 2]), ("a3T", [HID, 2]), ("a2T", [FFN_H, 2]),
        ("b1T", [2, FFN_H]), ("b3T", [2, FFN_H]), ("b2T", [2, HID]),
        ("u1w", [HID, RANK]), ("v1w", [RANK, 2 * INTER]), ("u2w", [INTER, RANK]),
        ("v2w", [RANK, HID]), ("b1r", [1, 2 * INTER]), ("b2r", [1, HID]),
        ("x_my", [MYT, HID]),
        ("tb_utri", [C, C]), ("tb_swam", [C, C + W]), ("tb_swam0", [C, C + W]),
    ]:
        T[nm] = dt(nm, shp, F32, kind="ExternalInput")
    for nm, shp in [
        ("tb_ident", [128, 128]), ("tb_alt", [128, HID]),
        ("tb_rcq", [n_tok, D]), ("tb_rsq", [n_tok, D]),
        ("tb_rck", [n_tok, D]), ("tb_rsk", [n_tok, D]),
        ("tb_dcos", [HID, NSPEC]), ("tb_dsin", [HID, NSPEC]), ("tb_dnyq", [128, HID // 128]),
        ("tb_icos", [NSPEC, HID]), ("tb_isin", [NSPEC, HID]),
    ]:
        T[nm] = dt(nm, shp, BF16, kind="ExternalInput")

    T["out_my"] = dt("out_my", [MYT, HID], F32, kind="ExternalOutput")

    T["xnT_sc"] = dt("xnT_sc", [NCH, 32, 128, C], BF16)
    T["qkv_sc"] = dt("qkv_sc", [NCH, C, QCOLS + 2 * KCOLS], F32)
    T["o_mine"] = dt("o_mine", [n_tok, QCOLS], BF16)
    T["o_all"] = dt("o_all", [2 * n_tok, QCOLS], BF16)
    T["x1_sc"] = dt("x1_sc", [MYT, HID], F32)
    T["xn1T_sc"] = dt("xn1T_sc", [MCH, 32, 128, C], BF16)
    T["fr_sc"] = dt("fr_sc", [MYT, NSPEC], F32)
    T["fi_sc"] = dt("fi_sc", [MYT, NSPEC], F32)
    T["fn_sc"] = dt("fn_sc", [MYT, 8], F32)
    T["p_sc"] = dt("p_sc", [MYT, NSPEC], BF16)
    T["hT_sc"] = dt("hT_sc", [MCH, 32, 128, C], BF16)
    T["r_sc"] = dt("r_sc", [MYT, NSPEC], BF16)
    T["i_sc"] = dt("i_sc", [MYT, NSPEC], BF16)
    T["rn_sc"] = dt("rn_sc", [MYT, 8], F32)
    T["act_sc"] = dt("act_sc", [MCH, 128, FFN_H], BF16)
    T["act1T_sc"] = dt("act1T_sc", [NHT, NTS, 128, TSW], BF16)

    taps = {}

    def tap(name, shape, dtype=F32):
        if name in debug_taps:
            taps[name] = nc.dram_tensor("tap_" + name, shape, dtype,
                                        kind="ExternalOutput")
            return taps[name]
        return None

    with ExitStack() as ctx:
        pg = _mk_prog(nc, ctx)
        _consts(pg, T)
        _phase_a(pg, T, n_tok, tap)
        _phase_b(pg, T, n_tok, tap)
        if single_core:
            nc.gpsimd.wait_ge(pg.sST.h, pg.sST.n)
            pg.sSG.inc(nc.gpsimd.dma_start(T["o_all"][0:n_tok, :],
                                           T["o_mine"][:, :]), 16)
            pg.sSG.inc(nc.gpsimd.dma_start(T["o_all"][n_tok:2 * n_tok, :],
                                           T["o_mine"][:, :]), 16)
            pg.cc_wait = (pg.sSG, pg.sSG.n)
        else:
            nc.gpsimd.wait_ge(pg.sST.h, pg.sST.n)
            cc = nc.gpsimd.collective_compute(
                "AllGather", ALU.bypass,
                replica_groups=[[0, 1], [2, 3], [4, 5], [6, 7]],
                ins=[T["o_mine"].ap().opt()],
                outs=[T["o_all"].ap().opt()],
            )
            pg.sCC.inc(cc, 1)
            pg.cc_wait = (pg.sCC, pg.sCC.n)
        for nm, fn in [("c_ffn0", _phase_c_ffn0), ("c_wo", _phase_c_wo),
                       ("d0", _phase_d0_ln), ("d1", _phase_d1_fwd),
                       ("d2", _phase_d2_corr), ("d3", _phase_d3_gprod),
                       ("d4", _phase_d4_inv), ("d5", _phase_d5_ffn1)]:
            if stop_after == "ab":
                break
            fn(pg, T, n_tok, tap)
            if stop_after == nm:
                break
        nc.sync.wait_ge(pg.sST.h, pg.sST.n)
        nc.sync.wait_ge(pg.sSG.h, pg.sSG.n)
        nc.gpsimd.wait_ge(pg.sSG.h, pg.sSG.n)

    return nc, T, taps


def _consts(pg, T):
    nc = pg.nc
    pg.ident = sbuf(pg, "ident", [128, 128], BF16)
    pg.utri = sbuf(pg, "utri", [C, C], F32)
    pg.swam = sbuf(pg, "swam", [C, C + W], F32)
    pg.swam0 = sbuf(pg, "swam0", [C, C + W], F32)
    pg.ones_bf = sbuf(pg, "ones_bf", [1, 512], BF16)
    pg.dnyq_b = sbuf(pg, "dnyq_b", [128, 32, 1], BF16)
    pg.cvals = sbuf(pg, "cvals", [128, 8], F32)

    pg.sLD.inc(nc.sync.dma_start(pg.ident[:, :], T["tb_ident"][:, :]), 16)
    pg.sLD.inc(nc.sync.dma_start(pg.utri[:, :], T["tb_utri"][:, :]), 16)
    pg.sLD.inc(nc.sync.dma_start(pg.swam[:, :], T["tb_swam"][:, :]), 16)
    pg.sLD.inc(nc.sync.dma_start(pg.swam0[:, :], T["tb_swam0"][:, :]), 16)
    pg.sLD.inc(nc.sync.dma_start(pg.dnyq_b[:, :, 0], T["tb_dnyq"][:, :]), 16)
    pg.sLD.wait(nc.vector)
    clist = [EPS, -1.0, 0.5, -0.5, 1.0, 0.0]
    for i, v in enumerate(clist):
        nc.vector.memset(pg.cvals[:, i:i + 1], v)
        nc.const_aps.register(F32, v, pg.cvals[:, i:i + 1])
    pg.sDV.inc(nc.vector.memset(pg.ones_bf[:, :], 1.0))
    pg.sDV.wait(nc.scalar)
    pg.sDV.wait(nc.tensor)


def _layernorm(pg, x_t, xn_bf, wrk, stat):
    nc = pg.nc
    n = x_t.shape[1]
    pg.sDV.wait(nc.scalar)
    nc.scalar.activation(wrk[:, :n], x_t[:, :], AF.Square, bias=0.0, scale=1.0,
                         accum_out=stat[:, 1:2])
    a2 = nc.scalar.activation(wrk[:, :n], x_t[:, :], AF.Copy, bias=0.0, scale=1.0,
                              accum_out=stat[:, 0:1])
    pg.sAC.inc(a2)
    pg.sAC.wait(nc.vector)
    nc.vector.tensor_scalar(stat[:, 2:3], stat[:, 0:1], 1.0 / n, None, ALU.mult)
    t0 = nc.vector.tensor_scalar(stat[:, 3:4], stat[:, 1:2], 1.0 / n, None, ALU.mult)
    pg.sDV.inc(t0)
    pg.sDV.wait(nc.vector)
    t1 = nc.vector.tensor_scalar(stat[:, 6:7], stat[:, 2:3], stat[:, 2:3], None,
                                 ALU.mult)
    pg.sDV.inc(t1)
    pg.sDV.wait(nc.vector)
    v = nc.vector.tensor_scalar(stat[:, 4:5], stat[:, 3:4], stat[:, 6:7], None,
                                ALU.subtract)
    pg.sDV.inc(v)
    pg.sDV.wait(nc.scalar)
    nc.scalar.activation(stat[:, 5:6], stat[:, 4:5], AF.Ln, bias=EPS, scale=1.0)
    a3 = nc.scalar.activation(stat[:, 5:6], stat[:, 5:6], AF.Exp, bias=0.0, scale=-0.5)
    pg.sAC.inc(a3)
    pg.sAC.wait(nc.vector)
    xn = nc.vector.tensor_scalar(xn_bf[:, :], x_t[:, :], stat[:, 2:3], stat[:, 5:6],
                                 ALU.subtract, ALU.mult)
    pg.sDV.inc(xn)


def _transpose_batch(pg, pairs, pbank, nper=8):
    nc = pg.nc
    i = 0
    while i < len(pairs):
        batch = pairs[i:i + nper]
        for j, (s, d) in enumerate(batch):
            mm = nc.tensor.transpose(pbank[:, j * 128:(j + 1) * 128], s,
                                     pg.ident[:, :])
        pg.sPE.inc(mm)
        pg.sPE.wait(nc.vector)
        for j, (s, d) in enumerate(batch):
            cp = nc.vector.tensor_copy(d, pbank[:, j * 128:(j + 1) * 128])
        pg.sDV.inc(cp)
        pg.sDV.wait(nc.tensor)
        i += nper



def _phase_a(pg, T, n_tok, tap):
    nc = pg.nc
    phase_barrier(pg)
    NCH = n_tok // C
    QCOLS = (H // 2) * D
    KCOLS = (KVH // 2) * D
    GRP = min(8, NCH)
    NGRP = NCH // GRP
    NSTR = (QCOLS + 2 * KCOLS) // 512

    tap_xn = tap("xn", [n_tok, HID], BF16)

    with ExitStack() as ctx:
        E = ctx.enter_context
        x_t = [E(nc.sbuf_tensor(f"pa_x{i}", [128, HID], F32)) for i in range(2)]
        wrk = E(nc.sbuf_tensor("pa_wrk", [128, HID], F32))
        stat = E(nc.sbuf_tensor("pa_stat", [128, 8], F32))
        xn_bf = E(nc.sbuf_tensor("pa_xnbf", [128, HID], BF16))
        xnT = E(nc.sbuf_tensor("pa_xnT", [128, GRP, 32, 128], BF16))
        wstr = [E(nc.sbuf_tensor(f"pa_w{i}", [128, 32, 512], BF16)) for i in range(2)]
        qkv_o = [E(nc.sbuf_tensor(f"pa_qo{i}", [128, 512], F32)) for i in range(4)]
        pb_t = E(nc.psum_tensor("pa_pt", [128, 1024], BF16))
        pb_mm = [E(nc.psum_tensor(f"pa_pm{i}", [128, 512], F32)) for i in range(4)]

        wsrc = [(T["wq"], s * 512) for s in range(QCOLS // 512)]
        wsrc.append((T["wk"], 0))
        wsrc.append((T["wv"], 0))

        lnfence = [0, 0]
        wfence = [0, 0]
        qfence = [0, 0, 0, 0]
        qstfence = [0, 0, 0, 0]
        for g in range(NGRP):
            grp_pe = pg.sPE.n
            nc.vector.wait_ge(pg.sPE.h, grp_pe)
            for cc in range(GRP):
                ci = g * GRP + cc
                par = ci % 2
                nc.sync.wait_ge(pg.sDV.h, lnfence[par])
                ld = nc.sync.dma_start(x_t[par][:, :],
                                       T["x_b"][ci * C:(ci + 1) * C, :])
                pg.sLD.inc(ld, 16)
                pg.sLD.wait(nc.scalar)
                pg.sLD.wait(nc.vector)
                _layernorm(pg, x_t[par], xn_bf, wrk, stat)
                lnfence[par] = pg.sDV.n
                if tap_xn is not None:
                    pg.sDV.wait(nc.sync)
                    pg.sST.inc(nc.sync.dma_start(
                        tap_xn[ci * C:(ci + 1) * C, :], xn_bf[:, :]), 16)
                pg.sDV.wait(nc.tensor)
                _transpose_batch(
                    pg,
                    [(xn_bf[:, ft * 128:(ft + 1) * 128], xnT[:, cc, ft, :])
                     for ft in range(32)], pb_t)
                pg.sDV.wait(nc.sync)
                st = nc.sync.dma_start(
                    T["xnT_sc"][ci].rearrange("ft p t -> p ft t"),
                    xnT[:, cc, :, :])
                pg.sST.inc(st, 16)
            for s in range(NSTR):
                spar = s % 2
                wdram, col0 = wsrc[s]
                nc.gpsimd.wait_ge(pg.sPE.h, wfence[spar])
                ldw = nc.gpsimd.dma_start(
                    wstr[spar][:, :, :],
                    wdram[:, col0:col0 + 512].rearrange("(kt p) n -> p kt n", p=128))
                pg.sLG.inc(ldw, 16)
                pg.sLG.wait(nc.tensor)
                for cc in range(GRP):
                    ci = g * GRP + cc
                    mpar = (s * GRP + cc) % 4
                    nc.tensor.wait_ge(pg.sDV.h, qfence[mpar])
                    for kt in range(32):
                        mm = nc.tensor.matmul(pb_mm[mpar][:, :],
                                              xnT[:, cc, kt, :],
                                              wstr[spar][:, kt, :],
                                              start=(kt == 0), stop=(kt == 31))
                    pg.sPE.inc(mm)
                    wfence[spar] = pg.sPE.n
                    pg.sPE.wait(nc.vector)
                    nc.vector.wait_ge(pg.sST.h, qstfence[mpar])
                    ev = nc.vector.tensor_copy(qkv_o[mpar][:, :], pb_mm[mpar][:, :])
                    pg.sDV.inc(ev)
                    qfence[mpar] = pg.sDV.n
                    pg.sDV.wait(nc.sync)
                    st = nc.sync.dma_start(
                        T["qkv_sc"][ci, :, s * 512:(s + 1) * 512], qkv_o[mpar][:, :])
                    pg.sST.inc(st, 16)
                    qstfence[mpar] = pg.sST.n



def _phase_b(pg, T, n_tok, tap):
    nc = pg.nc
    phase_barrier(pg)
    NCH = n_tok // C
    QCOLS = (H // 2) * D
    KCOLS = (KVH // 2) * D
    NKV = KVH // 2
    NQ = H // 2

    tap_o = tap("o_", [n_tok, QCOLS], BF16)
    tap_qs = tap("qs", [n_tok, QCOLS])
    tap_gs = tap("gs", [n_tok, KCOLS])
    tap_ks = tap("ks", [n_tok, KCOLS])

    with ExitStack() as ctx:
        E = ctx.enter_context
        q4 = [E(nc.sbuf_tensor(f"pb_q{i}", [128, QCOLS], F32)) for i in range(2)]
        k4 = [E(nc.sbuf_tensor(f"pb_k{i}", [128, KCOLS], F32)) for i in range(2)]
        v4 = [E(nc.sbuf_tensor(f"pb_v{i}", [128, KCOLS], BF16)) for i in range(3)]
        vlast = [E(nc.sbuf_tensor(f"pb_vl{i}", [64, KCOLS], BF16)) for i in range(2)]
        rop = [E(nc.sbuf_tensor(f"pb_r{i}", [128, 4, D], BF16)) for i in range(2)]
        gs = E(nc.sbuf_tensor("pb_gs", [128, KCOLS], F32))
        w1 = E(nc.sbuf_tensor("pb_w1", [128, QCOLS], F32))
        w2 = E(nc.sbuf_tensor("pb_w2", [128, QCOLS], F32))
        qs = E(nc.sbuf_tensor("pb_qs", [128, QCOLS], F32))
        ks = E(nc.sbuf_tensor("pb_ks", [128, KCOLS], F32))
        qrope = E(nc.sbuf_tensor("pb_qrope", [128, QCOLS], BF16))
        krope = E(nc.sbuf_tensor("pb_krope", [128, KCOLS], BF16))
        stat = E(nc.sbuf_tensor("pb_stat", [128, 64], F32))
        qs_bf = E(nc.sbuf_tensor("pb_qsbf", [128, QCOLS], BF16))
        ks_bf = E(nc.sbuf_tensor("pb_ksbf", [128, KCOLS], BF16))
        qsT = E(nc.sbuf_tensor("pb_qsT", [128, NQ, 128], BF16))
        ksT = E(nc.sbuf_tensor("pb_ksT", [128, NKV, 128], BF16))
        qrT = E(nc.sbuf_tensor("pb_qrT", [128, NQ, 128], BF16))
        krT = [E(nc.sbuf_tensor(f"pb_krT{i}", [128, NKV, 128], BF16))
               for i in range(3)]
        eaT = E(nc.sbuf_tensor("pb_eaT", [128, KCOLS], F32))
        enaT = E(nc.sbuf_tensor("pb_enaT", [128, KCOLS], F32))
        ekH = E(nc.sbuf_tensor("pb_ekH", [128, KCOLS], F32))
        ktil = E(nc.sbuf_tensor("pb_ktil", [128, 128], BF16))
        kHbf = E(nc.sbuf_tensor("pb_kHbf", [128, 128], BF16))
        kHT = E(nc.sbuf_tensor("pb_kHT", [128, 128], BF16))
        qtilT = E(nc.sbuf_tensor("pb_qtilT", [128, 4, 128], BF16))
        ptm = E(nc.sbuf_tensor("pb_ptm", [128, 4, 128], BF16))
        S = E(nc.sbuf_tensor("pb_S", [128, KCOLS], F32))
        S_bf = E(nc.sbuf_tensor("pb_Sbf", [128, KCOLS], BF16))
        sco = E(nc.sbuf_tensor("pb_sco", [128, 4, C + W], F32))
        pt_sw = E(nc.sbuf_tensor("pb_ptsw", [128, 4, C + W], BF16))
        ptA = E(nc.sbuf_tensor("pb_ptA", [64, 4, 128], BF16))
        ptB = E(nc.sbuf_tensor("pb_ptB", [128, 4, 128], BF16))
        o_ch = E(nc.sbuf_tensor("pb_och", [128, QCOLS], BF16))

        pb_a = E(nc.psum_tensor("pb_pa", [128, 512], F32))
        pb_t = E(nc.psum_tensor("pb_pt", [128, 1024], BF16))
        pb_pt = E(nc.psum_tensor("pb_ppt", [128, 512], F32))
        pb_o = E(nc.psum_tensor("pb_po", [128, 512], F32))
        pb_s1 = E(nc.psum_tensor("pb_ps1", [128, 2, C + W], F32))
        pb_s2 = E(nc.psum_tensor("pb_ps2", [128, 2, C + W], F32))
        pb_ds = E(nc.psum_tensor("pb_pds", [128, 512], F32))

        nc.vector.memset(S[:, :], 0.0)
        nc.vector.memset(krT[2][:, :, :], 0.0)
        nc.vector.memset(v4[2][:, :], 0.0)
        mz = nc.vector.memset(vlast[0][:, :], 0.0)
        pg.sDV.inc(mz)

        pe_end = {-1: pg.sPE.n, -2: pg.sPE.n}
        dv_end = {-1: pg.sDV.n, -2: pg.sDV.n}
        ds_fence = 0
        o_fence = 0

        def dma_in(ci):
            par = ci % 2
            nc.sync.wait_ge(pg.sPE.h, pe_end[ci - 2])
            nc.sync.wait_ge(pg.sDV.h, dv_end[ci - 2])
            pg.sLD.inc(nc.sync.dma_start(q4[par][:, :],
                                         T["qkv_sc"][ci, :, 0:QCOLS]), 16)
            pg.sLD.inc(nc.sync.dma_start(k4[par][:, :],
                                         T["qkv_sc"][ci, :, QCOLS:QCOLS + KCOLS]), 16)
            for j, tb in enumerate(("tb_rcq", "tb_rsq", "tb_rck", "tb_rsk")):
                pg.sLD.inc(nc.sync.dma_start(rop[par][:, j, :],
                                             T[tb][ci * C:(ci + 1) * C, :]), 16)
            nc.gpsimd.wait_ge(pg.sPE.h, pe_end[ci - 2])
            pg.sLG.inc(nc.gpsimd.dma_start(v4[ci % 3][:, :],
                                           T["qkv_sc"][ci, :, QCOLS + KCOLS:]), 16)
            if ci >= 1:
                pg.sLG.inc(nc.gpsimd.dma_start(
                    vlast[ci % 2][:, :],
                    T["qkv_sc"][ci - 1, 64:128, QCOLS + KCOLS:]), 16)

        nc.sync.wait_ge(pg.sST.h, pg.sST.n)
        nc.gpsimd.wait_ge(pg.sST.h, pg.sST.n)
        dma_in(0)
        for ci in range(NCH):
            par = ci % 2
            v_cur = v4[ci % 3]
            krT_cur = krT[ci % 3]
            krT_prev = krT[(ci - 1) % 3]
            ld_mark = pg.sLD.n
            lg_mark = pg.sLG.n
            if ci + 1 < NCH:
                dma_in(ci + 1)
            nc.vector.wait_ge(pg.sPE.h, pe_end[ci - 1])
            nc.scalar.wait_ge(pg.sPE.h, pe_end[ci - 1])
            nc.vector.wait_ge(pg.sLD.h, ld_mark)
            nc.vector.wait_ge(pg.sLG.h, lg_mark)
            nc.scalar.wait_ge(pg.sLD.h, ld_mark)
            nc.tensor.wait_ge(pg.sLG.h, lg_mark)
            nc.scalar.wait_ge(pg.sST.h, o_fence)

            kt_ = k4[par]
            qt_ = q4[par]

            nc.scalar.activation(w2[:, 0:KCOLS], kt_[:, :], AF.Abs,
                                 bias=0.0, scale=1.0)
            nc.scalar.activation(w2[:, 0:KCOLS], w2[:, 0:KCOLS], AF.Exp,
                                 bias=0.0, scale=-1.0)
            ac = nc.scalar.activation(w2[:, 0:KCOLS], w2[:, 0:KCOLS], AF.Ln,
                                      bias=1.0, scale=1.0)
            pg.sAC.inc(ac)
            nc.vector.tensor_scalar(gs[:, :], kt_[:, :], -1.0, 0.0, ALU.mult, ALU.max)
            pg.sAC.wait(nc.vector)
            nc.vector.tensor_tensor(gs[:, :], gs[:, :], w2[:, 0:KCOLS], ALU.add)
            gsi = nc.vector.tensor_scalar(gs[:, :], gs[:, :], -1.0 / 16.0, None,
                                          ALU.mult)
            pg.sDV.inc(gsi)
            if tap_gs is not None:
                pg.sDV.wait(nc.sync)
                pg.sST.inc(nc.sync.dma_start(tap_gs[ci * C:(ci + 1) * C, :],
                                             gs[:, :]), 16)

            kg = kt_[:, :].rearrange("p (h d) -> p h d", d=D)
            nc.vector.tensor_reduce(
                stat[:, 0:NKV].rearrange("p (h o) -> p h o", o=1), kg, AX.X, ALU.max)
            qg = qt_[:, :].rearrange("p (h d) -> p h d", d=D)
            rr = nc.vector.tensor_reduce(
                stat[:, 16:16 + NQ].rearrange("p (h o) -> p h o", o=1),
                qg, AX.X, ALU.max)
            pg.sDV.inc(rr)
            pg.sDV.wait(nc.vector)
            nc.vector.tensor_scalar(stat[:, 8:8 + NKV], stat[:, 0:NKV], -1.0, None,
                                    ALU.mult)
            nv = nc.vector.tensor_scalar(stat[:, 32:32 + NQ], stat[:, 16:16 + NQ],
                                         -1.0, None, ALU.mult)
            pg.sDV.inc(nv)
            pg.sDV.wait(nc.scalar)
            for h_ in range(NKV):
                nc.scalar.activation(ks[:, h_ * D:(h_ + 1) * D],
                                     kt_[:, h_ * D:(h_ + 1) * D], AF.Exp,
                                     bias=stat[:, 8 + h_:9 + h_], scale=1.0,
                                     accum_out=stat[:, 12 + h_:13 + h_])
            for h_ in range(NQ):
                ac = nc.scalar.activation(qs[:, h_ * D:(h_ + 1) * D],
                                          qt_[:, h_ * D:(h_ + 1) * D], AF.Exp,
                                          bias=stat[:, 32 + h_:33 + h_], scale=1.0,
                                          accum_out=stat[:, 48 + h_:49 + h_])
            pg.sAC.inc(ac)
            pg.sAC.wait(nc.vector)
            nc.vector.reciprocal(stat[:, 12:12 + NKV], stat[:, 12:12 + NKV])
            rc = nc.vector.reciprocal(stat[:, 48:48 + NQ], stat[:, 48:48 + NQ])
            pg.sDV.inc(rc)
            pg.sDV.wait(nc.vector)
            for h_ in range(NKV):
                nc.vector.tensor_scalar(ks[:, h_ * D:(h_ + 1) * D],
                                        ks[:, h_ * D:(h_ + 1) * D],
                                        stat[:, 12 + h_:13 + h_], None, ALU.mult)
            nc.vector.tensor_copy(ks_bf[:, :], ks[:, :])
            for h_ in range(NQ):
                nc.vector.tensor_scalar(qs[:, h_ * D:(h_ + 1) * D],
                                        qs[:, h_ * D:(h_ + 1) * D],
                                        stat[:, 48 + h_:49 + h_], None, ALU.mult)
            qcast = nc.vector.tensor_copy(qs_bf[:, :], qs[:, :])
            pg.sDV.inc(qcast)
            if tap_qs is not None:
                pg.sDV.wait(nc.sync)
                pg.sST.inc(nc.sync.dma_start(tap_qs[ci * C:(ci + 1) * C, :],
                                             qs[:, :]), 16)
            if tap_ks is not None:
                pg.sDV.wait(nc.sync)
                pg.sST.inc(nc.sync.dma_start(tap_ks[ci * C:(ci + 1) * C, :],
                                             ks[:, :]), 16)

            def rope(dst, src, ncols, cosi, sini):
                nh = ncols // D
                g3 = src.rearrange("p (h d) -> p h d", d=D)
                gd = dst.rearrange("p (h d) -> p h d", d=D)
                wv = w2[:, 0:ncols].rearrange("p (h d) -> p h d", d=D)
                nc.vector.tensor_scalar(wv[:, :, 0:64], g3[:, :, 64:128], -1.0,
                                        None, ALU.mult)
                nc.vector.tensor_copy(wv[:, :, 64:128], g3[:, :, 0:64])
                cosb = rop[par][:, cosi:cosi + 1, :].to_broadcast([128, nh, D])
                sinb = rop[par][:, sini:sini + 1, :].to_broadcast([128, nh, D])
                nc.vector.tensor_tensor(wv, wv, sinb, ALU.mult)
                t2v = w1[:, 0:ncols].rearrange("p (h d) -> p h d", d=D)
                nc.vector.tensor_tensor(t2v, g3, cosb, ALU.mult)
                return nc.vector.tensor_tensor(gd, t2v, wv, ALU.add)

            rope(qrope[:, :], qt_[:, :], QCOLS, 0, 1)
            rp = rope(krope[:, :], kt_[:, :], KCOLS, 2, 3)
            pg.sDV.inc(rp)

            pg.sDV.wait(nc.tensor)
            pg.sAC.wait(nc.tensor)
            for h_ in range(NKV):
                mm = nc.tensor.matmul(pb_a[:, h_ * 128:(h_ + 1) * 128],
                                      gs[:, h_ * D:(h_ + 1) * D], pg.utri[:, :],
                                      start=True, stop=True)
            pg.sPE.inc(mm)
            pg.sPE.wait(nc.scalar)
            nc.scalar.activation(eaT[:, :], pb_a[:, :], AF.Exp, bias=0.0, scale=1.0)
            ac = nc.scalar.activation(enaT[:, :], pb_a[:, :], AF.Exp, bias=0.0,
                                      scale=-1.0)
            pg.sAC.inc(ac)
            pg.sAC.wait(nc.vector)
            for h_ in range(NKV):
                nc.vector.tensor_scalar(ekH[:, h_ * D:(h_ + 1) * D],
                                        enaT[:, h_ * D:(h_ + 1) * D],
                                        eaT[:, h_ * D + 127:h_ * D + 128],
                                        None, ALU.mult)
            scst = nc.vector.tensor_copy(S_bf[:, :], S[:, :])
            pg.sDV.inc(scst)

            pg.sDV.wait(nc.tensor)
            pairs = ([(qs_bf[:, i * 128:(i + 1) * 128], qsT[:, i, :])
                      for i in range(NQ)]
                     + [(ks_bf[:, i * 128:(i + 1) * 128], ksT[:, i, :])
                        for i in range(NKV)]
                     + [(qrope[:, i * 128:(i + 1) * 128], qrT[:, i, :])
                        for i in range(NQ)]
                     + [(krope[:, i * 128:(i + 1) * 128], krT_cur[:, i, :])
                        for i in range(NKV)])
            _transpose_batch(pg, pairs, pb_t)

            for h_ in range(NKV):
                hsl = slice(h_ * D, (h_ + 1) * D)
                nc.vector.tensor_tensor(ktil[:, :], ksT[:, h_, :], enaT[:, hsl],
                                        ALU.mult)
                nc.vector.tensor_tensor(kHbf[:, :], ksT[:, h_, :], ekH[:, hsl],
                                        ALU.mult)
                for j in range(4):
                    tt = nc.vector.tensor_tensor(qtilT[:, j, :],
                                                 qsT[:, h_ * 4 + j, :],
                                                 eaT[:, hsl], ALU.mult)
                pg.sDV.inc(tt)
                pg.sDV.wait(nc.tensor)
                for j in range(4):
                    nc.tensor.matmul(pb_pt[:, j * 128:(j + 1) * 128], ktil[:, :],
                                     qtilT[:, j, :], start=True, stop=True)
                nc.tensor.transpose(pb_t[:, 0:128], kHbf[:, :], pg.ident[:, :])
                for j in range(4):
                    qh = h_ * 4 + j
                    pb_s = pb_s1 if j < 2 else pb_s2
                    jj = j % 2
                    nc.tensor.matmul(pb_s[:, jj, 0:W], qrT[:, qh, :],
                                     krT_prev[:, h_, 64:128], start=True, stop=True)
                    mm = nc.tensor.matmul(pb_s[:, jj, W:W + C], qrT[:, qh, :],
                                          krT_cur[:, h_, :], start=True, stop=True)
                pg.sPE.inc(mm)
                pg.sPE.wait(nc.vector)
                for j in range(4):
                    nc.vector.tensor_tensor(ptm[:, j, :],
                                            pb_pt[:, j * 128:(j + 1) * 128],
                                            pg.utri[:, :], ALU.mult)
                nc.vector.tensor_copy(kHT[:, :], pb_t[:, 0:128])
                msk = pg.swam0 if ci == 0 else pg.swam
                for j in range(4):
                    pb_s = pb_s1 if j < 2 else pb_s2
                    nc.vector.tensor_tensor(sco[:, j, :], pb_s[:, j % 2, :],
                                            msk[:, :], ALU.add)
                rr = nc.vector.tensor_reduce(
                    stat[:, 0:4].rearrange("p (h o) -> p h o", o=1),
                    sco[:, :, :], AX.X, ALU.max)
                pg.sDV.inc(rr)
                pg.sDV.wait(nc.vector)
                rd = nc.vector.tensor_scalar(stat[:, 4:8], stat[:, 0:4], -1.0,
                                             None, ALU.mult)
                pg.sDV.inc(rd)
                pg.sDV.wait(nc.scalar)
                pg.sDV.wait(nc.tensor)
                for j in range(4):
                    ac = nc.scalar.activation(sco[:, j, :], sco[:, j, :], AF.Exp,
                                              bias=stat[:, 4 + j:5 + j], scale=1.0,
                                              accum_out=stat[:, 8 + j:9 + j])
                pg.sAC.inc(ac)
                pg.sAC.wait(nc.vector)
                rc = nc.vector.reciprocal(stat[:, 8:12], stat[:, 8:12])
                pg.sDV.inc(rc)
                pg.sDV.wait(nc.vector)
                for j in range(4):
                    sw = nc.vector.tensor_scalar(pt_sw[:, j, :], sco[:, j, :],
                                                 stat[:, 8 + j:9 + j], None, ALU.mult)
                pg.sDV.inc(sw)
                pg.sDV.wait(nc.tensor)
                nc.tensor.wait_ge(pg.sDV.h, ds_fence)
                nc.tensor.matmul(pb_ds[:, h_ * 128:(h_ + 1) * 128], kHT[:, :],
                                 v_cur[:, hsl], start=True, stop=True)
                for j in range(4):
                    nc.tensor.transpose(pb_t[0:64, j * 128:(j + 1) * 128],
                                        pt_sw[:, j, 0:64], pg.ident[:, :])
                    tr = nc.tensor.transpose(
                        pb_t[:, 512 + j * 128:512 + (j + 1) * 128],
                        pt_sw[:, j, 64:192], pg.ident[:, :])
                pg.sPE.inc(tr)
                pg.sPE.wait(nc.vector)
                nc.vector.tensor_copy(
                    ptA[:, :, :],
                    pb_t[0:64, 0:512].rearrange("p (j t) -> p j t", t=128))
                pa = nc.vector.tensor_copy(
                    ptB[:, :, :],
                    pb_t[:, 512:1024].rearrange("p (j t) -> p j t", t=128))
                pg.sDV.inc(pa)
                pg.sDV.wait(nc.tensor)
                for j in range(4):
                    po = pb_o[:, j * 128:(j + 1) * 128]
                    nc.tensor.matmul(po, ptm[:, j, :], v_cur[:, hsl],
                                     start=True, stop=False)
                    nc.tensor.matmul(po, qtilT[:, j, :], S_bf[:, hsl],
                                     start=False, stop=False)
                    nc.tensor.matmul(po, ptA[:, j, :], vlast[ci % 2][:, hsl],
                                     start=False, stop=False)
                    mm = nc.tensor.matmul(po, ptB[:, j, :], v_cur[:, hsl],
                                          start=False, stop=True)
                pg.sPE.inc(mm)
                pg.sPE.wait(nc.scalar)
                oc = nc.scalar.activation(o_ch[:, h_ * 512:(h_ + 1) * 512],
                                          pb_o[:, :], AF.Copy, bias=0.0, scale=0.5)
                pg.sAC.inc(oc)
                pg.sAC.wait(nc.tensor)
                pg.sPE.wait(nc.vector)
                nc.vector.tensor_scalar(S[:, hsl], S[:, hsl],
                                        eaT[:, h_ * D + 127:h_ * D + 128], None,
                                        ALU.mult)
                su = nc.vector.tensor_tensor(S[:, hsl], S[:, hsl],
                                             pb_ds[:, h_ * 128:(h_ + 1) * 128],
                                             ALU.add)
                pg.sDV.inc(su)
                ds_fence = pg.sDV.n

            pg.sAC.wait(nc.sync)
            st = nc.sync.dma_start(T["o_mine"][ci * C:(ci + 1) * C, :], o_ch[:, :])
            pg.sST.inc(st, 16)
            o_fence = pg.sST.n
            if tap_o is not None:
                pg.sST.inc(nc.sync.dma_start(tap_o[ci * C:(ci + 1) * C, :],
                                             o_ch[:, :]), 16)
            pe_end[ci] = pg.sPE.n
            dv_end[ci] = pg.sDV.n



def _phase_c_ffn0(pg, T, n_tok, tap):
    nc = pg.nc
    phase_barrier(pg)
    MYT = n_tok // 2
    MCH = MYT // C
    NCH = n_tok // C

    tap_f = tap("ffn0", [MYT, HID])
    tap_act = tap("act0", [MYT, FFN_H], BF16)

    with ExitStack() as ctx:
        E = ctx.enter_context
        xnT_ch = E(nc.sbuf_tensor("c1_xnT", [128, 32, 128], BF16))
        xfT = E(nc.sbuf_tensor("c1_xfT", [128, 8, MYT], BF16))
        u1T = E(nc.sbuf_tensor("c1_u1T", [2, MYT], BF16))
        u3T = E(nc.sbuf_tensor("c1_u3T", [2, MYT], BF16))
        wrkf = E(nc.sbuf_tensor("c1_wrkf", [128, 128], F32))
        pb_u = E(nc.psum_tensor("c1_pu", [2, 512], F32))
        w1s = [E(nc.sbuf_tensor(f"c1_w1s{i}", [128, 8, 512], BF16)) for i in range(2)]
        w3s = [E(nc.sbuf_tensor(f"c1_w3s{i}", [128, 8, 512], BF16)) for i in range(2)]
        silu_t = E(nc.sbuf_tensor("c1_silu", [128, 512], F32))
        act_t = E(nc.sbuf_tensor("c1_act", [128, 512], BF16))
        pb_m = [E(nc.psum_tensor(f"c1_pm{i}", [128, 512], F32)) for i in range(6)]
        pg.a1T_b = E(nc.sbuf_tensor("a1T_b", [128, 32, 2], BF16))
        pg.a3T_b = E(nc.sbuf_tensor("a3T_b", [128, 32, 2], BF16))
        pg.b1T_b = E(nc.sbuf_tensor("b1T_b", [2, FFN_H], BF16))
        pg.b3T_b = E(nc.sbuf_tensor("b3T_b", [2, FFN_H], BF16))
        pg.sLG.inc(nc.gpsimd.dma_start(
            pg.a1T_b[:, :, :],
            T["a1T"][:, :].rearrange("(ft p) r -> p ft r", p=128)), 16)
        pg.sLG.inc(nc.gpsimd.dma_start(
            pg.a3T_b[:, :, :],
            T["a3T"][:, :].rearrange("(ft p) r -> p ft r", p=128)), 16)
        pg.sLG.inc(nc.gpsimd.dma_start(pg.b1T_b[:, :], T["b1T"][:, :]), 16)
        pg.sLG.inc(nc.gpsimd.dma_start(pg.b3T_b[:, :], T["b3T"][:, :]), 16)
        pg.sLG.wait(nc.tensor)
        pg.sLG.wait(nc.vector)

        ld_fence = 0
        dvf = 0
        for ci in range(MCH):
            nc.sync.wait_ge(pg.sPE.h, ld_fence)
            nc.sync.wait_ge(pg.sDV.h, dvf)
            br_load(pg, nc.sync, pg.sLD, xnT_ch[:, :, :],
                    T["xnT_sc"][ci].rearrange("ft p t -> p ft t"),
                    T["xnT_sc"][NCH // 2 + ci].rearrange("ft p t -> p ft t"))
            pg.sLD.wait(nc.vector)
            pg.sLD.wait(nc.tensor)
            tsl = slice(ci * C, (ci + 1) * C)
            for fq in range(8):
                nc.vector.tensor_tensor(wrkf[:, :], xnT_ch[:, fq, :],
                                        xnT_ch[:, fq + 8, :], ALU.add)
                nc.vector.tensor_tensor(wrkf[:, :], wrkf[:, :],
                                        xnT_ch[:, fq + 16, :], ALU.add)
                fv = nc.vector.tensor_tensor(xfT[:, fq, tsl], wrkf[:, :],
                                             xnT_ch[:, fq + 24, :], ALU.add)
            pg.sDV.inc(fv)
            dvf = pg.sDV.n
            pg.sDV.wait(nc.tensor)
            nc.tensor.wait_ge(pg.sAC.h, pg.c1uf if hasattr(pg, "c1uf") else 0)
            for kt in range(32):
                nc.tensor.matmul(pb_u[:, 0:128], pg.a1T_b[:, kt, :],
                                 xnT_ch[:, kt, :], start=(kt == 0), stop=(kt == 31))
            for kt in range(32):
                mm = nc.tensor.matmul(pb_u[:, 128:256], pg.a3T_b[:, kt, :],
                                      xnT_ch[:, kt, :], start=(kt == 0),
                                      stop=(kt == 31))
            pg.sPE.inc(mm)
            ld_fence = pg.sPE.n
            pg.sPE.wait(nc.scalar)
            nc.scalar.activation(u1T[:, tsl], pb_u[:, 0:128], AF.Copy, bias=0.0,
                                 scale=0.5)
            ua = nc.scalar.activation(u3T[:, tsl], pb_u[:, 128:256], AF.Copy,
                                      bias=0.0, scale=0.5)
            pg.sAC.inc(ua)
            pg.c1uf = pg.sAC.n

        nstrips = (FFN_H + 511) // 512
        wf = [0, 0]
        acf = [0, 0, 0]
        dvf2 = [0, 0, 0]
        stf = 0
        pg.sDV.wait(nc.tensor)
        pg.sAC.wait(nc.tensor)
        for si in range(nstrips):
            h0 = si * 512
            hw = min(512, FFN_H - h0)
            spar = si % 2
            nc.gpsimd.wait_ge(pg.sPE.h, wf[spar])
            pg.sLG.inc(nc.gpsimd.dma_start(
                w1s[spar][:, :, 0:hw],
                T["w1qT"][:, h0:h0 + hw].rearrange("(kt p) n -> p kt n", p=128)), 16)
            pg.sLG.inc(nc.gpsimd.dma_start(
                w3s[spar][:, :, 0:hw],
                T["w3qT"][:, h0:h0 + hw].rearrange("(kt p) n -> p kt n", p=128)), 16)
            lg_mark = pg.sLG.n
            nc.tensor.wait_ge(pg.sLG.h, lg_mark)
            for ci in range(MCH):
                tsl = slice(ci * C, (ci + 1) * C)
                bank = ci % 3
                bankU = 3 + bank
                nc.tensor.wait_ge(pg.sAC.h, acf[bank])
                nc.tensor.wait_ge(pg.sDV.h, dvf2[bank])
                for kt in range(8):
                    nc.tensor.matmul(pb_m[bank][:, 0:hw], xfT[:, kt, tsl],
                                     w1s[spar][:, kt, 0:hw], start=(kt == 0),
                                     stop=False)
                nc.tensor.matmul(pb_m[bank][:, 0:hw], u1T[:, tsl],
                                 pg.b1T_b[:, h0:h0 + hw], start=False, stop=True)
                for kt in range(8):
                    nc.tensor.matmul(pb_m[bankU][:, 0:hw], xfT[:, kt, tsl],
                                     w3s[spar][:, kt, 0:hw], start=(kt == 0),
                                     stop=False)
                mm = nc.tensor.matmul(pb_m[bankU][:, 0:hw], u3T[:, tsl],
                                      pg.b3T_b[:, h0:h0 + hw], start=False,
                                      stop=True)
                pg.sPE.inc(mm)
                wf[spar] = pg.sPE.n
                pg.sPE.wait(nc.scalar)
                nc.scalar.wait_ge(pg.sDV.h, dvf2[(bank + 2) % 3])
                sl = nc.scalar.activation(silu_t[:, 0:hw], pb_m[bank][:, 0:hw],
                                          AF.Silu, bias=0.0, scale=1.0)
                pg.sAC.inc(sl)
                acf[bank] = pg.sAC.n
                pg.sAC.wait(nc.vector)
                nc.vector.wait_ge(pg.sST.h, stf)
                av = nc.vector.tensor_tensor(act_t[:, 0:hw], silu_t[:, 0:hw],
                                             pb_m[bankU][:, 0:hw], ALU.mult)
                pg.sDV.inc(av)
                dvf2[bank] = pg.sDV.n
                pg.sDV.wait(nc.sync)
                st = nc.sync.dma_start(T["act_sc"][ci, :, h0:h0 + hw],
                                       act_t[:, 0:hw])
                pg.sST.inc(st, 16)
                stf = pg.sST.n
                if tap_act is not None:
                    pg.sST.inc(nc.sync.dma_start(
                        tap_act[tsl, h0:h0 + hw], act_t[:, 0:hw]), 16)

    phase_barrier(pg)
    with ExitStack() as ctx:
        E = ctx.enter_context
        act_ch = E(nc.sbuf_tensor("c2_act", [128, FFN_H], BF16))
        hf_bf = E(nc.sbuf_tensor("c2_hf", [128, 2752], BF16))
        hfT = E(nc.sbuf_tensor("c2_hfT", [128, 22, MYT], BF16))
        u2T = E(nc.sbuf_tensor("c2_u2T", [2, MYT], BF16))
        aT_st = E(nc.sbuf_tensor("c2_aTst", [128, 8, 128], BF16))
        wrkf = E(nc.sbuf_tensor("c2_wrkf", [128, 2752], F32))
        pb_u = E(nc.psum_tensor("c2_pu", [2, 512], F32))
        pb_t = E(nc.psum_tensor("c2_pt", [128, 1024], BF16))
        w2s = [E(nc.sbuf_tensor(f"c2_w2s{i}", [128, 22, 512], BF16)) for i in range(2)]
        pb_m = [E(nc.psum_tensor(f"c2_pm{i}", [128, 512], F32)) for i in range(4)]
        x_t = E(nc.sbuf_tensor("c2_xt", [128, 512], F32))
        o_t = E(nc.sbuf_tensor("c2_ot", [128, 512], F32))
        pg.a2T_b = E(nc.sbuf_tensor("a2T_b", [128, NHT, 2], BF16))
        pg.b2T_b = E(nc.sbuf_tensor("b2T_b", [2, HID], BF16))
        pg.sLG.inc(nc.gpsimd.dma_start(
            pg.a2T_b[:, :, :],
            T["a2T"][:, :].rearrange("(ft p) r -> p ft r", p=128)), 16)
        pg.sLG.inc(nc.gpsimd.dma_start(pg.b2T_b[:, :], T["b2T"][:, :]), 16)
        pg.sLG.wait(nc.tensor)

        nc.sync.wait_ge(pg.sST.h, pg.sST.n)
        ld_fence = 0
        dvf = 0
        for ci in range(MCH):
            tsl = slice(ci * C, (ci + 1) * C)
            nc.sync.wait_ge(pg.sPE.h, ld_fence)
            nc.sync.wait_ge(pg.sDV.h, dvf)
            ld = nc.sync.dma_start(act_ch[:, :], T["act_sc"][ci, :, :])
            pg.sLD.inc(ld, 16)
            pg.sLD.wait(nc.vector)
            pg.sLD.wait(nc.tensor)
            nc.vector.tensor_tensor(wrkf[:, :], act_ch[:, 0:2752],
                                    act_ch[:, 2752:5504], ALU.add)
            nc.vector.tensor_tensor(wrkf[:, :], wrkf[:, :],
                                    act_ch[:, 5504:8256], ALU.add)
            fv = nc.vector.tensor_tensor(hf_bf[:, :], wrkf[:, :],
                                         act_ch[:, 8256:11008], ALU.add)
            pg.sDV.inc(fv)
            pg.sDV.wait(nc.tensor)
            _transpose_batch(
                pg,
                [(hf_bf[:, kt * 128:(kt + 1) * 128], hfT[:, kt, tsl])
                 for kt in range(21)], pb_t)
            tr = nc.tensor.transpose(pb_t[0:64, 0:128], hf_bf[:, 2688:2752],
                                     pg.ident[:, :])
            pg.sPE.inc(tr)
            pg.sPE.wait(nc.vector)
            hv = nc.vector.tensor_copy(hfT[0:64, 21, tsl], pb_t[0:64, 0:128])
            pg.sDV.inc(hv)
            pg.sDV.wait(nc.tensor)
            for bi in range(NHT // 8 + (1 if NHT % 8 else 0)):
                kts = list(range(bi * 8, min(bi * 8 + 8, NHT)))
                for j, kt in enumerate(kts):
                    tr = nc.tensor.transpose(pb_t[:, j * 128:(j + 1) * 128],
                                             act_ch[:, kt * 128:(kt + 1) * 128],
                                             pg.ident[:, :])
                pg.sPE.inc(tr)
                pg.sPE.wait(nc.vector)
                for j, kt in enumerate(kts):
                    cp = nc.vector.tensor_copy(aT_st[:, j, :],
                                               pb_t[:, j * 128:(j + 1) * 128])
                pg.sDV.inc(cp)
                pg.sDV.wait(nc.tensor)
                for j, kt in enumerate(kts):
                    mm = nc.tensor.matmul(pb_u[:, 0:128], pg.a2T_b[:, kt, :],
                                          aT_st[:, j, :], start=(kt == 0),
                                          stop=(kt == NHT - 1))
                pg.sPE.inc(mm)
            ld_fence = pg.sPE.n
            dvf = pg.sDV.n
            pg.sPE.wait(nc.scalar)
            ua = nc.scalar.activation(u2T[:, tsl], pb_u[:, 0:128], AF.Copy,
                                      bias=0.0, scale=0.5)
            pg.sAC.inc(ua)
            pg.sAC.wait(nc.tensor)

        pg.sDV.wait(nc.tensor)
        pg.sAC.wait(nc.tensor)
        wf = [0, 0]
        evf = [0, 0, 0, 0]
        stf = 0
        for s in range(8):
            spar = s % 2
            nc.gpsimd.wait_ge(pg.sPE.h, wf[spar])
            pg.sLG.inc(nc.gpsimd.dma_start(
                w2s[spar][:, 0:21, :],
                T["w2qT"][0:2688, s * 512:(s + 1) * 512].rearrange(
                    "(kt p) n -> p kt n", p=128)), 16)
            pg.sLG.inc(nc.gpsimd.dma_start(
                w2s[spar][0:64, 21, :],
                T["w2qT"][2688:2752, s * 512:(s + 1) * 512]), 16)
            lg_mark = pg.sLG.n
            nc.tensor.wait_ge(pg.sLG.h, lg_mark)
            for ci in range(MCH):
                bank = ci % 4
                tsl = slice(ci * C, (ci + 1) * C)
                nc.tensor.wait_ge(pg.sDV.h, evf[bank])
                for kt in range(21):
                    nc.tensor.matmul(pb_m[bank][:, :], hfT[:, kt, tsl],
                                     w2s[spar][:, kt, :], start=(kt == 0),
                                     stop=False)
                nc.tensor.matmul(pb_m[bank][:, :], hfT[0:64, 21, tsl],
                                 w2s[spar][0:64, 21, :], start=False, stop=False)
                mm = nc.tensor.matmul(pb_m[bank][:, :], u2T[:, tsl],
                                      pg.b2T_b[:, s * 512:(s + 1) * 512],
                                      start=False, stop=True)
                pg.sPE.inc(mm)
                wf[spar] = pg.sPE.n
                nc.sync.wait_ge(pg.sDV.h, evf[bank])
                ld = nc.sync.dma_start(x_t[:, :],
                                       T["x_my"][tsl, s * 512:(s + 1) * 512])
                pg.sLD.inc(ld, 16)
                pg.sLD.wait(nc.vector)
                pg.sPE.wait(nc.vector)
                nc.vector.wait_ge(pg.sST.h, stf)
                ev = nc.vector.tensor_tensor(o_t[:, :], pb_m[bank][:, :],
                                             x_t[:, :], ALU.add)
                pg.sDV.inc(ev)
                evf[bank] = pg.sDV.n
                pg.sDV.wait(nc.sync)
                st = nc.sync.dma_start(T["x1_sc"][tsl, s * 512:(s + 1) * 512],
                                       o_t[:, :])
                pg.sST.inc(st, 16)
                stf = pg.sST.n
                if tap_f is not None:
                    pg.sST.inc(nc.sync.dma_start(
                        tap_f[tsl, s * 512:(s + 1) * 512], o_t[:, :]), 16)



def _phase_c_wo(pg, T, n_tok, tap):
    nc = pg.nc
    phase_barrier(pg)
    MYT = n_tok // 2
    MCH = MYT // C
    QCOLS = (H // 2) * D
    GRP = min(8, MCH)
    NGRP = MCH // GRP

    tap_wo = tap("wo_out", [MYT, HID])

    with ExitStack() as ctx:
        E = ctx.enter_context
        o_in = E(nc.sbuf_tensor("cw_oin", [128, HID], BF16))
        oT = E(nc.sbuf_tensor("cw_oT", [128, GRP, 32, 128], BF16))
        wos = [E(nc.sbuf_tensor(f"cw_wos{i}", [128, 32, 512], BF16)) for i in range(2)]
        o_ev = E(nc.sbuf_tensor("cw_oev", [128, 512], F32))
        pb_t = E(nc.psum_tensor("cw_pt", [128, 1024], BF16))
        pb_m = [E(nc.psum_tensor(f"cw_pm{i}", [128, 512], F32)) for i in range(4)]

        sem, val = pg.cc_wait
        nc.sync.wait_ge(sem.h, val)
        nc.gpsimd.wait_ge(sem.h, val)
        nc.gpsimd.wait_ge(pg.sST.h, pg.sST.n)

        wf = [0, 0]
        evf = [0, 0, 0, 0]
        sgf = 0
        for g in range(NGRP):
            grp_pe = pg.sPE.n
            nc.vector.wait_ge(pg.sPE.h, grp_pe)
            for cc in range(GRP):
                ci = g * GRP + cc
                pg.sDV.wait(nc.sync)
                for slot in range(2):
                    br_load(pg, nc.sync, pg.sLD,
                            o_in[:, slot * QCOLS:(slot + 1) * QCOLS],
                            T["o_all"][slot * n_tok + ci * C:
                                       slot * n_tok + (ci + 1) * C, :],
                            T["o_all"][slot * n_tok + MYT + ci * C:
                                       slot * n_tok + MYT + (ci + 1) * C, :])
                pg.sLD.wait(nc.tensor)
                _transpose_batch(pg, [(o_in[:, ft * 128:(ft + 1) * 128],
                                       oT[:, cc, ft, :]) for ft in range(32)], pb_t)
            for s in range(8):
                spar = s % 2
                nc.gpsimd.wait_ge(pg.sPE.h, wf[spar])
                pg.sLG.inc(nc.gpsimd.dma_start(
                    wos[spar][:, :, :],
                    T["wo"][:, s * 512:(s + 1) * 512].rearrange(
                        "(kt p) n -> p kt n", p=128)), 16)
                lg_mark = pg.sLG.n
                nc.tensor.wait_ge(pg.sLG.h, lg_mark)
                for cc in range(GRP):
                    ci = g * GRP + cc
                    bank = cc % 4
                    nc.tensor.wait_ge(pg.sDV.h, evf[bank])
                    for kt in range(32):
                        mm = nc.tensor.matmul(pb_m[bank][:, :], oT[:, cc, kt, :],
                                              wos[spar][:, kt, :],
                                              start=(kt == 0), stop=(kt == 31))
                    pg.sPE.inc(mm)
                    wf[spar] = pg.sPE.n
                    pg.sPE.wait(nc.vector)
                    nc.vector.wait_ge(pg.sSG.h, sgf)
                    ev = nc.vector.tensor_copy(o_ev[:, :], pb_m[bank][:, :])
                    pg.sDV.inc(ev)
                    evf[bank] = pg.sDV.n
                    pg.sDV.wait(nc.gpsimd)
                    st = nc.gpsimd.dma_start(
                        T["x1_sc"][ci * C:(ci + 1) * C, s * 512:(s + 1) * 512],
                        o_ev[:, :], accum_op=ALU.add)
                    pg.sSG.inc(st, 16)
                    sgf = pg.sSG.n
                    if tap_wo is not None:
                        pg.sDV.wait(nc.sync)
                        pg.sST.inc(nc.sync.dma_start(
                            tap_wo[ci * C:(ci + 1) * C, s * 512:(s + 1) * 512],
                            o_ev[:, :]), 16)



def _phase_d0_ln(pg, T, n_tok, tap):
    nc = pg.nc
    phase_barrier(pg)
    MYT = n_tok // 2
    MCH = MYT // C
    tap_xn1 = tap("xn1", [MYT, HID], BF16)
    tap_xd0 = tap("xd0", [MCH, 32, 128, C], BF16)

    with ExitStack() as ctx:
        E = ctx.enter_context
        x_t = [E(nc.sbuf_tensor(f"d0_x{i}", [128, HID], F32)) for i in range(2)]
        wrk = E(nc.sbuf_tensor("d0_wrk", [128, HID], F32))
        stat = E(nc.sbuf_tensor("d0_stat", [128, 8], F32))
        xn_bf = E(nc.sbuf_tensor("d0_xnbf", [128, HID], BF16))
        xnT = E(nc.sbuf_tensor("d0_xnT", [128, 32, 128], BF16))
        pb_t = E(nc.psum_tensor("d0_pt", [128, 1024], BF16))

        nc.sync.wait_ge(pg.sSG.h, pg.sSG.n)
        nc.sync.wait_ge(pg.sST.h, pg.sST.n)
        lnf = [0, 0]
        stf = 0
        for ci in range(MCH):
            par = ci % 2
            nc.sync.wait_ge(pg.sDV.h, lnf[par])
            ld = nc.sync.dma_start(x_t[par][:, :], T["x1_sc"][ci * C:(ci + 1) * C, :])
            pg.sLD.inc(ld, 16)
            pg.sLD.wait(nc.scalar)
            pg.sLD.wait(nc.vector)
            _layernorm(pg, x_t[par], xn_bf, wrk, stat)
            lnf[par] = pg.sDV.n
            if tap_xn1 is not None:
                pg.sDV.wait(nc.sync)
                pg.sST.inc(nc.sync.dma_start(tap_xn1[ci * C:(ci + 1) * C, :],
                                             xn_bf[:, :]), 16)
            pg.sDV.wait(nc.tensor)
            nc.vector.wait_ge(pg.sST.h, stf)
            _transpose_batch(pg, [(xn_bf[:, ft * 128:(ft + 1) * 128], xnT[:, ft, :])
                                  for ft in range(32)], pb_t)
            pg.sDV.wait(nc.sync)
            st = nc.sync.dma_start(T["xn1T_sc"][ci].rearrange("ft p t -> p ft t"),
                                   xnT[:, :, :])
            pg.sST.inc(st, 16)
            if tap_xd0 is not None:
                pg.sST.inc(nc.sync.dma_start(
                    tap_xd0[ci].rearrange("ft p t -> p ft t"), xnT[:, :, :]), 16)
            stf = pg.sST.n



def _phase_d1_fwd(pg, T, n_tok, tap):
    nc = pg.nc
    phase_barrier(pg)
    MYT = n_tok // 2
    MCH = MYT // C
    GRP = min(8, MCH)
    NGRP = MCH // GRP

    with ExitStack() as ctx:
        E = ctx.enter_context
        xnT = E(nc.sbuf_tensor("d1_xnT", [128, GRP, 32, 128], BF16))
        dcs = [E(nc.sbuf_tensor(f"d1_dc{i}", [128, 32, 512], BF16)) for i in range(2)]
        dss = [E(nc.sbuf_tensor(f"d1_ds{i}", [128, 32, 512], BF16)) for i in range(2)]
        fr_t = E(nc.sbuf_tensor("d1_fr", [128, 512], F32))
        fi_t = E(nc.sbuf_tensor("d1_fi", [128, 512], F32))
        sq_t = E(nc.sbuf_tensor("d1_sq", [128, 512], F32))
        sq2_t = E(nc.sbuf_tensor("d1_sq2", [128, 512], F32))
        p_t = E(nc.sbuf_tensor("d1_p", [128, 512], BF16))
        fn_t = E(nc.sbuf_tensor("d1_fn", [128, 8], F32))
        pb_r = [E(nc.psum_tensor(f"d1_pr{i}", [128, 512], F32)) for i in range(2)]
        pb_i = [E(nc.psum_tensor(f"d1_pi{i}", [128, 512], F32)) for i in range(2)]
        pb_n = E(nc.psum_tensor("d1_pn", [128, 8], F32))

        tap_fr = tap("frd", [MYT, NSPEC])
        tap_xg = tap("xng", [MCH, 32, 128, C], BF16)
        tap_pp = tap("ppd", [MYT, NSPEC], BF16)
        tap_fn = tap("fnd", [MYT, 8])
        nc.sync.wait_ge(pg.sST.h, pg.sST.n)
        wfd = [0, 0]
        for g in range(NGRP):
            nc.sync.wait_ge(pg.sPE.h, pg.sPE.n)
            for cc in range(GRP):
                ld = nc.sync.dma_start(
                    xnT[:, cc, :, :],
                    T["xn1T_sc"][g * GRP + cc].rearrange("ft p t -> p ft t"))
                pg.sLD.inc(ld, 16)
            pg.sLD.wait(nc.tensor)
            if tap_xg is not None:
                pg.sLD.wait(nc.sync)
                for cc in range(GRP):
                    pg.sST.inc(nc.sync.dma_start(
                        tap_xg[g * GRP + cc].rearrange("ft p t -> p ft t"),
                        xnT[:, cc, :, :]), 16)
            for cc in range(GRP):
                ci = g * GRP + cc
                nc.tensor.wait_ge(pg.sDV.h, pg.d1nf if hasattr(pg, "d1nf") else 0)
                for kt in range(32):
                    mm = nc.tensor.matmul(pb_n[:, 0:1], xnT[:, cc, kt, :],
                                          pg.dnyq_b[:, kt, :],
                                          start=(kt == 0), stop=(kt == 31))
                pg.sPE.inc(mm)
                pg.sPE.wait(nc.vector)
                nc.vector.wait_ge(pg.sST.h, pg.d1nsf if hasattr(pg, "d1nsf") else 0)
                fn = nc.vector.tensor_copy(fn_t[:, 0:1], pb_n[:, 0:1])
                nc.vector.memset(fn_t[:, 1:8], 0.0)
                pg.sDV.inc(fn)
                pg.d1nf = pg.sDV.n
                pg.sDV.wait(nc.sync)
                pg.sST.inc(nc.sync.dma_start(T["fn_sc"][ci * C:(ci + 1) * C, :],
                                             fn_t[:, :]), 16)
                if tap_fn is not None:
                    pg.sST.inc(nc.sync.dma_start(
                        tap_fn[ci * C:(ci + 1) * C, :], fn_t[:, :]), 16)
                pg.d1nsf = pg.sST.n
            for s in range(4):
                spar = s % 2
                nc.gpsimd.wait_ge(pg.sPE.h, wfd[spar])
                pg.sLG.inc(nc.gpsimd.dma_start(
                    dcs[spar][:, :, :],
                    T["tb_dcos"][:, s * 512:(s + 1) * 512].rearrange(
                        "(kt p) n -> p kt n", p=128)), 16)
                pg.sLG.inc(nc.gpsimd.dma_start(
                    dss[spar][:, :, :],
                    T["tb_dsin"][:, s * 512:(s + 1) * 512].rearrange(
                        "(kt p) n -> p kt n", p=128)), 16)
                lg_mark = pg.sLG.n
                nc.tensor.wait_ge(pg.sLG.h, lg_mark)
                for cc in range(GRP):
                    ci = g * GRP + cc
                    bk = cc % 2
                    if not hasattr(pg, "d1ef"):
                        pg.d1ef = [0, 0]
                    nc.tensor.wait_ge(pg.sDV.h, pg.d1ef[bk])
                    for kt in range(32):
                        nc.tensor.matmul(pb_r[bk][:, :], xnT[:, cc, kt, :],
                                         dcs[spar][:, kt, :],
                                         start=(kt == 0), stop=(kt == 31))
                    for kt in range(32):
                        mm = nc.tensor.matmul(pb_i[bk][:, :], xnT[:, cc, kt, :],
                                              dss[spar][:, kt, :],
                                              start=(kt == 0), stop=(kt == 31))
                    pg.sPE.inc(mm)
                    wfd[spar] = pg.sPE.n
                    pg.sPE.wait(nc.vector)
                    nc.vector.wait_ge(pg.sST.h, pg.d1sf if hasattr(pg, "d1sf") else 0)
                    nc.vector.tensor_copy(fr_t[:, :], pb_r[bk][:, :])
                    nc.vector.tensor_scalar(fi_t[:, :], pb_i[bk][:, :], -1.0, None,
                                            ALU.mult)
                    nc.vector.tensor_tensor(sq_t[:, :], fr_t[:, :], fr_t[:, :],
                                            ALU.mult)
                    nc.vector.tensor_tensor(sq2_t[:, :], fi_t[:, :], fi_t[:, :],
                                            ALU.mult)
                    ev = nc.vector.tensor_tensor(p_t[:, :], sq_t[:, :], sq2_t[:, :],
                                                 ALU.add)
                    pg.sDV.inc(ev)
                    pg.d1ef[bk] = pg.sDV.n
                    pg.sDV.wait(nc.sync)
                    ssl = slice(s * 512, (s + 1) * 512)
                    tsl = slice(ci * C, (ci + 1) * C)
                    pg.sST.inc(nc.sync.dma_start(T["fr_sc"][tsl, ssl],
                                                 fr_t[:, :]), 16)
                    pg.sST.inc(nc.sync.dma_start(T["fi_sc"][tsl, ssl],
                                                 fi_t[:, :]), 16)
                    pg.sST.inc(nc.sync.dma_start(T["p_sc"][tsl, ssl], p_t[:, :]), 16)
                    if tap_fr is not None:
                        pg.sST.inc(nc.sync.dma_start(tap_fr[tsl, ssl],
                                                     fr_t[:, :]), 16)
                    if tap_pp is not None:
                        pg.sST.inc(nc.sync.dma_start(tap_pp[tsl, ssl],
                                                     p_t[:, :]), 16)
                    pg.d1sf = pg.sST.n



def _phase_d2_corr(pg, T, n_tok, tap):
    nc = pg.nc
    phase_barrier(pg)
    MYT = n_tok // 2
    MCH = MYT // C
    tap_corr = tap("corr", [MYT, HID])
    tap_h = tap("hvec", [MYT, HID], BF16)

    with ExitStack() as ctx:
        E = ctx.enter_context
        icos = E(nc.sbuf_tensor("d2_icos", [128, 16, HID], BF16))
        alt = E(nc.sbuf_tensor("d2_alt", [128, HID], BF16))
        p_ch = E(nc.sbuf_tensor("d2_p", [128, NSPEC], BF16))
        pT = E(nc.sbuf_tensor("d2_pT", [128, 16, 128], BF16))
        fn_t = E(nc.sbuf_tensor("d2_fn", [128, 8], F32))
        corr = E(nc.sbuf_tensor("d2_corr", [128, HID], F32))
        mask = E(nc.sbuf_tensor("d2_mask", [128, HID], F32))
        h_bf = E(nc.sbuf_tensor("d2_h", [128, HID], BF16))
        m8 = E(nc.sbuf_tensor("d2_m8", [128, 8], F32))
        st8 = E(nc.sbuf_tensor("d2_st8", [128, 8], F32))
        e8 = E(nc.sbuf_tensor("d2_e8", [128, 8], F32))
        hT = E(nc.sbuf_tensor("d2_hT", [128, 32, 128], BF16))
        pb_t = E(nc.psum_tensor("d2_pt", [128, 1024], BF16))
        pb_c = [E(nc.psum_tensor(f"d2_pc{i}", [128, 512], F32)) for i in range(4)]

        nc.sync.wait_ge(pg.sST.h, pg.sST.n)
        ld = nc.sync.dma_start(icos[:, :, :],
                               T["tb_icos"][:, :].rearrange("(kt p) t -> p kt t",
                                                            p=128))
        pg.sLD.inc(ld, 16)
        pg.sLD.inc(nc.sync.dma_start(alt[:, :], T["tb_alt"][:, :]), 16)
        ldm = pg.sLD.n

        dvf = 0
        stf = 0
        for ci in range(MCH):
            tsl = slice(ci * C, (ci + 1) * C)
            nc.sync.wait_ge(pg.sPE.h, pg.sPE.n)
            nc.sync.wait_ge(pg.sDV.h, dvf)
            pg.sLD.inc(nc.sync.dma_start(p_ch[:, :], T["p_sc"][tsl, :]), 16)
            pg.sLD.inc(nc.sync.dma_start(fn_t[:, :], T["fn_sc"][tsl, :]), 16)
            pg.sLD.wait(nc.tensor)
            pg.sLD.wait(nc.vector)
            nc.vector.wait_ge(pg.sLD.h, ldm)
            pn = nc.vector.tensor_tensor(fn_t[:, 1:2], fn_t[:, 0:1], fn_t[:, 0:1],
                                         ALU.mult)
            pg.sDV.inc(pn)
            pg.sDV.wait(nc.vector)
            _transpose_batch(pg, [(p_ch[:, i * 128:(i + 1) * 128], pT[:, i, :])
                                  for i in range(16)], pb_t)
            pg.sDV.wait(nc.tensor)
            for s in range(8):
                bank = s % 4
                ssl = slice(s * 512, (s + 1) * 512)
                nc.tensor.wait_ge(pg.sDV.h, pg.d2cf[bank]
                                  if hasattr(pg, "d2cf") else 0)
                for kt in range(16):
                    mm = nc.tensor.matmul(pb_c[bank][:, :], pT[:, kt, :],
                                          icos[:, kt, ssl], start=(kt == 0),
                                          stop=(kt == 15))
                pg.sPE.inc(mm)
                pg.sPE.wait(nc.vector)
                cv = nc.vector.scalar_tensor_tensor(corr[:, ssl], alt[:, ssl],
                                                    fn_t[:, 1:2], pb_c[bank][:, :],
                                                    ALU.mult, ALU.add)
                pg.sDV.inc(cv)
                if not hasattr(pg, "d2cf"):
                    pg.d2cf = [0, 0, 0, 0]
                pg.d2cf[bank] = pg.sDV.n
            if tap_corr is not None:
                pg.sDV.wait(nc.sync)
                pg.sST.inc(nc.sync.dma_start(tap_corr[tsl, :], corr[:, :]), 16)
            m8i = nc.vector.max(out=m8[:, :], in_=corr[:, :])
            pg.sDV.inc(m8i)
            pg.sDV.wait(nc.vector)
            nc.vector.tensor_scalar(st8[:, 0:1], m8[:, 0:1], -1.0, None, ALU.mult)
            nv = nc.vector.tensor_scalar(mask[:, :], corr[:, :], m8[:, 7:8], None,
                                         ALU.is_ge)
            pg.sDV.inc(nv)
            pg.sDV.wait(nc.scalar)
            ac = nc.scalar.activation(e8[:, :], m8[:, :], AF.Exp,
                                      bias=st8[:, 0:1], scale=1.0,
                                      accum_out=st8[:, 2:3])
            pg.sAC.inc(ac)
            ac2 = nc.scalar.activation(corr[:, :], corr[:, :], AF.Exp,
                                       bias=st8[:, 0:1], scale=1.0)
            pg.sAC.inc(ac2)
            pg.sAC.wait(nc.vector)
            rc = nc.vector.reciprocal(st8[:, 3:4], st8[:, 2:3])
            pg.sDV.inc(rc)
            pg.sDV.wait(nc.vector)
            hv = nc.vector.scalar_tensor_tensor(h_bf[:, :], corr[:, :],
                                                st8[:, 3:4], mask[:, :],
                                                ALU.mult, ALU.mult)
            pg.sDV.inc(hv)
            if tap_h is not None:
                pg.sDV.wait(nc.sync)
                pg.sST.inc(nc.sync.dma_start(tap_h[tsl, :], h_bf[:, :]), 16)
            pg.sDV.wait(nc.tensor)
            nc.vector.wait_ge(pg.sST.h, stf)
            _transpose_batch(pg, [(h_bf[:, i * 128:(i + 1) * 128], hT[:, i, :])
                                  for i in range(32)], pb_t)
            dvf = pg.sDV.n
            pg.sDV.wait(nc.sync)
            st = nc.sync.dma_start(T["hT_sc"][ci].rearrange("ft p t -> p ft t"),
                                   hT[:, :, :])
            pg.sST.inc(st, 16)
            stf = pg.sST.n



D3_MODE = "full"


def _phase_d3_gprod(pg, T, n_tok, tap):
    nc = pg.nc
    phase_barrier(pg)
    MYT = n_tok // 2
    MCH = MYT // C
    GRP = min(8, MCH)
    NGRP = MCH // GRP

    with ExitStack() as ctx:
        E = ctx.enter_context
        hT = E(nc.sbuf_tensor("d3_hT", [128, GRP, 32, 128], BF16))
        dcs = [E(nc.sbuf_tensor(f"d3_dc{i}", [128, 32, 512], BF16)) for i in range(2)]
        dss = [E(nc.sbuf_tensor(f"d3_ds{i}", [128, 32, 512], BF16)) for i in range(2)]
        fr_t = E(nc.sbuf_tensor("d3_fr", [128, 512], F32))
        fi_t = E(nc.sbuf_tensor("d3_fi", [128, 512], F32))
        t1 = E(nc.sbuf_tensor("d3_t1", [128, 512], F32))
        t2 = E(nc.sbuf_tensor("d3_t2", [128, 512], F32))
        r_t = E(nc.sbuf_tensor("d3_r", [128, 512], BF16))
        i_t = E(nc.sbuf_tensor("d3_i", [128, 512], BF16))
        fn_t = E(nc.sbuf_tensor("d3_fn", [128, 8], F32))
        hn_t = E(nc.sbuf_tensor("d3_hn", [128, 8], F32))
        pb_r = [E(nc.psum_tensor(f"d3_pr{i}", [128, 512], F32)) for i in range(2)]
        pb_i = [E(nc.psum_tensor(f"d3_pi{i}", [128, 512], F32)) for i in range(2)]
        pb_n = E(nc.psum_tensor("d3_pn", [128, 8], F32))

        nc.sync.wait_ge(pg.sST.h, pg.sST.n)
        wfd = [0, 0]
        for g in range(NGRP):
            nc.sync.wait_ge(pg.sPE.h, pg.sPE.n)
            for cc in range(GRP):
                ld = nc.sync.dma_start(
                    hT[:, cc, :, :],
                    T["hT_sc"][g * GRP + cc].rearrange("ft p t -> p ft t"))
                pg.sLD.inc(ld, 16)
            pg.sLD.wait(nc.tensor)
            for cc in range(GRP):
                if D3_MODE == "strips":
                    break
                ci = g * GRP + cc
                tsl = slice(ci * C, (ci + 1) * C)
                nc.sync.wait_ge(pg.sDV.h, pg.d3nf if hasattr(pg, "d3nf") else 0)
                pg.sLD.inc(nc.sync.dma_start(fn_t[:, :], T["fn_sc"][tsl, :]), 16)
                ldm = pg.sLD.n
                nc.tensor.wait_ge(pg.sDV.h, pg.d3nf if hasattr(pg, "d3nf") else 0)
                for kt in range(32):
                    mm = nc.tensor.matmul(pb_n[:, 0:1], hT[:, cc, kt, :],
                                          pg.dnyq_b[:, kt, :],
                                          start=(kt == 0), stop=(kt == 31))
                pg.sPE.inc(mm)
                pg.sPE.wait(nc.vector)
                nc.vector.wait_ge(pg.sLD.h, ldm)
                nc.vector.wait_ge(pg.sST.h, pg.d3nsf if hasattr(pg, "d3nsf") else 0)
                nc.vector.tensor_copy(hn_t[:, 0:8], fn_t[:, 0:8])
                hn = nc.vector.tensor_tensor(hn_t[:, 0:1], fn_t[:, 0:1],
                                             pb_n[:, 0:1], ALU.mult)
                pg.sDV.inc(hn)
                pg.d3nf = pg.sDV.n
                pg.sDV.wait(nc.sync)
                pg.sST.inc(nc.sync.dma_start(T["rn_sc"][tsl, :], hn_t[:, :]), 16)
                pg.d3nsf = pg.sST.n
            for s in range(4):
                if D3_MODE == "nyq":
                    break
                spar = s % 2
                ssl = slice(s * 512, (s + 1) * 512)
                nc.gpsimd.wait_ge(pg.sPE.h, wfd[spar])
                pg.sLG.inc(nc.gpsimd.dma_start(
                    dcs[spar][:, :, :],
                    T["tb_dcos"][:, ssl].rearrange("(kt p) n -> p kt n", p=128)), 16)
                pg.sLG.inc(nc.gpsimd.dma_start(
                    dss[spar][:, :, :],
                    T["tb_dsin"][:, ssl].rearrange("(kt p) n -> p kt n", p=128)), 16)
                lg_mark = pg.sLG.n
                nc.tensor.wait_ge(pg.sLG.h, lg_mark)
                for cc in range(GRP):
                    ci = g * GRP + cc
                    tsl = slice(ci * C, (ci + 1) * C)
                    bk = cc % 2
                    if not hasattr(pg, "d3ef"):
                        pg.d3ef = [0, 0]
                    nc.sync.wait_ge(pg.sDV.h, pg.d3ef[bk])
                    pg.sLD.inc(nc.sync.dma_start(fr_t[:, :], T["fr_sc"][tsl, ssl]),
                               16)
                    pg.sLD.inc(nc.sync.dma_start(fi_t[:, :], T["fi_sc"][tsl, ssl]),
                               16)
                    ldm = pg.sLD.n
                    nc.tensor.wait_ge(pg.sDV.h, pg.d3ef[bk])
                    for kt in range(32):
                        nc.tensor.matmul(pb_r[bk][:, :], hT[:, cc, kt, :],
                                         dcs[spar][:, kt, :],
                                         start=(kt == 0), stop=(kt == 31))
                    for kt in range(32):
                        mm = nc.tensor.matmul(pb_i[bk][:, :], hT[:, cc, kt, :],
                                              dss[spar][:, kt, :],
                                              start=(kt == 0), stop=(kt == 31))
                    pg.sPE.inc(mm)
                    pg.sPE.wait(nc.vector)
                    nc.vector.wait_ge(pg.sLD.h, ldm)
                    nc.vector.wait_ge(pg.sST.h, pg.d3sf if hasattr(pg, "d3sf") else 0)
                    nc.vector.tensor_tensor(t1[:, :], fr_t[:, :], pb_r[bk][:, :],
                                            ALU.mult)
                    nc.vector.tensor_tensor(t2[:, :], fi_t[:, :], pb_i[bk][:, :],
                                            ALU.mult)
                    nc.vector.tensor_tensor(r_t[:, :], t1[:, :], t2[:, :],
                                            ALU.subtract)
                    nc.vector.tensor_tensor(t1[:, :], fr_t[:, :], pb_i[bk][:, :],
                                            ALU.mult)
                    nc.vector.tensor_tensor(t2[:, :], fi_t[:, :], pb_r[bk][:, :],
                                            ALU.mult)
                    ev = nc.vector.tensor_tensor(i_t[:, :], t1[:, :], t2[:, :],
                                                 ALU.add)
                    pg.sDV.inc(ev)
                    pg.d3ef[bk] = pg.sDV.n
                    pg.sDV.wait(nc.sync)
                    pg.sST.inc(nc.sync.dma_start(T["r_sc"][tsl, ssl], r_t[:, :]), 16)
                    pg.sST.inc(nc.sync.dma_start(T["i_sc"][tsl, ssl], i_t[:, :]), 16)
                    pg.d3sf = pg.sST.n



def _phase_d4_inv(pg, T, n_tok, tap):
    nc = pg.nc
    phase_barrier(pg)
    MYT = n_tok // 2
    MCH = MYT // C
    tap_a1 = tap("attn1", [MYT, HID])

    with ExitStack() as ctx:
        E = ctx.enter_context
        icos = E(nc.sbuf_tensor("d4_icos", [128, 16, NSPEC], BF16))
        isin = E(nc.sbuf_tensor("d4_isin", [128, 16, NSPEC], BF16))
        alt = E(nc.sbuf_tensor("d4_alt", [128, HID], BF16))
        r_ch = E(nc.sbuf_tensor("d4_r", [128, NSPEC], BF16))
        i_ch = E(nc.sbuf_tensor("d4_i", [128, NSPEC], BF16))
        rT = E(nc.sbuf_tensor("d4_rT", [128, 16, 128], BF16))
        iT = E(nc.sbuf_tensor("d4_iT", [128, 16, 128], BF16))
        rn_t = E(nc.sbuf_tensor("d4_rn", [128, 8], F32))
        x1_t = E(nc.sbuf_tensor("d4_x1", [128, NSPEC], F32))
        o_t = E(nc.sbuf_tensor("d4_o", [128, 512], F32))
        pb_t = E(nc.psum_tensor("d4_pt", [128, 1024], BF16))
        pb_m = [E(nc.psum_tensor(f"d4_pm{i}", [128, 512], F32)) for i in range(4)]

        nc.sync.wait_ge(pg.sST.h, pg.sST.n)
        for half in range(2):
            hssl = slice(half * NSPEC, (half + 1) * NSPEC)
            nc.sync.wait_ge(pg.sPE.h, pg.sPE.n)
            nc.sync.wait_ge(pg.sDV.h, pg.sDV.n)
            ld = nc.sync.dma_start(
                icos[:, :, :],
                T["tb_icos"][:, hssl].rearrange("(kt p) t -> p kt t", p=128))
            pg.sLD.inc(ld, 16)
            ld = nc.sync.dma_start(
                isin[:, :, :],
                T["tb_isin"][:, hssl].rearrange("(kt p) t -> p kt t", p=128))
            pg.sLD.inc(ld, 16)
            pg.sLD.inc(nc.sync.dma_start(alt[:, :], T["tb_alt"][:, :]), 16)
            ldtab = pg.sLD.n
            dvf = 0
            for ci in range(MCH):
                tsl = slice(ci * C, (ci + 1) * C)
                nc.sync.wait_ge(pg.sPE.h, pg.sPE.n)
                nc.sync.wait_ge(pg.sDV.h, dvf)
                pg.sLD.inc(nc.sync.dma_start(r_ch[:, :], T["r_sc"][tsl, :]), 16)
                pg.sLD.inc(nc.sync.dma_start(i_ch[:, :], T["i_sc"][tsl, :]), 16)
                pg.sLD.inc(nc.sync.dma_start(rn_t[:, :], T["rn_sc"][tsl, :]), 16)
                pg.sLD.inc(nc.sync.dma_start(x1_t[:, :], T["x1_sc"][tsl, hssl]), 16)
                pg.sLD.wait(nc.tensor)
                pg.sLD.wait(nc.vector)
                nc.vector.wait_ge(pg.sLD.h, ldtab)
                _transpose_batch(pg, [(r_ch[:, i * 128:(i + 1) * 128], rT[:, i, :])
                                      for i in range(16)]
                                 + [(i_ch[:, i * 128:(i + 1) * 128], iT[:, i, :])
                                    for i in range(16)], pb_t)
                dvf = pg.sDV.n
                pg.sDV.wait(nc.tensor)
                for s in range(4):
                    bank = s % 4
                    ssl = slice(s * 512, (s + 1) * 512)
                    osl = slice(half * NSPEC + s * 512, half * NSPEC + (s + 1) * 512)
                    nc.tensor.wait_ge(pg.sDV.h, pg.d4f[bank]
                                      if hasattr(pg, "d4f") else 0)
                    for kt in range(16):
                        nc.tensor.matmul(pb_m[bank][:, :], rT[:, kt, :],
                                         icos[:, kt, ssl], start=(kt == 0),
                                         stop=False)
                    for kt in range(16):
                        mm = nc.tensor.matmul(pb_m[bank][:, :], iT[:, kt, :],
                                              isin[:, kt, ssl],
                                              start=False, stop=(kt == 15))
                    pg.sPE.inc(mm)
                    pg.sPE.wait(nc.vector)
                    nc.vector.wait_ge(pg.sST.h, pg.d4sf
                                      if hasattr(pg, "d4sf") else 0)
                    nc.vector.scalar_tensor_tensor(o_t[:, :], alt[:, osl],
                                                   rn_t[:, 0:1], pb_m[bank][:, :],
                                                   ALU.mult, ALU.add)
                    ev = nc.vector.tensor_tensor(o_t[:, :], o_t[:, :],
                                                 x1_t[:, ssl], ALU.add)
                    pg.sDV.inc(ev)
                    if not hasattr(pg, "d4f"):
                        pg.d4f = [0, 0, 0, 0]
                    pg.d4f[bank] = pg.sDV.n
                    pg.sDV.wait(nc.sync)
                    st = nc.sync.dma_start(T["out_my"][tsl, osl], o_t[:, :])
                    pg.sST.inc(st, 16)
                    pg.d4sf = pg.sST.n
                    if tap_a1 is not None:
                        pg.sST.inc(nc.sync.dma_start(tap_a1[tsl, osl],
                                                     o_t[:, :]), 16)



def _phase_d5_ffn1(pg, T, n_tok, tap):
    nc = pg.nc
    phase_barrier(pg)
    MYT = n_tok // 2
    MCH = MYT // C
    TSW = min(512, MYT)
    NTS = MYT // TSW
    CPT = TSW // C
    tap_f1 = tap("ffn1", [MYT, HID])

    with ExitStack() as ctx:
        E = ctx.enter_context
        u1b = E(nc.sbuf_tensor("d5_u1", [128, 32, RANK], BF16))
        xns = E(nc.sbuf_tensor("d5_xns", [128, 32, TSW], BF16))
        pT = E(nc.sbuf_tensor("d5_pT", [128, 8, MYT], BF16))
        pb_m = [E(nc.psum_tensor(f"d5a_pm{i}", [128, TSW], F32)) for i in range(4)]

        nc.gpsimd.wait_ge(pg.sST.h, pg.sST.n)
        ldw = nc.gpsimd.dma_start(
            u1b[:, :, :], T["u1w"][:, :].rearrange("(kt p) r -> p kt r", p=128))
        pg.sLG.inc(ldw, 16)
        dvf = [0, 0, 0, 0]
        for ts in range(NTS):
            nc.sync.wait_ge(pg.sPE.h, pg.sPE.n)
            for cc in range(CPT):
                ld = nc.sync.dma_start(
                    xns[:, :, cc * C:(cc + 1) * C],
                    T["xn1T_sc"][ts * CPT + cc].rearrange("ft p t -> p ft t"))
                pg.sLD.inc(ld, 16)
            pg.sLD.wait(nc.tensor)
            pg.sLG.wait(nc.tensor)
            for rt in range(8):
                bank = rt % 4
                nc.tensor.wait_ge(pg.sDV.h, dvf[bank])
                for kt in range(32):
                    mm = nc.tensor.matmul(pb_m[bank][:, :],
                                          u1b[:, kt, rt * 128:(rt + 1) * 128],
                                          xns[:, kt, :], start=(kt == 0),
                                          stop=(kt == 31))
                pg.sPE.inc(mm)
                pg.sPE.wait(nc.vector)
                ev = nc.vector.tensor_copy(pT[:, rt, ts * TSW:(ts + 1) * TSW],
                                           pb_m[bank][:, :])
                pg.sDV.inc(ev)
                dvf[bank] = pg.sDV.n

        phase_barrier(pg)
        with ExitStack() as ctx2:
            E2 = ctx2.enter_context
            v1g = [E2(nc.sbuf_tensor(f"d5_v1g{i}", [128, 8, 128], BF16))
                   for i in range(2)]
            v1u = [E2(nc.sbuf_tensor(f"d5_v1u{i}", [128, 8, 128], BF16))
                   for i in range(2)]
            silu_t = E2(nc.sbuf_tensor("d5_silu", [128, TSW], F32))
            act_t = E2(nc.sbuf_tensor("d5_act", [128, TSW], BF16))
            pb_g = [E2(nc.psum_tensor(f"d5_pg{i}", [128, TSW], F32)) for i in range(2)]
            pb_up = [E2(nc.psum_tensor(f"d5_pu{i}", [128, TSW], F32)) for i in range(2)]
            pg.b1r_b = E2(nc.sbuf_tensor("b1r_b", [1, 2 * INTER], BF16))
            pg.sLG.inc(nc.gpsimd.dma_start(pg.b1r_b[:, :], T["b1r"][:, :]), 16)
            pg.sLG.wait(nc.tensor)

            wf = [0, 0]
            acf = [0, 0]
            dv2 = [0, 0]
            stf = 0
            for gi in range(NHT):
                spar = gi % 2
                nc.gpsimd.wait_ge(pg.sPE.h, wf[spar])
                pg.sLG.inc(nc.gpsimd.dma_start(
                    v1g[spar][:, :, :],
                    T["v1w"][:, gi * 128:(gi + 1) * 128].rearrange(
                        "(kt p) n -> p kt n", p=128)), 16)
                pg.sLG.inc(nc.gpsimd.dma_start(
                    v1u[spar][:, :, :],
                    T["v1w"][:, INTER + gi * 128:INTER + (gi + 1) * 128].rearrange(
                        "(kt p) n -> p kt n", p=128)), 16)
                lg_mark = pg.sLG.n
                nc.tensor.wait_ge(pg.sLG.h, lg_mark)
                for ts in range(NTS):
                    tsl = slice(ts * TSW, (ts + 1) * TSW)
                    bk = (gi * NTS + ts) % 2
                    nc.tensor.wait_ge(pg.sAC.h, acf[bk])
                    nc.tensor.wait_ge(pg.sDV.h, dv2[bk])
                    for kt in range(8):
                        nc.tensor.matmul(pb_g[bk][:, :], v1g[spar][:, kt, :],
                                         pT[:, kt, tsl], start=(kt == 0),
                                         stop=False)
                    nc.tensor.matmul(pb_g[bk][:, :],
                                     pg.b1r_b[0:1, gi * 128:(gi + 1) * 128],
                                     pg.ones_bf[0:1, 0:TSW], start=False, stop=True)
                    for kt in range(8):
                        nc.tensor.matmul(pb_up[bk][:, :], v1u[spar][:, kt, :],
                                         pT[:, kt, tsl], start=(kt == 0),
                                         stop=False)
                    mm = nc.tensor.matmul(
                        pb_up[bk][:, :],
                        pg.b1r_b[0:1, INTER + gi * 128:INTER + (gi + 1) * 128],
                        pg.ones_bf[0:1, 0:TSW], start=False, stop=True)
                    pg.sPE.inc(mm)
                    wf[spar] = pg.sPE.n
                    pg.sPE.wait(nc.scalar)
                    nc.scalar.wait_ge(pg.sDV.h, dv2[1 - bk])
                    sl = nc.scalar.activation(silu_t[:, :], pb_g[bk][:, :], AF.Silu,
                                              bias=0.0, scale=1.0)
                    pg.sAC.inc(sl)
                    acf[bk] = pg.sAC.n
                    pg.sAC.wait(nc.vector)
                    nc.vector.wait_ge(pg.sST.h, stf)
                    av = nc.vector.tensor_tensor(act_t[:, :], silu_t[:, :],
                                                 pb_up[bk][:, :], ALU.mult)
                    pg.sDV.inc(av)
                    dv2[bk] = pg.sDV.n
                    pg.sDV.wait(nc.sync)
                    st = nc.sync.dma_start(T["act1T_sc"][gi, ts], act_t[:, :])
                    pg.sST.inc(st, 16)
                    stf = pg.sST.n

    phase_barrier(pg)
    with ExitStack() as zctx:
      ZT = zctx.enter_context(nc.sbuf_tensor("d5_ZT", [128, 8, MYT], BF16))
      with ExitStack() as ctx:
        E = ctx.enter_context
        a1s = E(nc.sbuf_tensor("d5_a1s", [128, NHT, TSW], BF16))
        u2b = [E(nc.sbuf_tensor(f"d5_u2{i}", [128, NHT, 128], BF16))
               for i in range(2)]
        pb_m = [E(nc.psum_tensor(f"d5c_pm{i}", [128, TSW], F32)) for i in range(4)]

        nc.sync.wait_ge(pg.sST.h, pg.sST.n)
        wf = [0, 0]
        dvf = [0, 0, 0, 0]
        for ts in range(NTS):
            nc.sync.wait_ge(pg.sPE.h, pg.sPE.n)
            ld = nc.sync.dma_start(a1s[:, :, :],
                                   T["act1T_sc"][:, ts].rearrange("h p t -> p h t"))
            pg.sLD.inc(ld, 16)
            for rt in range(8):
                spar = rt % 2
                bank = rt % 4
                nc.gpsimd.wait_ge(pg.sPE.h, wf[spar])
                ldw = nc.gpsimd.dma_start(
                    u2b[spar][:, :, :],
                    T["u2w"][:, rt * 128:(rt + 1) * 128].rearrange(
                        "(kt p) n -> p kt n", p=128))
                pg.sLG.inc(ldw, 16)
                pg.sLD.wait(nc.tensor)
                pg.sLG.wait(nc.tensor)
                nc.tensor.wait_ge(pg.sDV.h, dvf[bank])
                for kt in range(NHT):
                    mm = nc.tensor.matmul(pb_m[bank][:, :], u2b[spar][:, kt, :],
                                          a1s[:, kt, :], start=(kt == 0),
                                          stop=(kt == NHT - 1))
                pg.sPE.inc(mm)
                wf[spar] = pg.sPE.n
                pg.sPE.wait(nc.vector)
                ev = nc.vector.tensor_copy(ZT[:, rt, ts * TSW:(ts + 1) * TSW],
                                           pb_m[bank][:, :])
                pg.sDV.inc(ev)
                dvf[bank] = pg.sDV.n

      phase_barrier(pg)
      if True:
        with ExitStack() as ctx2:
            E2 = ctx2.enter_context
            v2b = E2(nc.sbuf_tensor("d5_v2", [128, 8, HID], BF16))
            o_t = E2(nc.sbuf_tensor("d5_o", [128, 512], F32))
            pb_o = [E2(nc.psum_tensor(f"d5d_pm{i}", [128, 512], F32))
                    for i in range(4)]
            pg.b2r_b = E2(nc.sbuf_tensor("b2r_b", [1, HID], BF16))
            pg.sLG.inc(nc.gpsimd.dma_start(pg.b2r_b[:, :], T["b2r"][:, :]), 16)

            ldw = nc.gpsimd.dma_start(
                v2b[:, :, :], T["v2w"][:, :].rearrange("(kt p) n -> p kt n", p=128))
            pg.sLG.inc(ldw, 16)
            pg.sLG.wait(nc.tensor)
            nc.gpsimd.wait_ge(pg.sST.h, pg.sST.n)
            dvf2 = [0, 0, 0, 0]
            sgf = 0
            for ci in range(MCH):
                tsl = slice(ci * C, (ci + 1) * C)
                for s in range(8):
                    bank = s % 4
                    ssl = slice(s * 512, (s + 1) * 512)
                    nc.tensor.wait_ge(pg.sDV.h, dvf2[bank])
                    for kt in range(8):
                        nc.tensor.matmul(pb_o[bank][:, :], ZT[:, kt, tsl],
                                         v2b[:, kt, ssl], start=(kt == 0),
                                         stop=False)
                    mm = nc.tensor.matmul(pb_o[bank][:, :], pg.ones_bf[0:1, 0:128],
                                          pg.b2r_b[0:1, ssl], start=False, stop=True)
                    pg.sPE.inc(mm)
                    pg.sPE.wait(nc.vector)
                    nc.vector.wait_ge(pg.sSG.h, sgf)
                    ev = nc.vector.tensor_copy(o_t[:, :], pb_o[bank][:, :])
                    pg.sDV.inc(ev)
                    dvf2[bank] = pg.sDV.n
                    pg.sDV.wait(nc.gpsimd)
                    st = nc.gpsimd.dma_start(T["out_my"][tsl, ssl], o_t[:, :],
                                             accum_op=ALU.add)
                    pg.sSG.inc(st, 16)
                    sgf = pg.sSG.n
                    if tap_f1 is not None:
                        pg.sDV.wait(nc.sync)
                        pg.sST.inc(nc.sync.dma_start(tap_f1[tsl, ssl],
                                                     o_t[:, :]), 16)



_CACHE = {}
_LAST_RES = None


def kernel(**inputs):
    n_tok = int(inputs["L0_attn_hidden_states"].shape[1])
    key = ("full", n_tok)
    if key not in _CACHE:
        _CACHE[key] = build_program(n_tok=n_tok)
    nc, T, taps = _CACHE[key]

    x = np.ascontiguousarray(inputs["L0_attn_hidden_states"], dtype=np.float32)
    tb = make_tables(n_tok)
    MYT = n_tok // 2
    ct = np.ascontiguousarray

    shared = {
        "wo": ct(inputs["L0_attn_Wo"]),
        "w1qT": ct(inputs["L0_ffn_w1q"].T),
        "w3qT": ct(inputs["L0_ffn_w3q"].T),
        "w2qT": ct(inputs["L0_ffn_w2q"].T),
        "a1T": ct(inputs["L0_ffn_a1"].T),
        "a3T": ct(inputs["L0_ffn_a3"].T),
        "a2T": ct(inputs["L0_ffn_a2"].T),
        "b1T": ct(inputs["L0_ffn_b1"].T),
        "b3T": ct(inputs["L0_ffn_b3"].T),
        "b2T": ct(inputs["L0_ffn_b2"].T),
        "u1w": ct(inputs["L1_ffn_U1"]),
        "v1w": ct(inputs["L1_ffn_V1"]),
        "u2w": ct(inputs["L1_ffn_U2"]),
        "v2w": ct(inputs["L1_ffn_V2"]),
        "b1r": ct(inputs["L1_ffn_b1"].reshape(1, -1)),
        "b2r": ct(inputs["L1_ffn_b2"].reshape(1, -1)),
    }
    shared.update(tb)

    in_maps = []
    for c in range(8):
        b, g = c // 2, c % 2
        m = dict(shared)
        m["x_b"] = ct(x[b])
        m["x_my"] = ct(x[b, g * MYT:(g + 1) * MYT])
        m["wq"] = ct(inputs["L0_attn_Wq"][:, g * 2048:(g + 1) * 2048])
        m["wk"] = ct(inputs["L0_attn_Wk"][:, g * 512:(g + 1) * 512])
        m["wv"] = ct(inputs["L0_attn_Wv"][:, g * 512:(g + 1) * 512])
        in_maps.append(m)

    res = run_bass_kernel_spmd(nc, in_maps, core_ids=list(range(8)))
    global _LAST_RES
    _LAST_RES = res
    out = np.empty((4, n_tok, HID), dtype=np.float32)
    for c in range(8):
        b, g = c // 2, c % 2
        out[b, g * MYT:(g + 1) * MYT] = res.results[c]["out_my"]
    return out


# version 04
# speedup vs baseline: 1.0798x; 1.0237x over previous
import math
from contextlib import ExitStack

import numpy as np
import ml_dtypes

import concourse.bass as bass
import concourse.mybir as mybir
from concourse.bass_utils import run_bass_kernel_spmd

F32 = mybir.dt.float32
BF16 = mybir.dt.bfloat16
AF = mybir.ActivationFunctionType
ALU = mybir.AluOpType
AX = mybir.AxisListType
BF = ml_dtypes.bfloat16

HID = 4096
H, KVH, D, W = 32, 8, 128, 64
THETA = 10000.0
C = 128
FFN_H = 11008
RANK = 1024
INTER = 11008
NSPEC = 2048
EPS = 1e-5
NEG = -1.0e30
RSD = 1.0 / math.sqrt(D)
NHT = FFN_H // 128


def make_tables(n_tok):
    t = {}
    t["tb_ident"] = np.eye(128, dtype=BF)
    iu = np.arange(C)
    t["tb_utri"] = (iu[:, None] <= iu[None, :]).astype(np.float32)
    a = np.arange(C)[:, None]
    wp = np.arange(C + W)[None, :]
    base = (wp >= a) & (wp <= a + W)
    m = np.where(base, 0.0, NEG).astype(np.float32)
    t["tb_swam"] = m
    m0 = m.copy()
    m0[:, :W] = NEG
    t["tb_swam0"] = m0
    half = np.arange(D // 2, dtype=np.float64)
    inv = 1.0 / THETA ** (2.0 * half / D)
    pos = np.arange(n_tok, dtype=np.float64)
    fr = pos[:, None] * inv[None, :]
    emb = np.concatenate([fr, fr], axis=-1)
    cos, sin = np.cos(emb), np.sin(emb)
    t["tb_rcq"] = (cos * RSD).astype(BF)
    t["tb_rsq"] = (sin * RSD).astype(BF)
    t["tb_rck"] = cos.astype(BF)
    t["tb_rsk"] = sin.astype(BF)
    L = HID
    kk = np.arange(NSPEC, dtype=np.float64)
    tt = np.arange(L, dtype=np.float64)
    ang = 2.0 * np.pi * np.outer(tt, kk) / L
    t["tb_dcos"] = np.cos(ang).astype(BF)
    t["tb_dsin"] = np.sin(ang).astype(BF)
    t["tb_dnyq"] = np.ascontiguousarray(np.cos(np.pi * tt).astype(BF).reshape(L // 128, 128).T)
    ck = np.full(NSPEC, 2.0 / L)
    ck[0] = 1.0 / L
    angi = 2.0 * np.pi * np.outer(kk, tt) / L
    t["tb_icos"] = (ck[:, None] * np.cos(angi)).astype(BF)
    t["tb_isin"] = (-ck[:, None] * np.sin(angi)).astype(BF)
    t["tb_alt"] = np.ascontiguousarray(
        np.broadcast_to(((-1.0) ** tt / L).astype(BF), (128, L)))
    return t


class Sem:
    def __init__(self, handle):
        self.h = handle
        self.n = 0

    def inc(self, instr, k=1):
        instr.then_inc(self.h, k)
        self.n += k
        return instr

    def wait(self, eng, val=None):
        eng.wait_ge(self.h, self.n if val is None else val)


class P:
    pass


def _mk_prog(nc, ctx):
    pg = P()
    pg.nc = nc
    pg.ctx = ctx
    E = ctx.enter_context
    for nm in ("sLD", "sLG", "sPE", "sDV", "sAC", "sST", "sSG", "sCC"):
        setattr(pg, nm, Sem(E(nc.semaphore(nm))))
    pg.par = nc.partition_id() % 2
    return pg


def sbuf(pg, name, shape, dtype):
    return pg.ctx.enter_context(pg.nc.sbuf_tensor(name, shape, dtype))


def br_load(pg, q, sem, dst, src_lo, src_hi):
    with q.If(pg.par):
        q.dma_start(dst, src_hi).then_inc(sem.h, 16)
    with q.Else():
        q.dma_start(dst, src_lo).then_inc(sem.h, 16)
    sem.n += 16


def phase_barrier(pg):
    nc = pg.nc
    for eng in (nc.sync, nc.gpsimd, nc.vector, nc.scalar, nc.tensor):
        for s in (pg.sPE, pg.sDV, pg.sAC, pg.sST, pg.sSG, pg.sLD, pg.sLG):
            eng.wait_ge(s.h, s.n)



def build_program(n_tok=4096, debug_taps=(), single_core=False, stop_after=None):
    nc = bass.Bass("TRN2", target_bir_lowering=False)
    NCH = n_tok // C
    MYT = n_tok // 2
    MCH = MYT // C
    QCOLS = (H // 2) * D
    KCOLS = (KVH // 2) * D
    TSW = min(512, MYT)
    NTS = MYT // TSW

    dt = nc.dram_tensor
    T = {}
    for nm, shp in [
        ("x_b", [n_tok, HID]), ("wq", [HID, QCOLS]), ("wk", [HID, KCOLS]),
        ("wv", [HID, KCOLS]), ("wo", [HID, HID]),
        ("w1qT", [1024, FFN_H]), ("w3qT", [1024, FFN_H]), ("w2qT", [2752, HID]),
        ("a1T", [HID, 2]), ("a3T", [HID, 2]), ("a2T", [FFN_H, 2]),
        ("b1T", [2, FFN_H]), ("b3T", [2, FFN_H]), ("b2T", [2, HID]),
        ("u1w", [HID, RANK]), ("v1w", [RANK, 2 * INTER]), ("u2w", [INTER, RANK]),
        ("v2w", [RANK, HID]), ("b1r", [1, 2 * INTER]), ("b2r", [1, HID]),
        ("x_my", [MYT, HID]),
        ("tb_utri", [C, C]), ("tb_swam", [C, C + W]), ("tb_swam0", [C, C + W]),
    ]:
        T[nm] = dt(nm, shp, F32, kind="ExternalInput")
    for nm, shp in [
        ("tb_ident", [128, 128]), ("tb_alt", [128, HID]),
        ("tb_rcq", [n_tok, D]), ("tb_rsq", [n_tok, D]),
        ("tb_rck", [n_tok, D]), ("tb_rsk", [n_tok, D]),
        ("tb_dcos", [HID, NSPEC]), ("tb_dsin", [HID, NSPEC]), ("tb_dnyq", [128, HID // 128]),
        ("tb_icos", [NSPEC, HID]), ("tb_isin", [NSPEC, HID]),
    ]:
        T[nm] = dt(nm, shp, BF16, kind="ExternalInput")

    T["out_my"] = dt("out_my", [MYT, HID], F32, kind="ExternalOutput")

    T["xnT_sc"] = dt("xnT_sc", [NCH, 32, 128, C], BF16)
    T["qkv_sc"] = dt("qkv_sc", [NCH, C, QCOLS + 2 * KCOLS], F32)
    T["o_mine"] = dt("o_mine", [n_tok, QCOLS], BF16)
    T["o_all"] = dt("o_all", [2 * n_tok, QCOLS], BF16)
    T["x1_sc"] = dt("x1_sc", [MYT, HID], F32)
    T["xn1T_sc"] = dt("xn1T_sc", [MCH, 32, 128, C], BF16)
    T["fr_sc"] = dt("fr_sc", [MYT, NSPEC], F32)
    T["fi_sc"] = dt("fi_sc", [MYT, NSPEC], F32)
    T["fn_sc"] = dt("fn_sc", [MYT, 8], F32)
    T["p_sc"] = dt("p_sc", [MYT, NSPEC], BF16)
    T["hT_sc"] = dt("hT_sc", [MCH, 32, 128, C], BF16)
    T["r_sc"] = dt("r_sc", [MYT, NSPEC], BF16)
    T["i_sc"] = dt("i_sc", [MYT, NSPEC], BF16)
    T["rn_sc"] = dt("rn_sc", [MYT, 8], F32)
    T["act_sc"] = dt("act_sc", [MCH, 128, FFN_H], BF16)
    T["act1T_sc"] = dt("act1T_sc", [NHT, NTS, 128, TSW], BF16)

    taps = {}

    def tap(name, shape, dtype=F32):
        if name in debug_taps:
            taps[name] = nc.dram_tensor("tap_" + name, shape, dtype,
                                        kind="ExternalOutput")
            return taps[name]
        return None

    with ExitStack() as ctx:
        pg = _mk_prog(nc, ctx)
        _consts(pg, T)
        _phase_a(pg, T, n_tok, tap)
        _phase_b(pg, T, n_tok, tap)
        if single_core:
            nc.gpsimd.wait_ge(pg.sST.h, pg.sST.n)
            pg.sSG.inc(nc.gpsimd.dma_start(T["o_all"][0:n_tok, :],
                                           T["o_mine"][:, :]), 16)
            pg.sSG.inc(nc.gpsimd.dma_start(T["o_all"][n_tok:2 * n_tok, :],
                                           T["o_mine"][:, :]), 16)
            pg.cc_wait = (pg.sSG, pg.sSG.n)
        else:
            nc.gpsimd.wait_ge(pg.sST.h, pg.sST.n)
            cc = nc.gpsimd.collective_compute(
                "AllGather", ALU.bypass,
                replica_groups=[[0, 1], [2, 3], [4, 5], [6, 7]],
                ins=[T["o_mine"].ap().opt()],
                outs=[T["o_all"].ap().opt()],
            )
            pg.sCC.inc(cc, 1)
            pg.cc_wait = (pg.sCC, pg.sCC.n)
        for nm, fn in [("c_ffn0", _phase_c_ffn0), ("c_wo", _phase_c_wo),
                       ("d0", _phase_d0_ln), ("d1", _phase_d1_fwd),
                       ("d2", _phase_d2_corr), ("d3", _phase_d3_gprod),
                       ("d4", _phase_d4_inv), ("d5", _phase_d5_ffn1)]:
            if stop_after == "ab":
                break
            fn(pg, T, n_tok, tap)
            if stop_after == nm:
                break
        nc.sync.wait_ge(pg.sST.h, pg.sST.n)
        nc.sync.wait_ge(pg.sSG.h, pg.sSG.n)
        nc.gpsimd.wait_ge(pg.sSG.h, pg.sSG.n)

    return nc, T, taps


def _consts(pg, T):
    nc = pg.nc
    pg.ident = sbuf(pg, "ident", [128, 128], BF16)
    pg.utri = sbuf(pg, "utri", [C, C], F32)
    pg.swam = sbuf(pg, "swam", [C, C + W], F32)
    pg.swam0 = sbuf(pg, "swam0", [C, C + W], F32)
    pg.ones_bf = sbuf(pg, "ones_bf", [1, 512], BF16)
    pg.dnyq_b = sbuf(pg, "dnyq_b", [128, 32, 1], BF16)
    pg.cvals = sbuf(pg, "cvals", [128, 8], F32)

    pg.sLD.inc(nc.sync.dma_start(pg.ident[:, :], T["tb_ident"][:, :]), 16)
    pg.sLD.inc(nc.sync.dma_start(pg.utri[:, :], T["tb_utri"][:, :]), 16)
    pg.sLD.inc(nc.sync.dma_start(pg.swam[:, :], T["tb_swam"][:, :]), 16)
    pg.sLD.inc(nc.sync.dma_start(pg.swam0[:, :], T["tb_swam0"][:, :]), 16)
    pg.sLD.inc(nc.sync.dma_start(pg.dnyq_b[:, :, 0], T["tb_dnyq"][:, :]), 16)
    pg.sLD.wait(nc.vector)
    clist = [EPS, -1.0, 0.5, -0.5, 1.0, 0.0]
    for i, v in enumerate(clist):
        nc.vector.memset(pg.cvals[:, i:i + 1], v)
        nc.const_aps.register(F32, v, pg.cvals[:, i:i + 1])
    pg.sDV.inc(nc.vector.memset(pg.ones_bf[:, :], 1.0))
    pg.sDV.wait(nc.scalar)
    pg.sDV.wait(nc.tensor)


def _layernorm(pg, x_t, xn_bf, wrk, stat):
    nc = pg.nc
    n = x_t.shape[1]
    pg.sDV.wait(nc.scalar)
    nc.scalar.activation(wrk[:, :n], x_t[:, :], AF.Square, bias=0.0, scale=1.0,
                         accum_out=stat[:, 1:2])
    a2 = nc.scalar.activation(wrk[:, :n], x_t[:, :], AF.Copy, bias=0.0, scale=1.0,
                              accum_out=stat[:, 0:1])
    pg.sAC.inc(a2)
    pg.sAC.wait(nc.vector)
    nc.vector.tensor_scalar(stat[:, 2:3], stat[:, 0:1], 1.0 / n, None, ALU.mult)
    t0 = nc.vector.tensor_scalar(stat[:, 3:4], stat[:, 1:2], 1.0 / n, None, ALU.mult)
    pg.sDV.inc(t0)
    pg.sDV.wait(nc.vector)
    t1 = nc.vector.tensor_scalar(stat[:, 6:7], stat[:, 2:3], stat[:, 2:3], None,
                                 ALU.mult)
    pg.sDV.inc(t1)
    pg.sDV.wait(nc.vector)
    v = nc.vector.tensor_scalar(stat[:, 4:5], stat[:, 3:4], stat[:, 6:7], None,
                                ALU.subtract)
    pg.sDV.inc(v)
    pg.sDV.wait(nc.scalar)
    nc.scalar.activation(stat[:, 5:6], stat[:, 4:5], AF.Ln, bias=EPS, scale=1.0)
    a3 = nc.scalar.activation(stat[:, 5:6], stat[:, 5:6], AF.Exp, bias=0.0, scale=-0.5)
    pg.sAC.inc(a3)
    pg.sAC.wait(nc.vector)
    xn = nc.vector.tensor_scalar(xn_bf[:, :], x_t[:, :], stat[:, 2:3], stat[:, 5:6],
                                 ALU.subtract, ALU.mult)
    pg.sDV.inc(xn)


def _transpose_batch(pg, pairs, pbank, nper=8):
    nc = pg.nc
    i = 0
    while i < len(pairs):
        batch = pairs[i:i + nper]
        for j, (s, d) in enumerate(batch):
            mm = nc.tensor.transpose(pbank[:, j * 128:(j + 1) * 128], s,
                                     pg.ident[:, :])
        pg.sPE.inc(mm)
        pg.sPE.wait(nc.vector)
        for j, (s, d) in enumerate(batch):
            cp = nc.vector.tensor_copy(d, pbank[:, j * 128:(j + 1) * 128])
        pg.sDV.inc(cp)
        pg.sDV.wait(nc.tensor)
        i += nper



def _phase_a(pg, T, n_tok, tap):
    nc = pg.nc
    phase_barrier(pg)
    NCH = n_tok // C
    QCOLS = (H // 2) * D
    KCOLS = (KVH // 2) * D
    GRP = min(8, NCH)
    NGRP = NCH // GRP
    NSTR = (QCOLS + 2 * KCOLS) // 512

    tap_xn = tap("xn", [n_tok, HID], BF16)

    with ExitStack() as ctx:
        E = ctx.enter_context
        x_t = [E(nc.sbuf_tensor(f"pa_x{i}", [128, HID], F32)) for i in range(2)]
        wrk = E(nc.sbuf_tensor("pa_wrk", [128, HID], F32))
        stat = E(nc.sbuf_tensor("pa_stat", [128, 8], F32))
        xn_bf = E(nc.sbuf_tensor("pa_xnbf", [128, HID], BF16))
        xnT = E(nc.sbuf_tensor("pa_xnT", [128, GRP, 32, 128], BF16))
        wstr = [E(nc.sbuf_tensor(f"pa_w{i}", [128, 32, 512], BF16)) for i in range(2)]
        qkv_o = [E(nc.sbuf_tensor(f"pa_qo{i}", [128, 512], F32)) for i in range(4)]
        pb_t = E(nc.psum_tensor("pa_pt", [128, 1024], BF16))
        pb_mm = [E(nc.psum_tensor(f"pa_pm{i}", [128, 512], F32)) for i in range(4)]

        wsrc = [(T["wq"], s * 512) for s in range(QCOLS // 512)]
        wsrc.append((T["wk"], 0))
        wsrc.append((T["wv"], 0))

        lnfence = [0, 0]
        wfence = [0, 0]
        qfence = [0, 0, 0, 0]
        qstfence = [0, 0, 0, 0]
        for g in range(NGRP):
            grp_pe = pg.sPE.n
            nc.vector.wait_ge(pg.sPE.h, grp_pe)
            for cc in range(GRP):
                ci = g * GRP + cc
                par = ci % 2
                nc.sync.wait_ge(pg.sDV.h, lnfence[par])
                ld = nc.sync.dma_start(x_t[par][:, :],
                                       T["x_b"][ci * C:(ci + 1) * C, :])
                pg.sLD.inc(ld, 16)
                pg.sLD.wait(nc.scalar)
                pg.sLD.wait(nc.vector)
                _layernorm(pg, x_t[par], xn_bf, wrk, stat)
                lnfence[par] = pg.sDV.n
                if tap_xn is not None:
                    pg.sDV.wait(nc.sync)
                    pg.sST.inc(nc.sync.dma_start(
                        tap_xn[ci * C:(ci + 1) * C, :], xn_bf[:, :]), 16)
                pg.sDV.wait(nc.tensor)
                _transpose_batch(
                    pg,
                    [(xn_bf[:, ft * 128:(ft + 1) * 128], xnT[:, cc, ft, :])
                     for ft in range(32)], pb_t)
                pg.sDV.wait(nc.sync)
                st = nc.sync.dma_start(
                    T["xnT_sc"][ci].rearrange("ft p t -> p ft t"),
                    xnT[:, cc, :, :])
                pg.sST.inc(st, 16)
            for s in range(NSTR):
                spar = s % 2
                wdram, col0 = wsrc[s]
                nc.gpsimd.wait_ge(pg.sPE.h, wfence[spar])
                ldw = nc.gpsimd.dma_start(
                    wstr[spar][:, :, :],
                    wdram[:, col0:col0 + 512].rearrange("(kt p) n -> p kt n", p=128))
                pg.sLG.inc(ldw, 16)
                pg.sLG.wait(nc.tensor)
                for cc in range(GRP):
                    ci = g * GRP + cc
                    mpar = (s * GRP + cc) % 4
                    nc.tensor.wait_ge(pg.sDV.h, qfence[mpar])
                    for kt in range(32):
                        mm = nc.tensor.matmul(pb_mm[mpar][:, :],
                                              xnT[:, cc, kt, :],
                                              wstr[spar][:, kt, :],
                                              start=(kt == 0), stop=(kt == 31))
                    pg.sPE.inc(mm)
                    wfence[spar] = pg.sPE.n
                    pg.sPE.wait(nc.vector)
                    nc.vector.wait_ge(pg.sST.h, qstfence[mpar])
                    ev = nc.vector.tensor_copy(qkv_o[mpar][:, :], pb_mm[mpar][:, :])
                    pg.sDV.inc(ev)
                    qfence[mpar] = pg.sDV.n
                    pg.sDV.wait(nc.sync)
                    st = nc.sync.dma_start(
                        T["qkv_sc"][ci, :, s * 512:(s + 1) * 512], qkv_o[mpar][:, :])
                    pg.sST.inc(st, 16)
                    qstfence[mpar] = pg.sST.n



def _phase_b(pg, T, n_tok, tap):
    nc = pg.nc
    phase_barrier(pg)
    NCH = n_tok // C
    QCOLS = (H // 2) * D
    KCOLS = (KVH // 2) * D
    NKV = KVH // 2
    NQ = H // 2

    tap_o = tap("o_", [n_tok, QCOLS], BF16)
    tap_qs = tap("qs", [n_tok, QCOLS])
    tap_gs = tap("gs", [n_tok, KCOLS])
    tap_ks = tap("ks", [n_tok, KCOLS])

    with ExitStack() as ctx:
        E = ctx.enter_context
        q4 = [E(nc.sbuf_tensor(f"pb_q{i}", [128, QCOLS], F32)) for i in range(2)]
        k4 = [E(nc.sbuf_tensor(f"pb_k{i}", [128, KCOLS], F32)) for i in range(2)]
        v4 = [E(nc.sbuf_tensor(f"pb_v{i}", [128, KCOLS], BF16)) for i in range(3)]
        vlast = [E(nc.sbuf_tensor(f"pb_vl{i}", [64, KCOLS], BF16)) for i in range(2)]
        rop = [E(nc.sbuf_tensor(f"pb_r{i}", [128, 4, D], BF16)) for i in range(2)]
        gs2 = [E(nc.sbuf_tensor(f"pb_gs{i}", [128, KCOLS], F32)) for i in range(2)]
        w1x = [E(nc.sbuf_tensor(f"pb_w1{i}", [128, QCOLS], F32)) for i in range(2)]
        w2x = [E(nc.sbuf_tensor(f"pb_w2{i}", [128, QCOLS], F32)) for i in range(2)]
        qs = E(nc.sbuf_tensor("pb_qs", [128, QCOLS], F32))
        ks = E(nc.sbuf_tensor("pb_ks", [128, KCOLS], F32))
        qrope2 = [E(nc.sbuf_tensor(f"pb_qrope{i}", [128, QCOLS], BF16))
                  for i in range(2)]
        krope2 = [E(nc.sbuf_tensor(f"pb_krope{i}", [128, KCOLS], BF16))
                  for i in range(2)]
        statx = [E(nc.sbuf_tensor(f"pb_stat{i}", [128, 64], F32)) for i in range(2)]
        statB = E(nc.sbuf_tensor("pb_statB", [128, 16], F32))
        qs_bf2 = [E(nc.sbuf_tensor(f"pb_qsbf{i}", [128, QCOLS], BF16))
                  for i in range(2)]
        ks_bf2 = [E(nc.sbuf_tensor(f"pb_ksbf{i}", [128, KCOLS], BF16))
                  for i in range(2)]
        qsT = E(nc.sbuf_tensor("pb_qsT", [128, NQ, 128], BF16))
        ksT = E(nc.sbuf_tensor("pb_ksT", [128, NKV, 128], BF16))
        qrT = E(nc.sbuf_tensor("pb_qrT", [128, NQ, 128], BF16))
        krT = [E(nc.sbuf_tensor(f"pb_krT{i}", [128, NKV, 128], BF16))
               for i in range(3)]
        eaT2 = [E(nc.sbuf_tensor(f"pb_eaT{i}", [128, KCOLS], F32)) for i in range(2)]
        enaT2 = [E(nc.sbuf_tensor(f"pb_enaT{i}", [128, KCOLS], F32))
                 for i in range(2)]
        ekH2 = [E(nc.sbuf_tensor(f"pb_ekH{i}", [128, KCOLS], F32)) for i in range(2)]
        ktil = E(nc.sbuf_tensor("pb_ktil", [128, 128], BF16))
        kHbf = E(nc.sbuf_tensor("pb_kHbf", [128, 128], BF16))
        kHT = E(nc.sbuf_tensor("pb_kHT", [128, 128], BF16))
        qtilT = E(nc.sbuf_tensor("pb_qtilT", [128, 4, 128], BF16))
        ptm = E(nc.sbuf_tensor("pb_ptm", [128, 4, 128], BF16))
        S = E(nc.sbuf_tensor("pb_S", [128, KCOLS], F32))
        S_bf = E(nc.sbuf_tensor("pb_Sbf", [128, KCOLS], BF16))
        sco = E(nc.sbuf_tensor("pb_sco", [128, 4, C + W], F32))
        pt_sw = E(nc.sbuf_tensor("pb_ptsw", [128, 4, C + W], BF16))
        ptA = E(nc.sbuf_tensor("pb_ptA", [64, 4, 128], BF16))
        ptB = E(nc.sbuf_tensor("pb_ptB", [128, 4, 128], BF16))
        o_ch = E(nc.sbuf_tensor("pb_och", [128, QCOLS], BF16))

        pb_a = E(nc.psum_tensor("pb_pa", [128, 512], F32))
        pb_t = E(nc.psum_tensor("pb_pt", [128, 1024], BF16))
        pb_pt = E(nc.psum_tensor("pb_ppt", [128, 512], F32))
        pb_o = E(nc.psum_tensor("pb_po", [128, 512], F32))
        pb_s1 = E(nc.psum_tensor("pb_ps1", [128, 2, C + W], F32))
        pb_s2 = E(nc.psum_tensor("pb_ps2", [128, 2, C + W], F32))
        pb_ds = E(nc.psum_tensor("pb_pds", [128, 512], F32))

        nc.vector.memset(S[:, :], 0.0)
        nc.vector.memset(krT[2][:, :, :], 0.0)
        nc.vector.memset(v4[2][:, :], 0.0)
        mz = nc.vector.memset(vlast[0][:, :], 0.0)
        pg.sDV.inc(mz)

        pe_end = {-1: pg.sPE.n, -2: pg.sPE.n}
        dv_end = {-1: pg.sDV.n, -2: pg.sDV.n}
        dv_pre = {}
        ds_fence = 0
        o_fence = 0

        def dma_in(ci):
            par = ci % 2
            nc.sync.wait_ge(pg.sPE.h, pe_end[ci - 2])
            nc.sync.wait_ge(pg.sDV.h, dv_end[ci - 2])
            pg.sLD.inc(nc.sync.dma_start(q4[par][:, :],
                                         T["qkv_sc"][ci, :, 0:QCOLS]), 16)
            pg.sLD.inc(nc.sync.dma_start(k4[par][:, :],
                                         T["qkv_sc"][ci, :, QCOLS:QCOLS + KCOLS]), 16)
            for j, tb in enumerate(("tb_rcq", "tb_rsq", "tb_rck", "tb_rsk")):
                pg.sLD.inc(nc.sync.dma_start(rop[par][:, j, :],
                                             T[tb][ci * C:(ci + 1) * C, :]), 16)
            nc.gpsimd.wait_ge(pg.sPE.h, pe_end[ci - 2])
            pg.sLG.inc(nc.gpsimd.dma_start(v4[ci % 3][:, :],
                                           T["qkv_sc"][ci, :, QCOLS + KCOLS:]), 16)
            if ci >= 1:
                pg.sLG.inc(nc.gpsimd.dma_start(
                    vlast[ci % 2][:, :],
                    T["qkv_sc"][ci - 1, 64:128, QCOLS + KCOLS:]), 16)

        nc.sync.wait_ge(pg.sST.h, pg.sST.n)
        nc.gpsimd.wait_ge(pg.sST.h, pg.sST.n)
        dma_in(0)
        for ci in range(NCH):
            par = ci % 2
            v_cur = v4[ci % 3]
            krT_cur = krT[ci % 3]
            krT_prev = krT[(ci - 1) % 3]
            ld_mark = pg.sLD.n
            lg_mark = pg.sLG.n
            if ci + 1 < NCH:
                dma_in(ci + 1)
            gs = gs2[par]
            w1 = w1x[par]
            w2 = w2x[par]
            qrope = qrope2[par]
            krope = krope2[par]
            stat = statx[par]
            qs_bf = qs_bf2[par]
            ks_bf = ks_bf2[par]
            eaT = eaT2[par]
            enaT = enaT2[par]
            ekH = ekH2[par]
            nc.vector.wait_ge(pg.sPE.h, pe_end[ci - 2])
            nc.scalar.wait_ge(pg.sPE.h, pe_end[ci - 2])
            nc.scalar.wait_ge(pg.sDV.h, dv_pre.get(ci - 1, 0))
            nc.vector.wait_ge(pg.sLD.h, ld_mark)
            nc.vector.wait_ge(pg.sLG.h, lg_mark)
            nc.scalar.wait_ge(pg.sLD.h, ld_mark)
            nc.tensor.wait_ge(pg.sLG.h, lg_mark)
            nc.scalar.wait_ge(pg.sST.h, o_fence)

            kt_ = k4[par]
            qt_ = q4[par]

            nc.scalar.activation(w2[:, 0:KCOLS], kt_[:, :], AF.Abs,
                                 bias=0.0, scale=1.0)
            nc.scalar.activation(w2[:, 0:KCOLS], w2[:, 0:KCOLS], AF.Exp,
                                 bias=0.0, scale=-1.0)
            ac = nc.scalar.activation(w2[:, 0:KCOLS], w2[:, 0:KCOLS], AF.Ln,
                                      bias=1.0, scale=1.0)
            pg.sAC.inc(ac)
            nc.vector.tensor_scalar(gs[:, :], kt_[:, :], -1.0, 0.0, ALU.mult, ALU.max)
            pg.sAC.wait(nc.vector)
            nc.vector.tensor_tensor(gs[:, :], gs[:, :], w2[:, 0:KCOLS], ALU.add)
            gsi = nc.vector.tensor_scalar(gs[:, :], gs[:, :], -1.0 / 16.0, None,
                                          ALU.mult)
            pg.sDV.inc(gsi)
            if tap_gs is not None:
                pg.sDV.wait(nc.sync)
                pg.sST.inc(nc.sync.dma_start(tap_gs[ci * C:(ci + 1) * C, :],
                                             gs[:, :]), 16)

            kg = kt_[:, :].rearrange("p (h d) -> p h d", d=D)
            nc.vector.tensor_reduce(
                stat[:, 0:NKV].rearrange("p (h o) -> p h o", o=1), kg, AX.X, ALU.max)
            qg = qt_[:, :].rearrange("p (h d) -> p h d", d=D)
            rr = nc.vector.tensor_reduce(
                stat[:, 16:16 + NQ].rearrange("p (h o) -> p h o", o=1),
                qg, AX.X, ALU.max)
            pg.sDV.inc(rr)
            pg.sDV.wait(nc.vector)
            nc.vector.tensor_scalar(stat[:, 8:8 + NKV], stat[:, 0:NKV], -1.0, None,
                                    ALU.mult)
            nv = nc.vector.tensor_scalar(stat[:, 32:32 + NQ], stat[:, 16:16 + NQ],
                                         -1.0, None, ALU.mult)
            pg.sDV.inc(nv)
            pg.sDV.wait(nc.scalar)
            for h_ in range(NKV):
                nc.scalar.activation(ks[:, h_ * D:(h_ + 1) * D],
                                     kt_[:, h_ * D:(h_ + 1) * D], AF.Exp,
                                     bias=stat[:, 8 + h_:9 + h_], scale=1.0,
                                     accum_out=stat[:, 12 + h_:13 + h_])
            for h_ in range(NQ):
                ac = nc.scalar.activation(qs[:, h_ * D:(h_ + 1) * D],
                                          qt_[:, h_ * D:(h_ + 1) * D], AF.Exp,
                                          bias=stat[:, 32 + h_:33 + h_], scale=1.0,
                                          accum_out=stat[:, 48 + h_:49 + h_])
            pg.sAC.inc(ac)
            pg.sAC.wait(nc.vector)
            nc.vector.reciprocal(stat[:, 12:12 + NKV], stat[:, 12:12 + NKV])
            rc = nc.vector.reciprocal(stat[:, 48:48 + NQ], stat[:, 48:48 + NQ])
            pg.sDV.inc(rc)
            pg.sDV.wait(nc.vector)
            for h_ in range(NKV):
                nc.vector.tensor_scalar(ks[:, h_ * D:(h_ + 1) * D],
                                        ks[:, h_ * D:(h_ + 1) * D],
                                        stat[:, 12 + h_:13 + h_], None, ALU.mult)
            nc.vector.tensor_copy(ks_bf[:, :], ks[:, :])
            for h_ in range(NQ):
                nc.vector.tensor_scalar(qs[:, h_ * D:(h_ + 1) * D],
                                        qs[:, h_ * D:(h_ + 1) * D],
                                        stat[:, 48 + h_:49 + h_], None, ALU.mult)
            qcast = nc.vector.tensor_copy(qs_bf[:, :], qs[:, :])
            pg.sDV.inc(qcast)
            if tap_qs is not None:
                pg.sDV.wait(nc.sync)
                pg.sST.inc(nc.sync.dma_start(tap_qs[ci * C:(ci + 1) * C, :],
                                             qs[:, :]), 16)
            if tap_ks is not None:
                pg.sDV.wait(nc.sync)
                pg.sST.inc(nc.sync.dma_start(tap_ks[ci * C:(ci + 1) * C, :],
                                             ks[:, :]), 16)

            def rope(dst, src, ncols, cosi, sini):
                nh = ncols // D
                g3 = src.rearrange("p (h d) -> p h d", d=D)
                gd = dst.rearrange("p (h d) -> p h d", d=D)
                wv = w2[:, 0:ncols].rearrange("p (h d) -> p h d", d=D)
                nc.vector.tensor_scalar(wv[:, :, 0:64], g3[:, :, 64:128], -1.0,
                                        None, ALU.mult)
                nc.vector.tensor_copy(wv[:, :, 64:128], g3[:, :, 0:64])
                cosb = rop[par][:, cosi:cosi + 1, :].to_broadcast([128, nh, D])
                sinb = rop[par][:, sini:sini + 1, :].to_broadcast([128, nh, D])
                nc.vector.tensor_tensor(wv, wv, sinb, ALU.mult)
                t2v = w1[:, 0:ncols].rearrange("p (h d) -> p h d", d=D)
                nc.vector.tensor_tensor(t2v, g3, cosb, ALU.mult)
                return nc.vector.tensor_tensor(gd, t2v, wv, ALU.add)

            rope(qrope[:, :], qt_[:, :], QCOLS, 0, 1)
            rp = rope(krope[:, :], kt_[:, :], KCOLS, 2, 3)
            pg.sDV.inc(rp)
            dv_pre[ci] = pg.sDV.n

            pg.sDV.wait(nc.tensor)
            pg.sAC.wait(nc.tensor)
            for h_ in range(NKV):
                mm = nc.tensor.matmul(pb_a[:, h_ * 128:(h_ + 1) * 128],
                                      gs[:, h_ * D:(h_ + 1) * D], pg.utri[:, :],
                                      start=True, stop=True)
            pg.sPE.inc(mm)
            pg.sPE.wait(nc.scalar)
            nc.scalar.activation(eaT[:, :], pb_a[:, :], AF.Exp, bias=0.0, scale=1.0)
            ac = nc.scalar.activation(enaT[:, :], pb_a[:, :], AF.Exp, bias=0.0,
                                      scale=-1.0)
            pg.sAC.inc(ac)
            pg.sAC.wait(nc.vector)
            for h_ in range(NKV):
                nc.vector.tensor_scalar(ekH[:, h_ * D:(h_ + 1) * D],
                                        enaT[:, h_ * D:(h_ + 1) * D],
                                        eaT[:, h_ * D + 127:h_ * D + 128],
                                        None, ALU.mult)
            scst = nc.vector.tensor_copy(S_bf[:, :], S[:, :])
            pg.sDV.inc(scst)

            pg.sDV.wait(nc.tensor)
            pairs = ([(qs_bf[:, i * 128:(i + 1) * 128], qsT[:, i, :])
                      for i in range(NQ)]
                     + [(ks_bf[:, i * 128:(i + 1) * 128], ksT[:, i, :])
                        for i in range(NKV)]
                     + [(qrope[:, i * 128:(i + 1) * 128], qrT[:, i, :])
                        for i in range(NQ)]
                     + [(krope[:, i * 128:(i + 1) * 128], krT_cur[:, i, :])
                        for i in range(NKV)])
            _transpose_batch(pg, pairs, pb_t)

            for h_ in range(NKV):
                hsl = slice(h_ * D, (h_ + 1) * D)
                nc.vector.tensor_tensor(ktil[:, :], ksT[:, h_, :], enaT[:, hsl],
                                        ALU.mult)
                nc.vector.tensor_tensor(kHbf[:, :], ksT[:, h_, :], ekH[:, hsl],
                                        ALU.mult)
                for j in range(4):
                    tt = nc.vector.tensor_tensor(qtilT[:, j, :],
                                                 qsT[:, h_ * 4 + j, :],
                                                 eaT[:, hsl], ALU.mult)
                pg.sDV.inc(tt)
                pg.sDV.wait(nc.tensor)
                for j in range(4):
                    nc.tensor.matmul(pb_pt[:, j * 128:(j + 1) * 128], ktil[:, :],
                                     qtilT[:, j, :], start=True, stop=True)
                nc.tensor.transpose(pb_t[:, 0:128], kHbf[:, :], pg.ident[:, :])
                for j in range(4):
                    qh = h_ * 4 + j
                    pb_s = pb_s1 if j < 2 else pb_s2
                    jj = j % 2
                    nc.tensor.matmul(pb_s[:, jj, 0:W], qrT[:, qh, :],
                                     krT_prev[:, h_, 64:128], start=True, stop=True)
                    mm = nc.tensor.matmul(pb_s[:, jj, W:W + C], qrT[:, qh, :],
                                          krT_cur[:, h_, :], start=True, stop=True)
                pg.sPE.inc(mm)
                pg.sPE.wait(nc.vector)
                for j in range(4):
                    nc.vector.tensor_tensor(ptm[:, j, :],
                                            pb_pt[:, j * 128:(j + 1) * 128],
                                            pg.utri[:, :], ALU.mult)
                nc.vector.tensor_copy(kHT[:, :], pb_t[:, 0:128])
                msk = pg.swam0 if ci == 0 else pg.swam
                for j in range(4):
                    pb_s = pb_s1 if j < 2 else pb_s2
                    nc.vector.tensor_tensor(sco[:, j, :], pb_s[:, j % 2, :],
                                            msk[:, :], ALU.add)
                rr = nc.vector.tensor_reduce(
                    statB[:, 0:4].rearrange("p (h o) -> p h o", o=1),
                    sco[:, :, :], AX.X, ALU.max)
                pg.sDV.inc(rr)
                pg.sDV.wait(nc.vector)
                rd = nc.vector.tensor_scalar(statB[:, 4:8], statB[:, 0:4], -1.0,
                                             None, ALU.mult)
                pg.sDV.inc(rd)
                pg.sDV.wait(nc.scalar)
                pg.sDV.wait(nc.tensor)
                for j in range(4):
                    ac = nc.scalar.activation(sco[:, j, :], sco[:, j, :], AF.Exp,
                                              bias=statB[:, 4 + j:5 + j], scale=1.0,
                                              accum_out=statB[:, 8 + j:9 + j])
                pg.sAC.inc(ac)
                pg.sAC.wait(nc.vector)
                rc = nc.vector.reciprocal(statB[:, 8:12], statB[:, 8:12])
                pg.sDV.inc(rc)
                pg.sDV.wait(nc.vector)
                for j in range(4):
                    sw = nc.vector.tensor_scalar(pt_sw[:, j, :], sco[:, j, :],
                                                 statB[:, 8 + j:9 + j], None, ALU.mult)
                pg.sDV.inc(sw)
                pg.sDV.wait(nc.tensor)
                nc.tensor.wait_ge(pg.sDV.h, ds_fence)
                nc.tensor.matmul(pb_ds[:, h_ * 128:(h_ + 1) * 128], kHT[:, :],
                                 v_cur[:, hsl], start=True, stop=True)
                for j in range(4):
                    nc.tensor.transpose(pb_t[0:64, j * 128:(j + 1) * 128],
                                        pt_sw[:, j, 0:64], pg.ident[:, :])
                    tr = nc.tensor.transpose(
                        pb_t[:, 512 + j * 128:512 + (j + 1) * 128],
                        pt_sw[:, j, 64:192], pg.ident[:, :])
                pg.sPE.inc(tr)
                pg.sPE.wait(nc.vector)
                nc.vector.tensor_copy(
                    ptA[:, :, :],
                    pb_t[0:64, 0:512].rearrange("p (j t) -> p j t", t=128))
                pa = nc.vector.tensor_copy(
                    ptB[:, :, :],
                    pb_t[:, 512:1024].rearrange("p (j t) -> p j t", t=128))
                pg.sDV.inc(pa)
                pg.sDV.wait(nc.tensor)
                for j in range(4):
                    po = pb_o[:, j * 128:(j + 1) * 128]
                    nc.tensor.matmul(po, ptm[:, j, :], v_cur[:, hsl],
                                     start=True, stop=False)
                    nc.tensor.matmul(po, qtilT[:, j, :], S_bf[:, hsl],
                                     start=False, stop=False)
                    nc.tensor.matmul(po, ptA[:, j, :], vlast[ci % 2][:, hsl],
                                     start=False, stop=False)
                    mm = nc.tensor.matmul(po, ptB[:, j, :], v_cur[:, hsl],
                                          start=False, stop=True)
                pg.sPE.inc(mm)
                pg.sPE.wait(nc.scalar)
                oc = nc.scalar.activation(o_ch[:, h_ * 512:(h_ + 1) * 512],
                                          pb_o[:, :], AF.Copy, bias=0.0, scale=0.5)
                pg.sAC.inc(oc)
                pg.sAC.wait(nc.tensor)
                pg.sPE.wait(nc.vector)
                nc.vector.tensor_scalar(S[:, hsl], S[:, hsl],
                                        eaT[:, h_ * D + 127:h_ * D + 128], None,
                                        ALU.mult)
                su = nc.vector.tensor_tensor(S[:, hsl], S[:, hsl],
                                             pb_ds[:, h_ * 128:(h_ + 1) * 128],
                                             ALU.add)
                pg.sDV.inc(su)
                ds_fence = pg.sDV.n

            pg.sAC.wait(nc.sync)
            st = nc.sync.dma_start(T["o_mine"][ci * C:(ci + 1) * C, :], o_ch[:, :])
            pg.sST.inc(st, 16)
            o_fence = pg.sST.n
            if tap_o is not None:
                pg.sST.inc(nc.sync.dma_start(tap_o[ci * C:(ci + 1) * C, :],
                                             o_ch[:, :]), 16)
            pe_end[ci] = pg.sPE.n
            dv_end[ci] = pg.sDV.n



def _phase_c_ffn0(pg, T, n_tok, tap):
    nc = pg.nc
    phase_barrier(pg)
    MYT = n_tok // 2
    MCH = MYT // C
    NCH = n_tok // C

    tap_f = tap("ffn0", [MYT, HID])
    tap_act = tap("act0", [MYT, FFN_H], BF16)

    with ExitStack() as ctx:
        E = ctx.enter_context
        xnT_ch = E(nc.sbuf_tensor("c1_xnT", [128, 32, 128], BF16))
        xfT = E(nc.sbuf_tensor("c1_xfT", [128, 8, MYT], BF16))
        u1T = E(nc.sbuf_tensor("c1_u1T", [2, MYT], BF16))
        u3T = E(nc.sbuf_tensor("c1_u3T", [2, MYT], BF16))
        wrkf = E(nc.sbuf_tensor("c1_wrkf", [128, 128], F32))
        pb_u = E(nc.psum_tensor("c1_pu", [2, 512], F32))
        w1s = [E(nc.sbuf_tensor(f"c1_w1s{i}", [128, 8, 512], BF16)) for i in range(2)]
        w3s = [E(nc.sbuf_tensor(f"c1_w3s{i}", [128, 8, 512], BF16)) for i in range(2)]
        silu_t = E(nc.sbuf_tensor("c1_silu", [128, 512], F32))
        act_t = E(nc.sbuf_tensor("c1_act", [128, 512], BF16))
        pb_m = [E(nc.psum_tensor(f"c1_pm{i}", [128, 512], F32)) for i in range(6)]
        pg.a1T_b = E(nc.sbuf_tensor("a1T_b", [128, 32, 2], BF16))
        pg.a3T_b = E(nc.sbuf_tensor("a3T_b", [128, 32, 2], BF16))
        pg.b1T_b = E(nc.sbuf_tensor("b1T_b", [2, FFN_H], BF16))
        pg.b3T_b = E(nc.sbuf_tensor("b3T_b", [2, FFN_H], BF16))
        pg.sLG.inc(nc.gpsimd.dma_start(
            pg.a1T_b[:, :, :],
            T["a1T"][:, :].rearrange("(ft p) r -> p ft r", p=128)), 16)
        pg.sLG.inc(nc.gpsimd.dma_start(
            pg.a3T_b[:, :, :],
            T["a3T"][:, :].rearrange("(ft p) r -> p ft r", p=128)), 16)
        pg.sLG.inc(nc.gpsimd.dma_start(pg.b1T_b[:, :], T["b1T"][:, :]), 16)
        pg.sLG.inc(nc.gpsimd.dma_start(pg.b3T_b[:, :], T["b3T"][:, :]), 16)
        pg.sLG.wait(nc.tensor)
        pg.sLG.wait(nc.vector)

        ld_fence = 0
        dvf = 0
        for ci in range(MCH):
            nc.sync.wait_ge(pg.sPE.h, ld_fence)
            nc.sync.wait_ge(pg.sDV.h, dvf)
            br_load(pg, nc.sync, pg.sLD, xnT_ch[:, :, :],
                    T["xnT_sc"][ci].rearrange("ft p t -> p ft t"),
                    T["xnT_sc"][NCH // 2 + ci].rearrange("ft p t -> p ft t"))
            pg.sLD.wait(nc.vector)
            pg.sLD.wait(nc.tensor)
            tsl = slice(ci * C, (ci + 1) * C)
            for fq in range(8):
                nc.vector.tensor_tensor(wrkf[:, :], xnT_ch[:, fq, :],
                                        xnT_ch[:, fq + 8, :], ALU.add)
                nc.vector.tensor_tensor(wrkf[:, :], wrkf[:, :],
                                        xnT_ch[:, fq + 16, :], ALU.add)
                fv = nc.vector.tensor_tensor(xfT[:, fq, tsl], wrkf[:, :],
                                             xnT_ch[:, fq + 24, :], ALU.add)
            pg.sDV.inc(fv)
            dvf = pg.sDV.n
            pg.sDV.wait(nc.tensor)
            nc.tensor.wait_ge(pg.sAC.h, pg.c1uf if hasattr(pg, "c1uf") else 0)
            for kt in range(32):
                nc.tensor.matmul(pb_u[:, 0:128], pg.a1T_b[:, kt, :],
                                 xnT_ch[:, kt, :], start=(kt == 0), stop=(kt == 31))
            for kt in range(32):
                mm = nc.tensor.matmul(pb_u[:, 128:256], pg.a3T_b[:, kt, :],
                                      xnT_ch[:, kt, :], start=(kt == 0),
                                      stop=(kt == 31))
            pg.sPE.inc(mm)
            ld_fence = pg.sPE.n
            pg.sPE.wait(nc.scalar)
            nc.scalar.activation(u1T[:, tsl], pb_u[:, 0:128], AF.Copy, bias=0.0,
                                 scale=0.5)
            ua = nc.scalar.activation(u3T[:, tsl], pb_u[:, 128:256], AF.Copy,
                                      bias=0.0, scale=0.5)
            pg.sAC.inc(ua)
            pg.c1uf = pg.sAC.n

        nstrips = (FFN_H + 511) // 512
        wf = [0, 0]
        acf = [0, 0, 0]
        dvf2 = [0, 0, 0]
        stf = 0
        pg.sDV.wait(nc.tensor)
        pg.sAC.wait(nc.tensor)
        for si in range(nstrips):
            h0 = si * 512
            hw = min(512, FFN_H - h0)
            spar = si % 2
            nc.gpsimd.wait_ge(pg.sPE.h, wf[spar])
            pg.sLG.inc(nc.gpsimd.dma_start(
                w1s[spar][:, :, 0:hw],
                T["w1qT"][:, h0:h0 + hw].rearrange("(kt p) n -> p kt n", p=128)), 16)
            pg.sLG.inc(nc.gpsimd.dma_start(
                w3s[spar][:, :, 0:hw],
                T["w3qT"][:, h0:h0 + hw].rearrange("(kt p) n -> p kt n", p=128)), 16)
            lg_mark = pg.sLG.n
            nc.tensor.wait_ge(pg.sLG.h, lg_mark)
            for ci in range(MCH):
                tsl = slice(ci * C, (ci + 1) * C)
                bank = ci % 3
                bankU = 3 + bank
                nc.tensor.wait_ge(pg.sAC.h, acf[bank])
                nc.tensor.wait_ge(pg.sDV.h, dvf2[bank])
                for kt in range(8):
                    nc.tensor.matmul(pb_m[bank][:, 0:hw], xfT[:, kt, tsl],
                                     w1s[spar][:, kt, 0:hw], start=(kt == 0),
                                     stop=False)
                nc.tensor.matmul(pb_m[bank][:, 0:hw], u1T[:, tsl],
                                 pg.b1T_b[:, h0:h0 + hw], start=False, stop=True)
                for kt in range(8):
                    nc.tensor.matmul(pb_m[bankU][:, 0:hw], xfT[:, kt, tsl],
                                     w3s[spar][:, kt, 0:hw], start=(kt == 0),
                                     stop=False)
                mm = nc.tensor.matmul(pb_m[bankU][:, 0:hw], u3T[:, tsl],
                                      pg.b3T_b[:, h0:h0 + hw], start=False,
                                      stop=True)
                pg.sPE.inc(mm)
                wf[spar] = pg.sPE.n
                pg.sPE.wait(nc.scalar)
                nc.scalar.wait_ge(pg.sDV.h, dvf2[(bank + 2) % 3])
                sl = nc.scalar.activation(silu_t[:, 0:hw], pb_m[bank][:, 0:hw],
                                          AF.Silu, bias=0.0, scale=1.0)
                pg.sAC.inc(sl)
                acf[bank] = pg.sAC.n
                pg.sAC.wait(nc.vector)
                nc.vector.wait_ge(pg.sST.h, stf)
                av = nc.vector.tensor_tensor(act_t[:, 0:hw], silu_t[:, 0:hw],
                                             pb_m[bankU][:, 0:hw], ALU.mult)
                pg.sDV.inc(av)
                dvf2[bank] = pg.sDV.n
                pg.sDV.wait(nc.sync)
                st = nc.sync.dma_start(T["act_sc"][ci, :, h0:h0 + hw],
                                       act_t[:, 0:hw])
                pg.sST.inc(st, 16)
                stf = pg.sST.n
                if tap_act is not None:
                    pg.sST.inc(nc.sync.dma_start(
                        tap_act[tsl, h0:h0 + hw], act_t[:, 0:hw]), 16)

    phase_barrier(pg)
    with ExitStack() as ctx:
        E = ctx.enter_context
        act_ch = E(nc.sbuf_tensor("c2_act", [128, FFN_H], BF16))
        hf_bf = E(nc.sbuf_tensor("c2_hf", [128, 2752], BF16))
        hfT = E(nc.sbuf_tensor("c2_hfT", [128, 22, MYT], BF16))
        u2T = E(nc.sbuf_tensor("c2_u2T", [2, MYT], BF16))
        aT_st = E(nc.sbuf_tensor("c2_aTst", [128, 8, 128], BF16))
        wrkf = E(nc.sbuf_tensor("c2_wrkf", [128, 2752], F32))
        pb_u = E(nc.psum_tensor("c2_pu", [2, 512], F32))
        pb_t = E(nc.psum_tensor("c2_pt", [128, 1024], BF16))
        w2s = [E(nc.sbuf_tensor(f"c2_w2s{i}", [128, 22, 512], BF16)) for i in range(2)]
        pb_m = [E(nc.psum_tensor(f"c2_pm{i}", [128, 512], F32)) for i in range(4)]
        x_t = E(nc.sbuf_tensor("c2_xt", [128, 512], F32))
        o_t = E(nc.sbuf_tensor("c2_ot", [128, 512], F32))
        pg.a2T_b = E(nc.sbuf_tensor("a2T_b", [128, NHT, 2], BF16))
        pg.b2T_b = E(nc.sbuf_tensor("b2T_b", [2, HID], BF16))
        pg.sLG.inc(nc.gpsimd.dma_start(
            pg.a2T_b[:, :, :],
            T["a2T"][:, :].rearrange("(ft p) r -> p ft r", p=128)), 16)
        pg.sLG.inc(nc.gpsimd.dma_start(pg.b2T_b[:, :], T["b2T"][:, :]), 16)
        pg.sLG.wait(nc.tensor)

        nc.sync.wait_ge(pg.sST.h, pg.sST.n)
        ld_fence = 0
        dvf = 0
        for ci in range(MCH):
            tsl = slice(ci * C, (ci + 1) * C)
            nc.sync.wait_ge(pg.sPE.h, ld_fence)
            nc.sync.wait_ge(pg.sDV.h, dvf)
            ld = nc.sync.dma_start(act_ch[:, :], T["act_sc"][ci, :, :])
            pg.sLD.inc(ld, 16)
            pg.sLD.wait(nc.vector)
            pg.sLD.wait(nc.tensor)
            nc.vector.tensor_tensor(wrkf[:, :], act_ch[:, 0:2752],
                                    act_ch[:, 2752:5504], ALU.add)
            nc.vector.tensor_tensor(wrkf[:, :], wrkf[:, :],
                                    act_ch[:, 5504:8256], ALU.add)
            fv = nc.vector.tensor_tensor(hf_bf[:, :], wrkf[:, :],
                                         act_ch[:, 8256:11008], ALU.add)
            pg.sDV.inc(fv)
            pg.sDV.wait(nc.tensor)
            _transpose_batch(
                pg,
                [(hf_bf[:, kt * 128:(kt + 1) * 128], hfT[:, kt, tsl])
                 for kt in range(21)], pb_t)
            tr = nc.tensor.transpose(pb_t[0:64, 0:128], hf_bf[:, 2688:2752],
                                     pg.ident[:, :])
            pg.sPE.inc(tr)
            pg.sPE.wait(nc.vector)
            hv = nc.vector.tensor_copy(hfT[0:64, 21, tsl], pb_t[0:64, 0:128])
            pg.sDV.inc(hv)
            pg.sDV.wait(nc.tensor)
            for bi in range(NHT // 8 + (1 if NHT % 8 else 0)):
                kts = list(range(bi * 8, min(bi * 8 + 8, NHT)))
                for j, kt in enumerate(kts):
                    tr = nc.tensor.transpose(pb_t[:, j * 128:(j + 1) * 128],
                                             act_ch[:, kt * 128:(kt + 1) * 128],
                                             pg.ident[:, :])
                pg.sPE.inc(tr)
                pg.sPE.wait(nc.vector)
                for j, kt in enumerate(kts):
                    cp = nc.vector.tensor_copy(aT_st[:, j, :],
                                               pb_t[:, j * 128:(j + 1) * 128])
                pg.sDV.inc(cp)
                pg.sDV.wait(nc.tensor)
                for j, kt in enumerate(kts):
                    mm = nc.tensor.matmul(pb_u[:, 0:128], pg.a2T_b[:, kt, :],
                                          aT_st[:, j, :], start=(kt == 0),
                                          stop=(kt == NHT - 1))
                pg.sPE.inc(mm)
            ld_fence = pg.sPE.n
            dvf = pg.sDV.n
            pg.sPE.wait(nc.scalar)
            ua = nc.scalar.activation(u2T[:, tsl], pb_u[:, 0:128], AF.Copy,
                                      bias=0.0, scale=0.5)
            pg.sAC.inc(ua)
            pg.sAC.wait(nc.tensor)

        pg.sDV.wait(nc.tensor)
        pg.sAC.wait(nc.tensor)
        wf = [0, 0]
        evf = [0, 0, 0, 0]
        stf = 0
        for s in range(8):
            spar = s % 2
            nc.gpsimd.wait_ge(pg.sPE.h, wf[spar])
            pg.sLG.inc(nc.gpsimd.dma_start(
                w2s[spar][:, 0:21, :],
                T["w2qT"][0:2688, s * 512:(s + 1) * 512].rearrange(
                    "(kt p) n -> p kt n", p=128)), 16)
            pg.sLG.inc(nc.gpsimd.dma_start(
                w2s[spar][0:64, 21, :],
                T["w2qT"][2688:2752, s * 512:(s + 1) * 512]), 16)
            lg_mark = pg.sLG.n
            nc.tensor.wait_ge(pg.sLG.h, lg_mark)
            for ci in range(MCH):
                bank = ci % 4
                tsl = slice(ci * C, (ci + 1) * C)
                nc.tensor.wait_ge(pg.sDV.h, evf[bank])
                for kt in range(21):
                    nc.tensor.matmul(pb_m[bank][:, :], hfT[:, kt, tsl],
                                     w2s[spar][:, kt, :], start=(kt == 0),
                                     stop=False)
                nc.tensor.matmul(pb_m[bank][:, :], hfT[0:64, 21, tsl],
                                 w2s[spar][0:64, 21, :], start=False, stop=False)
                mm = nc.tensor.matmul(pb_m[bank][:, :], u2T[:, tsl],
                                      pg.b2T_b[:, s * 512:(s + 1) * 512],
                                      start=False, stop=True)
                pg.sPE.inc(mm)
                wf[spar] = pg.sPE.n
                nc.sync.wait_ge(pg.sDV.h, evf[bank])
                ld = nc.sync.dma_start(x_t[:, :],
                                       T["x_my"][tsl, s * 512:(s + 1) * 512])
                pg.sLD.inc(ld, 16)
                pg.sLD.wait(nc.vector)
                pg.sPE.wait(nc.vector)
                nc.vector.wait_ge(pg.sST.h, stf)
                ev = nc.vector.tensor_tensor(o_t[:, :], pb_m[bank][:, :],
                                             x_t[:, :], ALU.add)
                pg.sDV.inc(ev)
                evf[bank] = pg.sDV.n
                pg.sDV.wait(nc.sync)
                st = nc.sync.dma_start(T["x1_sc"][tsl, s * 512:(s + 1) * 512],
                                       o_t[:, :])
                pg.sST.inc(st, 16)
                stf = pg.sST.n
                if tap_f is not None:
                    pg.sST.inc(nc.sync.dma_start(
                        tap_f[tsl, s * 512:(s + 1) * 512], o_t[:, :]), 16)



def _phase_c_wo(pg, T, n_tok, tap):
    nc = pg.nc
    phase_barrier(pg)
    MYT = n_tok // 2
    MCH = MYT // C
    QCOLS = (H // 2) * D
    GRP = min(8, MCH)
    NGRP = MCH // GRP

    tap_wo = tap("wo_out", [MYT, HID])

    with ExitStack() as ctx:
        E = ctx.enter_context
        o_in = E(nc.sbuf_tensor("cw_oin", [128, HID], BF16))
        oT = E(nc.sbuf_tensor("cw_oT", [128, GRP, 32, 128], BF16))
        wos = [E(nc.sbuf_tensor(f"cw_wos{i}", [128, 32, 512], BF16)) for i in range(2)]
        o_ev = E(nc.sbuf_tensor("cw_oev", [128, 512], F32))
        pb_t = E(nc.psum_tensor("cw_pt", [128, 1024], BF16))
        pb_m = [E(nc.psum_tensor(f"cw_pm{i}", [128, 512], F32)) for i in range(4)]

        sem, val = pg.cc_wait
        nc.sync.wait_ge(sem.h, val)
        nc.gpsimd.wait_ge(sem.h, val)
        nc.gpsimd.wait_ge(pg.sST.h, pg.sST.n)

        wf = [0, 0]
        evf = [0, 0, 0, 0]
        sgf = 0
        for g in range(NGRP):
            grp_pe = pg.sPE.n
            nc.vector.wait_ge(pg.sPE.h, grp_pe)
            for cc in range(GRP):
                ci = g * GRP + cc
                pg.sDV.wait(nc.sync)
                for slot in range(2):
                    br_load(pg, nc.sync, pg.sLD,
                            o_in[:, slot * QCOLS:(slot + 1) * QCOLS],
                            T["o_all"][slot * n_tok + ci * C:
                                       slot * n_tok + (ci + 1) * C, :],
                            T["o_all"][slot * n_tok + MYT + ci * C:
                                       slot * n_tok + MYT + (ci + 1) * C, :])
                pg.sLD.wait(nc.tensor)
                _transpose_batch(pg, [(o_in[:, ft * 128:(ft + 1) * 128],
                                       oT[:, cc, ft, :]) for ft in range(32)], pb_t)
            for s in range(8):
                spar = s % 2
                nc.gpsimd.wait_ge(pg.sPE.h, wf[spar])
                pg.sLG.inc(nc.gpsimd.dma_start(
                    wos[spar][:, :, :],
                    T["wo"][:, s * 512:(s + 1) * 512].rearrange(
                        "(kt p) n -> p kt n", p=128)), 16)
                lg_mark = pg.sLG.n
                nc.tensor.wait_ge(pg.sLG.h, lg_mark)
                for cc in range(GRP):
                    ci = g * GRP + cc
                    bank = cc % 4
                    nc.tensor.wait_ge(pg.sDV.h, evf[bank])
                    for kt in range(32):
                        mm = nc.tensor.matmul(pb_m[bank][:, :], oT[:, cc, kt, :],
                                              wos[spar][:, kt, :],
                                              start=(kt == 0), stop=(kt == 31))
                    pg.sPE.inc(mm)
                    wf[spar] = pg.sPE.n
                    pg.sPE.wait(nc.vector)
                    nc.vector.wait_ge(pg.sSG.h, sgf)
                    ev = nc.vector.tensor_copy(o_ev[:, :], pb_m[bank][:, :])
                    pg.sDV.inc(ev)
                    evf[bank] = pg.sDV.n
                    pg.sDV.wait(nc.gpsimd)
                    st = nc.gpsimd.dma_start(
                        T["x1_sc"][ci * C:(ci + 1) * C, s * 512:(s + 1) * 512],
                        o_ev[:, :], accum_op=ALU.add)
                    pg.sSG.inc(st, 16)
                    sgf = pg.sSG.n
                    if tap_wo is not None:
                        pg.sDV.wait(nc.sync)
                        pg.sST.inc(nc.sync.dma_start(
                            tap_wo[ci * C:(ci + 1) * C, s * 512:(s + 1) * 512],
                            o_ev[:, :]), 16)



def _phase_d0_ln(pg, T, n_tok, tap):
    nc = pg.nc
    phase_barrier(pg)
    MYT = n_tok // 2
    MCH = MYT // C
    tap_xn1 = tap("xn1", [MYT, HID], BF16)
    tap_xd0 = tap("xd0", [MCH, 32, 128, C], BF16)

    with ExitStack() as ctx:
        E = ctx.enter_context
        x_t = [E(nc.sbuf_tensor(f"d0_x{i}", [128, HID], F32)) for i in range(2)]
        wrk = E(nc.sbuf_tensor("d0_wrk", [128, HID], F32))
        stat = E(nc.sbuf_tensor("d0_stat", [128, 8], F32))
        xn_bf = E(nc.sbuf_tensor("d0_xnbf", [128, HID], BF16))
        xnT = E(nc.sbuf_tensor("d0_xnT", [128, 32, 128], BF16))
        pb_t = E(nc.psum_tensor("d0_pt", [128, 1024], BF16))

        nc.sync.wait_ge(pg.sSG.h, pg.sSG.n)
        nc.sync.wait_ge(pg.sST.h, pg.sST.n)
        lnf = [0, 0]
        stf = 0
        for ci in range(MCH):
            par = ci % 2
            nc.sync.wait_ge(pg.sDV.h, lnf[par])
            ld = nc.sync.dma_start(x_t[par][:, :], T["x1_sc"][ci * C:(ci + 1) * C, :])
            pg.sLD.inc(ld, 16)
            pg.sLD.wait(nc.scalar)
            pg.sLD.wait(nc.vector)
            _layernorm(pg, x_t[par], xn_bf, wrk, stat)
            lnf[par] = pg.sDV.n
            if tap_xn1 is not None:
                pg.sDV.wait(nc.sync)
                pg.sST.inc(nc.sync.dma_start(tap_xn1[ci * C:(ci + 1) * C, :],
                                             xn_bf[:, :]), 16)
            pg.sDV.wait(nc.tensor)
            nc.vector.wait_ge(pg.sST.h, stf)
            _transpose_batch(pg, [(xn_bf[:, ft * 128:(ft + 1) * 128], xnT[:, ft, :])
                                  for ft in range(32)], pb_t)
            pg.sDV.wait(nc.sync)
            st = nc.sync.dma_start(T["xn1T_sc"][ci].rearrange("ft p t -> p ft t"),
                                   xnT[:, :, :])
            pg.sST.inc(st, 16)
            if tap_xd0 is not None:
                pg.sST.inc(nc.sync.dma_start(
                    tap_xd0[ci].rearrange("ft p t -> p ft t"), xnT[:, :, :]), 16)
            stf = pg.sST.n



def _phase_d1_fwd(pg, T, n_tok, tap):
    nc = pg.nc
    phase_barrier(pg)
    MYT = n_tok // 2
    MCH = MYT // C
    GRP = min(8, MCH)
    NGRP = MCH // GRP

    with ExitStack() as ctx:
        E = ctx.enter_context
        xnT = E(nc.sbuf_tensor("d1_xnT", [128, GRP, 32, 128], BF16))
        dcs = [E(nc.sbuf_tensor(f"d1_dc{i}", [128, 32, 512], BF16)) for i in range(2)]
        dss = [E(nc.sbuf_tensor(f"d1_ds{i}", [128, 32, 512], BF16)) for i in range(2)]
        fr_t = E(nc.sbuf_tensor("d1_fr", [128, 512], F32))
        fi_t = E(nc.sbuf_tensor("d1_fi", [128, 512], F32))
        sq_t = E(nc.sbuf_tensor("d1_sq", [128, 512], F32))
        sq2_t = E(nc.sbuf_tensor("d1_sq2", [128, 512], F32))
        p_t = E(nc.sbuf_tensor("d1_p", [128, 512], BF16))
        fn_t = E(nc.sbuf_tensor("d1_fn", [128, 8], F32))
        pb_r = [E(nc.psum_tensor(f"d1_pr{i}", [128, 512], F32)) for i in range(2)]
        pb_i = [E(nc.psum_tensor(f"d1_pi{i}", [128, 512], F32)) for i in range(2)]
        pb_n = E(nc.psum_tensor("d1_pn", [128, 8], F32))

        tap_fr = tap("frd", [MYT, NSPEC])
        tap_xg = tap("xng", [MCH, 32, 128, C], BF16)
        tap_pp = tap("ppd", [MYT, NSPEC], BF16)
        tap_fn = tap("fnd", [MYT, 8])
        nc.sync.wait_ge(pg.sST.h, pg.sST.n)
        wfd = [0, 0]
        for g in range(NGRP):
            nc.sync.wait_ge(pg.sPE.h, pg.sPE.n)
            for cc in range(GRP):
                ld = nc.sync.dma_start(
                    xnT[:, cc, :, :],
                    T["xn1T_sc"][g * GRP + cc].rearrange("ft p t -> p ft t"))
                pg.sLD.inc(ld, 16)
            pg.sLD.wait(nc.tensor)
            if tap_xg is not None:
                pg.sLD.wait(nc.sync)
                for cc in range(GRP):
                    pg.sST.inc(nc.sync.dma_start(
                        tap_xg[g * GRP + cc].rearrange("ft p t -> p ft t"),
                        xnT[:, cc, :, :]), 16)
            for cc in range(GRP):
                ci = g * GRP + cc
                nc.tensor.wait_ge(pg.sDV.h, pg.d1nf if hasattr(pg, "d1nf") else 0)
                for kt in range(32):
                    mm = nc.tensor.matmul(pb_n[:, 0:1], xnT[:, cc, kt, :],
                                          pg.dnyq_b[:, kt, :],
                                          start=(kt == 0), stop=(kt == 31))
                pg.sPE.inc(mm)
                pg.sPE.wait(nc.vector)
                nc.vector.wait_ge(pg.sST.h, pg.d1nsf if hasattr(pg, "d1nsf") else 0)
                fn = nc.vector.tensor_copy(fn_t[:, 0:1], pb_n[:, 0:1])
                nc.vector.memset(fn_t[:, 1:8], 0.0)
                pg.sDV.inc(fn)
                pg.d1nf = pg.sDV.n
                pg.sDV.wait(nc.sync)
                pg.sST.inc(nc.sync.dma_start(T["fn_sc"][ci * C:(ci + 1) * C, :],
                                             fn_t[:, :]), 16)
                if tap_fn is not None:
                    pg.sST.inc(nc.sync.dma_start(
                        tap_fn[ci * C:(ci + 1) * C, :], fn_t[:, :]), 16)
                pg.d1nsf = pg.sST.n
            for s in range(4):
                spar = s % 2
                nc.gpsimd.wait_ge(pg.sPE.h, wfd[spar])
                pg.sLG.inc(nc.gpsimd.dma_start(
                    dcs[spar][:, :, :],
                    T["tb_dcos"][:, s * 512:(s + 1) * 512].rearrange(
                        "(kt p) n -> p kt n", p=128)), 16)
                pg.sLG.inc(nc.gpsimd.dma_start(
                    dss[spar][:, :, :],
                    T["tb_dsin"][:, s * 512:(s + 1) * 512].rearrange(
                        "(kt p) n -> p kt n", p=128)), 16)
                lg_mark = pg.sLG.n
                nc.tensor.wait_ge(pg.sLG.h, lg_mark)
                for cc in range(GRP):
                    ci = g * GRP + cc
                    bk = cc % 2
                    if not hasattr(pg, "d1ef"):
                        pg.d1ef = [0, 0]
                    nc.tensor.wait_ge(pg.sDV.h, pg.d1ef[bk])
                    for kt in range(32):
                        nc.tensor.matmul(pb_r[bk][:, :], xnT[:, cc, kt, :],
                                         dcs[spar][:, kt, :],
                                         start=(kt == 0), stop=(kt == 31))
                    for kt in range(32):
                        mm = nc.tensor.matmul(pb_i[bk][:, :], xnT[:, cc, kt, :],
                                              dss[spar][:, kt, :],
                                              start=(kt == 0), stop=(kt == 31))
                    pg.sPE.inc(mm)
                    wfd[spar] = pg.sPE.n
                    pg.sPE.wait(nc.vector)
                    nc.vector.wait_ge(pg.sST.h, pg.d1sf if hasattr(pg, "d1sf") else 0)
                    nc.vector.tensor_copy(fr_t[:, :], pb_r[bk][:, :])
                    nc.vector.tensor_scalar(fi_t[:, :], pb_i[bk][:, :], -1.0, None,
                                            ALU.mult)
                    nc.vector.tensor_tensor(sq_t[:, :], fr_t[:, :], fr_t[:, :],
                                            ALU.mult)
                    nc.vector.tensor_tensor(sq2_t[:, :], fi_t[:, :], fi_t[:, :],
                                            ALU.mult)
                    ev = nc.vector.tensor_tensor(p_t[:, :], sq_t[:, :], sq2_t[:, :],
                                                 ALU.add)
                    pg.sDV.inc(ev)
                    pg.d1ef[bk] = pg.sDV.n
                    pg.sDV.wait(nc.sync)
                    ssl = slice(s * 512, (s + 1) * 512)
                    tsl = slice(ci * C, (ci + 1) * C)
                    pg.sST.inc(nc.sync.dma_start(T["fr_sc"][tsl, ssl],
                                                 fr_t[:, :]), 16)
                    pg.sST.inc(nc.sync.dma_start(T["fi_sc"][tsl, ssl],
                                                 fi_t[:, :]), 16)
                    pg.sST.inc(nc.sync.dma_start(T["p_sc"][tsl, ssl], p_t[:, :]), 16)
                    if tap_fr is not None:
                        pg.sST.inc(nc.sync.dma_start(tap_fr[tsl, ssl],
                                                     fr_t[:, :]), 16)
                    if tap_pp is not None:
                        pg.sST.inc(nc.sync.dma_start(tap_pp[tsl, ssl],
                                                     p_t[:, :]), 16)
                    pg.d1sf = pg.sST.n



def _phase_d2_corr(pg, T, n_tok, tap):
    nc = pg.nc
    phase_barrier(pg)
    MYT = n_tok // 2
    MCH = MYT // C
    tap_corr = tap("corr", [MYT, HID])
    tap_h = tap("hvec", [MYT, HID], BF16)

    with ExitStack() as ctx:
        E = ctx.enter_context
        icos = E(nc.sbuf_tensor("d2_icos", [128, 16, HID], BF16))
        alt = E(nc.sbuf_tensor("d2_alt", [128, HID], BF16))
        p_ch = E(nc.sbuf_tensor("d2_p", [128, NSPEC], BF16))
        pT = E(nc.sbuf_tensor("d2_pT", [128, 16, 128], BF16))
        fn_t = E(nc.sbuf_tensor("d2_fn", [128, 8], F32))
        corr = E(nc.sbuf_tensor("d2_corr", [128, HID], F32))
        mask = E(nc.sbuf_tensor("d2_mask", [128, HID], F32))
        h_bf = E(nc.sbuf_tensor("d2_h", [128, HID], BF16))
        m8 = E(nc.sbuf_tensor("d2_m8", [128, 8], F32))
        st8 = E(nc.sbuf_tensor("d2_st8", [128, 8], F32))
        e8 = E(nc.sbuf_tensor("d2_e8", [128, 8], F32))
        hT = E(nc.sbuf_tensor("d2_hT", [128, 32, 128], BF16))
        pb_t = E(nc.psum_tensor("d2_pt", [128, 1024], BF16))
        pb_c = [E(nc.psum_tensor(f"d2_pc{i}", [128, 512], F32)) for i in range(4)]

        nc.sync.wait_ge(pg.sST.h, pg.sST.n)
        ld = nc.sync.dma_start(icos[:, :, :],
                               T["tb_icos"][:, :].rearrange("(kt p) t -> p kt t",
                                                            p=128))
        pg.sLD.inc(ld, 16)
        pg.sLD.inc(nc.sync.dma_start(alt[:, :], T["tb_alt"][:, :]), 16)
        ldm = pg.sLD.n

        dvf = 0
        stf = 0
        for ci in range(MCH):
            tsl = slice(ci * C, (ci + 1) * C)
            nc.sync.wait_ge(pg.sPE.h, pg.sPE.n)
            nc.sync.wait_ge(pg.sDV.h, dvf)
            pg.sLD.inc(nc.sync.dma_start(p_ch[:, :], T["p_sc"][tsl, :]), 16)
            pg.sLD.inc(nc.sync.dma_start(fn_t[:, :], T["fn_sc"][tsl, :]), 16)
            pg.sLD.wait(nc.tensor)
            pg.sLD.wait(nc.vector)
            nc.vector.wait_ge(pg.sLD.h, ldm)
            pn = nc.vector.tensor_tensor(fn_t[:, 1:2], fn_t[:, 0:1], fn_t[:, 0:1],
                                         ALU.mult)
            pg.sDV.inc(pn)
            pg.sDV.wait(nc.vector)
            _transpose_batch(pg, [(p_ch[:, i * 128:(i + 1) * 128], pT[:, i, :])
                                  for i in range(16)], pb_t)
            pg.sDV.wait(nc.tensor)
            for s in range(8):
                bank = s % 4
                ssl = slice(s * 512, (s + 1) * 512)
                nc.tensor.wait_ge(pg.sDV.h, pg.d2cf[bank]
                                  if hasattr(pg, "d2cf") else 0)
                for kt in range(16):
                    mm = nc.tensor.matmul(pb_c[bank][:, :], pT[:, kt, :],
                                          icos[:, kt, ssl], start=(kt == 0),
                                          stop=(kt == 15))
                pg.sPE.inc(mm)
                pg.sPE.wait(nc.vector)
                cv = nc.vector.scalar_tensor_tensor(corr[:, ssl], alt[:, ssl],
                                                    fn_t[:, 1:2], pb_c[bank][:, :],
                                                    ALU.mult, ALU.add)
                pg.sDV.inc(cv)
                if not hasattr(pg, "d2cf"):
                    pg.d2cf = [0, 0, 0, 0]
                pg.d2cf[bank] = pg.sDV.n
            if tap_corr is not None:
                pg.sDV.wait(nc.sync)
                pg.sST.inc(nc.sync.dma_start(tap_corr[tsl, :], corr[:, :]), 16)
            m8i = nc.vector.max(out=m8[:, :], in_=corr[:, :])
            pg.sDV.inc(m8i)
            pg.sDV.wait(nc.vector)
            nc.vector.tensor_scalar(st8[:, 0:1], m8[:, 0:1], -1.0, None, ALU.mult)
            nv = nc.vector.tensor_scalar(mask[:, :], corr[:, :], m8[:, 7:8], None,
                                         ALU.is_ge)
            pg.sDV.inc(nv)
            pg.sDV.wait(nc.scalar)
            ac = nc.scalar.activation(e8[:, :], m8[:, :], AF.Exp,
                                      bias=st8[:, 0:1], scale=1.0,
                                      accum_out=st8[:, 2:3])
            pg.sAC.inc(ac)
            ac2 = nc.scalar.activation(corr[:, :], corr[:, :], AF.Exp,
                                       bias=st8[:, 0:1], scale=1.0)
            pg.sAC.inc(ac2)
            pg.sAC.wait(nc.vector)
            rc = nc.vector.reciprocal(st8[:, 3:4], st8[:, 2:3])
            pg.sDV.inc(rc)
            pg.sDV.wait(nc.vector)
            hv = nc.vector.scalar_tensor_tensor(h_bf[:, :], corr[:, :],
                                                st8[:, 3:4], mask[:, :],
                                                ALU.mult, ALU.mult)
            pg.sDV.inc(hv)
            if tap_h is not None:
                pg.sDV.wait(nc.sync)
                pg.sST.inc(nc.sync.dma_start(tap_h[tsl, :], h_bf[:, :]), 16)
            pg.sDV.wait(nc.tensor)
            nc.vector.wait_ge(pg.sST.h, stf)
            _transpose_batch(pg, [(h_bf[:, i * 128:(i + 1) * 128], hT[:, i, :])
                                  for i in range(32)], pb_t)
            dvf = pg.sDV.n
            pg.sDV.wait(nc.sync)
            st = nc.sync.dma_start(T["hT_sc"][ci].rearrange("ft p t -> p ft t"),
                                   hT[:, :, :])
            pg.sST.inc(st, 16)
            stf = pg.sST.n



D3_MODE = "full"


def _phase_d3_gprod(pg, T, n_tok, tap):
    nc = pg.nc
    phase_barrier(pg)
    MYT = n_tok // 2
    MCH = MYT // C
    GRP = min(8, MCH)
    NGRP = MCH // GRP

    with ExitStack() as ctx:
        E = ctx.enter_context
        hT = E(nc.sbuf_tensor("d3_hT", [128, GRP, 32, 128], BF16))
        dcs = [E(nc.sbuf_tensor(f"d3_dc{i}", [128, 32, 512], BF16)) for i in range(2)]
        dss = [E(nc.sbuf_tensor(f"d3_ds{i}", [128, 32, 512], BF16)) for i in range(2)]
        fr_t = E(nc.sbuf_tensor("d3_fr", [128, 512], F32))
        fi_t = E(nc.sbuf_tensor("d3_fi", [128, 512], F32))
        t1 = E(nc.sbuf_tensor("d3_t1", [128, 512], F32))
        t2 = E(nc.sbuf_tensor("d3_t2", [128, 512], F32))
        r_t = E(nc.sbuf_tensor("d3_r", [128, 512], BF16))
        i_t = E(nc.sbuf_tensor("d3_i", [128, 512], BF16))
        fn_t = E(nc.sbuf_tensor("d3_fn", [128, 8], F32))
        hn_t = E(nc.sbuf_tensor("d3_hn", [128, 8], F32))
        pb_r = [E(nc.psum_tensor(f"d3_pr{i}", [128, 512], F32)) for i in range(2)]
        pb_i = [E(nc.psum_tensor(f"d3_pi{i}", [128, 512], F32)) for i in range(2)]
        pb_n = E(nc.psum_tensor("d3_pn", [128, 8], F32))

        nc.sync.wait_ge(pg.sST.h, pg.sST.n)
        wfd = [0, 0]
        for g in range(NGRP):
            nc.sync.wait_ge(pg.sPE.h, pg.sPE.n)
            for cc in range(GRP):
                ld = nc.sync.dma_start(
                    hT[:, cc, :, :],
                    T["hT_sc"][g * GRP + cc].rearrange("ft p t -> p ft t"))
                pg.sLD.inc(ld, 16)
            pg.sLD.wait(nc.tensor)
            for cc in range(GRP):
                if D3_MODE == "strips":
                    break
                ci = g * GRP + cc
                tsl = slice(ci * C, (ci + 1) * C)
                nc.sync.wait_ge(pg.sDV.h, pg.d3nf if hasattr(pg, "d3nf") else 0)
                pg.sLD.inc(nc.sync.dma_start(fn_t[:, :], T["fn_sc"][tsl, :]), 16)
                ldm = pg.sLD.n
                nc.tensor.wait_ge(pg.sDV.h, pg.d3nf if hasattr(pg, "d3nf") else 0)
                for kt in range(32):
                    mm = nc.tensor.matmul(pb_n[:, 0:1], hT[:, cc, kt, :],
                                          pg.dnyq_b[:, kt, :],
                                          start=(kt == 0), stop=(kt == 31))
                pg.sPE.inc(mm)
                pg.sPE.wait(nc.vector)
                nc.vector.wait_ge(pg.sLD.h, ldm)
                nc.vector.wait_ge(pg.sST.h, pg.d3nsf if hasattr(pg, "d3nsf") else 0)
                nc.vector.tensor_copy(hn_t[:, 0:8], fn_t[:, 0:8])
                hn = nc.vector.tensor_tensor(hn_t[:, 0:1], fn_t[:, 0:1],
                                             pb_n[:, 0:1], ALU.mult)
                pg.sDV.inc(hn)
                pg.d3nf = pg.sDV.n
                pg.sDV.wait(nc.sync)
                pg.sST.inc(nc.sync.dma_start(T["rn_sc"][tsl, :], hn_t[:, :]), 16)
                pg.d3nsf = pg.sST.n
            for s in range(4):
                if D3_MODE == "nyq":
                    break
                spar = s % 2
                ssl = slice(s * 512, (s + 1) * 512)
                nc.gpsimd.wait_ge(pg.sPE.h, wfd[spar])
                pg.sLG.inc(nc.gpsimd.dma_start(
                    dcs[spar][:, :, :],
                    T["tb_dcos"][:, ssl].rearrange("(kt p) n -> p kt n", p=128)), 16)
                pg.sLG.inc(nc.gpsimd.dma_start(
                    dss[spar][:, :, :],
                    T["tb_dsin"][:, ssl].rearrange("(kt p) n -> p kt n", p=128)), 16)
                lg_mark = pg.sLG.n
                nc.tensor.wait_ge(pg.sLG.h, lg_mark)
                for cc in range(GRP):
                    ci = g * GRP + cc
                    tsl = slice(ci * C, (ci + 1) * C)
                    bk = cc % 2
                    if not hasattr(pg, "d3ef"):
                        pg.d3ef = [0, 0]
                    nc.sync.wait_ge(pg.sDV.h, pg.d3ef[bk])
                    pg.sLD.inc(nc.sync.dma_start(fr_t[:, :], T["fr_sc"][tsl, ssl]),
                               16)
                    pg.sLD.inc(nc.sync.dma_start(fi_t[:, :], T["fi_sc"][tsl, ssl]),
                               16)
                    ldm = pg.sLD.n
                    nc.tensor.wait_ge(pg.sDV.h, pg.d3ef[bk])
                    for kt in range(32):
                        nc.tensor.matmul(pb_r[bk][:, :], hT[:, cc, kt, :],
                                         dcs[spar][:, kt, :],
                                         start=(kt == 0), stop=(kt == 31))
                    for kt in range(32):
                        mm = nc.tensor.matmul(pb_i[bk][:, :], hT[:, cc, kt, :],
                                              dss[spar][:, kt, :],
                                              start=(kt == 0), stop=(kt == 31))
                    pg.sPE.inc(mm)
                    pg.sPE.wait(nc.vector)
                    nc.vector.wait_ge(pg.sLD.h, ldm)
                    nc.vector.wait_ge(pg.sST.h, pg.d3sf if hasattr(pg, "d3sf") else 0)
                    nc.vector.tensor_tensor(t1[:, :], fr_t[:, :], pb_r[bk][:, :],
                                            ALU.mult)
                    nc.vector.tensor_tensor(t2[:, :], fi_t[:, :], pb_i[bk][:, :],
                                            ALU.mult)
                    nc.vector.tensor_tensor(r_t[:, :], t1[:, :], t2[:, :],
                                            ALU.subtract)
                    nc.vector.tensor_tensor(t1[:, :], fr_t[:, :], pb_i[bk][:, :],
                                            ALU.mult)
                    nc.vector.tensor_tensor(t2[:, :], fi_t[:, :], pb_r[bk][:, :],
                                            ALU.mult)
                    ev = nc.vector.tensor_tensor(i_t[:, :], t1[:, :], t2[:, :],
                                                 ALU.add)
                    pg.sDV.inc(ev)
                    pg.d3ef[bk] = pg.sDV.n
                    pg.sDV.wait(nc.sync)
                    pg.sST.inc(nc.sync.dma_start(T["r_sc"][tsl, ssl], r_t[:, :]), 16)
                    pg.sST.inc(nc.sync.dma_start(T["i_sc"][tsl, ssl], i_t[:, :]), 16)
                    pg.d3sf = pg.sST.n



def _phase_d4_inv(pg, T, n_tok, tap):
    nc = pg.nc
    phase_barrier(pg)
    MYT = n_tok // 2
    MCH = MYT // C
    tap_a1 = tap("attn1", [MYT, HID])

    with ExitStack() as ctx:
        E = ctx.enter_context
        icos = E(nc.sbuf_tensor("d4_icos", [128, 16, NSPEC], BF16))
        isin = E(nc.sbuf_tensor("d4_isin", [128, 16, NSPEC], BF16))
        alt = E(nc.sbuf_tensor("d4_alt", [128, HID], BF16))
        r_ch = E(nc.sbuf_tensor("d4_r", [128, NSPEC], BF16))
        i_ch = E(nc.sbuf_tensor("d4_i", [128, NSPEC], BF16))
        rT = E(nc.sbuf_tensor("d4_rT", [128, 16, 128], BF16))
        iT = E(nc.sbuf_tensor("d4_iT", [128, 16, 128], BF16))
        rn_t = E(nc.sbuf_tensor("d4_rn", [128, 8], F32))
        x1_t = E(nc.sbuf_tensor("d4_x1", [128, NSPEC], F32))
        o_t = E(nc.sbuf_tensor("d4_o", [128, 512], F32))
        pb_t = E(nc.psum_tensor("d4_pt", [128, 1024], BF16))
        pb_m = [E(nc.psum_tensor(f"d4_pm{i}", [128, 512], F32)) for i in range(4)]

        nc.sync.wait_ge(pg.sST.h, pg.sST.n)
        for half in range(2):
            hssl = slice(half * NSPEC, (half + 1) * NSPEC)
            nc.sync.wait_ge(pg.sPE.h, pg.sPE.n)
            nc.sync.wait_ge(pg.sDV.h, pg.sDV.n)
            ld = nc.sync.dma_start(
                icos[:, :, :],
                T["tb_icos"][:, hssl].rearrange("(kt p) t -> p kt t", p=128))
            pg.sLD.inc(ld, 16)
            ld = nc.sync.dma_start(
                isin[:, :, :],
                T["tb_isin"][:, hssl].rearrange("(kt p) t -> p kt t", p=128))
            pg.sLD.inc(ld, 16)
            pg.sLD.inc(nc.sync.dma_start(alt[:, :], T["tb_alt"][:, :]), 16)
            ldtab = pg.sLD.n
            dvf = 0
            for ci in range(MCH):
                tsl = slice(ci * C, (ci + 1) * C)
                nc.sync.wait_ge(pg.sPE.h, pg.sPE.n)
                nc.sync.wait_ge(pg.sDV.h, dvf)
                pg.sLD.inc(nc.sync.dma_start(r_ch[:, :], T["r_sc"][tsl, :]), 16)
                pg.sLD.inc(nc.sync.dma_start(i_ch[:, :], T["i_sc"][tsl, :]), 16)
                pg.sLD.inc(nc.sync.dma_start(rn_t[:, :], T["rn_sc"][tsl, :]), 16)
                pg.sLD.inc(nc.sync.dma_start(x1_t[:, :], T["x1_sc"][tsl, hssl]), 16)
                pg.sLD.wait(nc.tensor)
                pg.sLD.wait(nc.vector)
                nc.vector.wait_ge(pg.sLD.h, ldtab)
                _transpose_batch(pg, [(r_ch[:, i * 128:(i + 1) * 128], rT[:, i, :])
                                      for i in range(16)]
                                 + [(i_ch[:, i * 128:(i + 1) * 128], iT[:, i, :])
                                    for i in range(16)], pb_t)
                dvf = pg.sDV.n
                pg.sDV.wait(nc.tensor)
                for s in range(4):
                    bank = s % 4
                    ssl = slice(s * 512, (s + 1) * 512)
                    osl = slice(half * NSPEC + s * 512, half * NSPEC + (s + 1) * 512)
                    nc.tensor.wait_ge(pg.sDV.h, pg.d4f[bank]
                                      if hasattr(pg, "d4f") else 0)
                    for kt in range(16):
                        nc.tensor.matmul(pb_m[bank][:, :], rT[:, kt, :],
                                         icos[:, kt, ssl], start=(kt == 0),
                                         stop=False)
                    for kt in range(16):
                        mm = nc.tensor.matmul(pb_m[bank][:, :], iT[:, kt, :],
                                              isin[:, kt, ssl],
                                              start=False, stop=(kt == 15))
                    pg.sPE.inc(mm)
                    pg.sPE.wait(nc.vector)
                    nc.vector.wait_ge(pg.sST.h, pg.d4sf
                                      if hasattr(pg, "d4sf") else 0)
                    nc.vector.scalar_tensor_tensor(o_t[:, :], alt[:, osl],
                                                   rn_t[:, 0:1], pb_m[bank][:, :],
                                                   ALU.mult, ALU.add)
                    ev = nc.vector.tensor_tensor(o_t[:, :], o_t[:, :],
                                                 x1_t[:, ssl], ALU.add)
                    pg.sDV.inc(ev)
                    if not hasattr(pg, "d4f"):
                        pg.d4f = [0, 0, 0, 0]
                    pg.d4f[bank] = pg.sDV.n
                    pg.sDV.wait(nc.sync)
                    st = nc.sync.dma_start(T["out_my"][tsl, osl], o_t[:, :])
                    pg.sST.inc(st, 16)
                    pg.d4sf = pg.sST.n
                    if tap_a1 is not None:
                        pg.sST.inc(nc.sync.dma_start(tap_a1[tsl, osl],
                                                     o_t[:, :]), 16)



def _phase_d5_ffn1(pg, T, n_tok, tap):
    nc = pg.nc
    phase_barrier(pg)
    MYT = n_tok // 2
    MCH = MYT // C
    TSW = min(512, MYT)
    NTS = MYT // TSW
    CPT = TSW // C
    tap_f1 = tap("ffn1", [MYT, HID])

    with ExitStack() as ctx:
        E = ctx.enter_context
        u1b = E(nc.sbuf_tensor("d5_u1", [128, 32, RANK], BF16))
        xns = E(nc.sbuf_tensor("d5_xns", [128, 32, TSW], BF16))
        pT = E(nc.sbuf_tensor("d5_pT", [128, 8, MYT], BF16))
        pb_m = [E(nc.psum_tensor(f"d5a_pm{i}", [128, TSW], F32)) for i in range(4)]

        nc.gpsimd.wait_ge(pg.sST.h, pg.sST.n)
        ldw = nc.gpsimd.dma_start(
            u1b[:, :, :], T["u1w"][:, :].rearrange("(kt p) r -> p kt r", p=128))
        pg.sLG.inc(ldw, 16)
        dvf = [0, 0, 0, 0]
        for ts in range(NTS):
            nc.sync.wait_ge(pg.sPE.h, pg.sPE.n)
            for cc in range(CPT):
                ld = nc.sync.dma_start(
                    xns[:, :, cc * C:(cc + 1) * C],
                    T["xn1T_sc"][ts * CPT + cc].rearrange("ft p t -> p ft t"))
                pg.sLD.inc(ld, 16)
            pg.sLD.wait(nc.tensor)
            pg.sLG.wait(nc.tensor)
            for rt in range(8):
                bank = rt % 4
                nc.tensor.wait_ge(pg.sDV.h, dvf[bank])
                for kt in range(32):
                    mm = nc.tensor.matmul(pb_m[bank][:, :],
                                          u1b[:, kt, rt * 128:(rt + 1) * 128],
                                          xns[:, kt, :], start=(kt == 0),
                                          stop=(kt == 31))
                pg.sPE.inc(mm)
                pg.sPE.wait(nc.vector)
                ev = nc.vector.tensor_copy(pT[:, rt, ts * TSW:(ts + 1) * TSW],
                                           pb_m[bank][:, :])
                pg.sDV.inc(ev)
                dvf[bank] = pg.sDV.n

        phase_barrier(pg)
        with ExitStack() as ctx2:
            E2 = ctx2.enter_context
            v1g = [E2(nc.sbuf_tensor(f"d5_v1g{i}", [128, 8, 128], BF16))
                   for i in range(2)]
            v1u = [E2(nc.sbuf_tensor(f"d5_v1u{i}", [128, 8, 128], BF16))
                   for i in range(2)]
            silu_t = E2(nc.sbuf_tensor("d5_silu", [128, TSW], F32))
            act_t = E2(nc.sbuf_tensor("d5_act", [128, TSW], BF16))
            pb_g = [E2(nc.psum_tensor(f"d5_pg{i}", [128, TSW], F32)) for i in range(2)]
            pb_up = [E2(nc.psum_tensor(f"d5_pu{i}", [128, TSW], F32)) for i in range(2)]
            pg.b1r_b = E2(nc.sbuf_tensor("b1r_b", [1, 2 * INTER], BF16))
            pg.sLG.inc(nc.gpsimd.dma_start(pg.b1r_b[:, :], T["b1r"][:, :]), 16)
            pg.sLG.wait(nc.tensor)

            wf = [0, 0]
            acf = [0, 0]
            dv2 = [0, 0]
            stf = 0
            for gi in range(NHT):
                spar = gi % 2
                nc.gpsimd.wait_ge(pg.sPE.h, wf[spar])
                pg.sLG.inc(nc.gpsimd.dma_start(
                    v1g[spar][:, :, :],
                    T["v1w"][:, gi * 128:(gi + 1) * 128].rearrange(
                        "(kt p) n -> p kt n", p=128)), 16)
                pg.sLG.inc(nc.gpsimd.dma_start(
                    v1u[spar][:, :, :],
                    T["v1w"][:, INTER + gi * 128:INTER + (gi + 1) * 128].rearrange(
                        "(kt p) n -> p kt n", p=128)), 16)
                lg_mark = pg.sLG.n
                nc.tensor.wait_ge(pg.sLG.h, lg_mark)
                for ts in range(NTS):
                    tsl = slice(ts * TSW, (ts + 1) * TSW)
                    bk = (gi * NTS + ts) % 2
                    nc.tensor.wait_ge(pg.sAC.h, acf[bk])
                    nc.tensor.wait_ge(pg.sDV.h, dv2[bk])
                    for kt in range(8):
                        nc.tensor.matmul(pb_g[bk][:, :], v1g[spar][:, kt, :],
                                         pT[:, kt, tsl], start=(kt == 0),
                                         stop=False)
                    nc.tensor.matmul(pb_g[bk][:, :],
                                     pg.b1r_b[0:1, gi * 128:(gi + 1) * 128],
                                     pg.ones_bf[0:1, 0:TSW], start=False, stop=True)
                    for kt in range(8):
                        nc.tensor.matmul(pb_up[bk][:, :], v1u[spar][:, kt, :],
                                         pT[:, kt, tsl], start=(kt == 0),
                                         stop=False)
                    mm = nc.tensor.matmul(
                        pb_up[bk][:, :],
                        pg.b1r_b[0:1, INTER + gi * 128:INTER + (gi + 1) * 128],
                        pg.ones_bf[0:1, 0:TSW], start=False, stop=True)
                    pg.sPE.inc(mm)
                    wf[spar] = pg.sPE.n
                    pg.sPE.wait(nc.scalar)
                    nc.scalar.wait_ge(pg.sDV.h, dv2[1 - bk])
                    sl = nc.scalar.activation(silu_t[:, :], pb_g[bk][:, :], AF.Silu,
                                              bias=0.0, scale=1.0)
                    pg.sAC.inc(sl)
                    acf[bk] = pg.sAC.n
                    pg.sAC.wait(nc.vector)
                    nc.vector.wait_ge(pg.sST.h, stf)
                    av = nc.vector.tensor_tensor(act_t[:, :], silu_t[:, :],
                                                 pb_up[bk][:, :], ALU.mult)
                    pg.sDV.inc(av)
                    dv2[bk] = pg.sDV.n
                    pg.sDV.wait(nc.sync)
                    st = nc.sync.dma_start(T["act1T_sc"][gi, ts], act_t[:, :])
                    pg.sST.inc(st, 16)
                    stf = pg.sST.n

    phase_barrier(pg)
    with ExitStack() as zctx:
      ZT = zctx.enter_context(nc.sbuf_tensor("d5_ZT", [128, 8, MYT], BF16))
      with ExitStack() as ctx:
        E = ctx.enter_context
        a1s = E(nc.sbuf_tensor("d5_a1s", [128, NHT, TSW], BF16))
        u2b = [E(nc.sbuf_tensor(f"d5_u2{i}", [128, NHT, 128], BF16))
               for i in range(2)]
        pb_m = [E(nc.psum_tensor(f"d5c_pm{i}", [128, TSW], F32)) for i in range(4)]

        nc.sync.wait_ge(pg.sST.h, pg.sST.n)
        wf = [0, 0]
        dvf = [0, 0, 0, 0]
        for ts in range(NTS):
            nc.sync.wait_ge(pg.sPE.h, pg.sPE.n)
            ld = nc.sync.dma_start(a1s[:, :, :],
                                   T["act1T_sc"][:, ts].rearrange("h p t -> p h t"))
            pg.sLD.inc(ld, 16)
            for rt in range(8):
                spar = rt % 2
                bank = rt % 4
                nc.gpsimd.wait_ge(pg.sPE.h, wf[spar])
                ldw = nc.gpsimd.dma_start(
                    u2b[spar][:, :, :],
                    T["u2w"][:, rt * 128:(rt + 1) * 128].rearrange(
                        "(kt p) n -> p kt n", p=128))
                pg.sLG.inc(ldw, 16)
                pg.sLD.wait(nc.tensor)
                pg.sLG.wait(nc.tensor)
                nc.tensor.wait_ge(pg.sDV.h, dvf[bank])
                for kt in range(NHT):
                    mm = nc.tensor.matmul(pb_m[bank][:, :], u2b[spar][:, kt, :],
                                          a1s[:, kt, :], start=(kt == 0),
                                          stop=(kt == NHT - 1))
                pg.sPE.inc(mm)
                wf[spar] = pg.sPE.n
                pg.sPE.wait(nc.vector)
                ev = nc.vector.tensor_copy(ZT[:, rt, ts * TSW:(ts + 1) * TSW],
                                           pb_m[bank][:, :])
                pg.sDV.inc(ev)
                dvf[bank] = pg.sDV.n

      phase_barrier(pg)
      if True:
        with ExitStack() as ctx2:
            E2 = ctx2.enter_context
            v2b = E2(nc.sbuf_tensor("d5_v2", [128, 8, HID], BF16))
            o_t = E2(nc.sbuf_tensor("d5_o", [128, 512], F32))
            pb_o = [E2(nc.psum_tensor(f"d5d_pm{i}", [128, 512], F32))
                    for i in range(4)]
            pg.b2r_b = E2(nc.sbuf_tensor("b2r_b", [1, HID], BF16))
            pg.sLG.inc(nc.gpsimd.dma_start(pg.b2r_b[:, :], T["b2r"][:, :]), 16)

            ldw = nc.gpsimd.dma_start(
                v2b[:, :, :], T["v2w"][:, :].rearrange("(kt p) n -> p kt n", p=128))
            pg.sLG.inc(ldw, 16)
            pg.sLG.wait(nc.tensor)
            nc.gpsimd.wait_ge(pg.sST.h, pg.sST.n)
            dvf2 = [0, 0, 0, 0]
            sgf = 0
            for ci in range(MCH):
                tsl = slice(ci * C, (ci + 1) * C)
                for s in range(8):
                    bank = s % 4
                    ssl = slice(s * 512, (s + 1) * 512)
                    nc.tensor.wait_ge(pg.sDV.h, dvf2[bank])
                    for kt in range(8):
                        nc.tensor.matmul(pb_o[bank][:, :], ZT[:, kt, tsl],
                                         v2b[:, kt, ssl], start=(kt == 0),
                                         stop=False)
                    mm = nc.tensor.matmul(pb_o[bank][:, :], pg.ones_bf[0:1, 0:128],
                                          pg.b2r_b[0:1, ssl], start=False, stop=True)
                    pg.sPE.inc(mm)
                    pg.sPE.wait(nc.vector)
                    nc.vector.wait_ge(pg.sSG.h, sgf)
                    ev = nc.vector.tensor_copy(o_t[:, :], pb_o[bank][:, :])
                    pg.sDV.inc(ev)
                    dvf2[bank] = pg.sDV.n
                    pg.sDV.wait(nc.gpsimd)
                    st = nc.gpsimd.dma_start(T["out_my"][tsl, ssl], o_t[:, :],
                                             accum_op=ALU.add)
                    pg.sSG.inc(st, 16)
                    sgf = pg.sSG.n
                    if tap_f1 is not None:
                        pg.sDV.wait(nc.sync)
                        pg.sST.inc(nc.sync.dma_start(tap_f1[tsl, ssl],
                                                     o_t[:, :]), 16)



_CACHE = {}
_LAST_RES = None


def kernel(**inputs):
    n_tok = int(inputs["L0_attn_hidden_states"].shape[1])
    key = ("full", n_tok)
    if key not in _CACHE:
        _CACHE[key] = build_program(n_tok=n_tok)
    nc, T, taps = _CACHE[key]

    x = np.ascontiguousarray(inputs["L0_attn_hidden_states"], dtype=np.float32)
    tb = make_tables(n_tok)
    MYT = n_tok // 2
    ct = np.ascontiguousarray

    shared = {
        "wo": ct(inputs["L0_attn_Wo"]),
        "w1qT": ct(inputs["L0_ffn_w1q"].T),
        "w3qT": ct(inputs["L0_ffn_w3q"].T),
        "w2qT": ct(inputs["L0_ffn_w2q"].T),
        "a1T": ct(inputs["L0_ffn_a1"].T),
        "a3T": ct(inputs["L0_ffn_a3"].T),
        "a2T": ct(inputs["L0_ffn_a2"].T),
        "b1T": ct(inputs["L0_ffn_b1"].T),
        "b3T": ct(inputs["L0_ffn_b3"].T),
        "b2T": ct(inputs["L0_ffn_b2"].T),
        "u1w": ct(inputs["L1_ffn_U1"]),
        "v1w": ct(inputs["L1_ffn_V1"]),
        "u2w": ct(inputs["L1_ffn_U2"]),
        "v2w": ct(inputs["L1_ffn_V2"]),
        "b1r": ct(inputs["L1_ffn_b1"].reshape(1, -1)),
        "b2r": ct(inputs["L1_ffn_b2"].reshape(1, -1)),
    }
    shared.update(tb)

    in_maps = []
    for c in range(8):
        b, g = c // 2, c % 2
        m = dict(shared)
        m["x_b"] = ct(x[b])
        m["x_my"] = ct(x[b, g * MYT:(g + 1) * MYT])
        m["wq"] = ct(inputs["L0_attn_Wq"][:, g * 2048:(g + 1) * 2048])
        m["wk"] = ct(inputs["L0_attn_Wk"][:, g * 512:(g + 1) * 512])
        m["wv"] = ct(inputs["L0_attn_Wv"][:, g * 512:(g + 1) * 512])
        in_maps.append(m)

    res = run_bass_kernel_spmd(nc, in_maps, core_ids=list(range(8)))
    global _LAST_RES
    _LAST_RES = res
    out = np.empty((4, n_tok, HID), dtype=np.float32)
    for c in range(8):
        b, g = c // 2, c % 2
        out[b, g * MYT:(g + 1) * MYT] = res.results[c]["out_my"]
    return out
